# version 05
# baseline (speedup 1.0000x reference)
import numpy as np
import ml_dtypes
import concourse.bass as bass
import concourse.mybir as mybir
from concourse.bass_utils import run_bass_kernel_spmd

F32 = mybir.dt.float32
BF16 = mybir.dt.bfloat16
U8 = mybir.dt.uint8
AF = mybir.ActivationFunctionType
ALU = mybir.AluOpType
AX = mybir.AxisListType

D = 1024
S = 4096
H = 16
DH = 64
NCHUNK = 8
NSUB = 8
TSUB = 512
E = 8
IMOE = 2048
SUBH = 256
LN_EPS = 1e-5
RMS_EPS = 1e-5
L1F_EPS = 1e-6
WIN = 256
STRIDE = 128
NWIN = 31
KCAP = 2048

ENGINES = ("pe", "act", "dve", "sp", "gp")


class Prog:

    POOL = 100

    def __init__(self, nc):
        self.nc = nc
        self.ops = {e: [] for e in ENGINES}
        self.cnt = {e: 0 for e in ENGINES}
        self.pool_cnt = {}
        self.key_sem = {}
        self.pool_next = {"sp": 0, "gp": 0, "act": 0}
        self.pool_used = set()
        self.last_write = {}
        self.readers = {}
        self.water = {}
        self.nbar = 0

    def _dma_sem(self, eng, key):
        k = (eng, key)
        if k not in self.key_sem:
            label = f"{eng}{self.pool_next[eng] % self.POOL}"
            self.pool_next[eng] += 1
            assert self.pool_next[eng] <= self.POOL, f"dma sem pool exhausted on {eng}"
            self.key_sem[k] = label
            self.pool_used.add(label)
        return self.key_sem[k]

    _PSUM_ALIAS = {"pp0": ("ps0", "ps1"), "pp1": ("ps2", "ps3"), "pp2": ("ps6", "ps7")}

    def _expand(self, keys):
        out = []
        for k in keys:
            if k in self._PSUM_ALIAS:
                out.extend(self._PSUM_ALIAS[k])
            else:
                out.append(k)
        return tuple(out)

    def op(self, eng, emit, reads=(), writes=(), dma=False):
        reads = self._expand(reads)
        writes = self._expand(writes)
        deps = []
        for r in reads:
            w = self.last_write.get(r)
            if w is not None:
                deps.append(w)
        for r in writes:
            w = self.last_write.get(r)
            if w is not None:
                deps.append(w)
            deps.extend(self.readers.get(r, ()))
        need = {}
        for (lbl, tc) in deps:
            if lbl == eng == "pe":
                continue
            if need.get(lbl, -1) < tc:
                need[lbl] = tc
        waits = []
        for lbl, tc in sorted(need.items()):
            k = (eng, lbl)
            if self.water.get(k, -1) >= tc:
                continue
            self.water[k] = tc
            waits.append((lbl, tc))
        if dma:
            assert eng in ("sp", "gp", "act")
            assert writes, "dma op needs a destination key"
            if dma == "sink":
                if reads:
                    lbl = self._dma_sem(eng, ("src", reads[0]))
                else:
                    lbl = f"{eng}_sink"
                    self.pool_used.add(lbl)
            else:
                lbl = self._dma_sem(eng, writes[0])
            self.pool_cnt[lbl] = self.pool_cnt.get(lbl, 0) + 16
            after = (lbl, self.pool_cnt[lbl])
            self.ops[eng].append((waits, emit, 16, lbl))
        else:
            self.cnt[eng] += 1
            after = (eng, self.cnt[eng])
            self.ops[eng].append((waits, emit, 1, eng))
        for r in reads:
            self.readers.setdefault(r, []).append(after)
        for r in writes:
            self.last_write[r] = after
            self.readers[r] = []
        return after

    def barrier(self):
        self.nbar += 1
        n = self.nbar
        gpw = []
        for e in ENGINES:
            if self.cnt[e] > 0:
                gpw.append((e, self.cnt[e]))
        for lbl, c in sorted(self.pool_cnt.items()):
            if c > 0:
                k = ("gp", lbl)
                if self.water.get(k, -1) < c:
                    self.water[k] = c
                    gpw.append((lbl, c))
        self.ops["gp"].append((gpw, "BARRIER_INC", 0, None))
        for e in ENGINES:
            self.ops[e].append(([("bar", n)], None, 0, None))
        self.last_write = {}
        self.readers = {}
        self.water = {}
        self.key_sem = {}
        self.pool_next = {"sp": 0, "gp": 0, "act": 0}

    def finalize(self):
        self.barrier()
        nc = self.nc
        labels = set(ENGINES) | self.pool_used | {"bar"}
        sems = {l: nc.semaphore(f"s_{l}").__enter__() for l in sorted(labels)}
        with nc.Block() as block:
            sections = {
                "pe": block.tensor, "act": block.scalar, "dve": block.vector,
                "sp": block.sync, "gp": block.gpsimd,
            }
            for eng in ENGINES:
                def make(eng):
                    def body(handle):
                        barn = [0]
                        for waits, emit, inc, lbl in self.ops[eng]:
                            for tl, tc in waits:
                                handle.wait_ge(sems[tl], tc)
                            if emit == "BARRIER_INC":
                                barn[0] += 1
                                handle.sem_inc(sems["bar"], 1)
                            elif emit is not None:
                                inst = emit(handle)
                                inst.then_inc(sems[lbl], inc)
                    return body
                sections[eng](make(eng))


class Arena:

    def __init__(self, tensor, total, name):
        self.t = tensor
        self.total = total
        self.off = 0
        self.name = name
        self.uid = 0

    def alloc(self, n):
        n32 = (n + 31) & ~31
        assert self.off + n32 <= self.total, f"{self.name} arena overflow: {self.off}+{n32}>{self.total}"
        ap = self.t.ap()[:, self.off:self.off + n]
        key = f"{self.name}:{self.off}:{self.uid}"
        self.uid += 1
        self.off += n32
        return ap, key

    def reset(self):
        self.off = 0


class Ctx:

    def __init__(self, nc, p, ab, af, small, dram):
        self.nc = nc
        self.p = p
        self.ab = ab
        self.af = af
        self.s = small
        self.d = dram



def emit_weight_prep(ctx, pairs):
    p = ctx.p
    for src, dst, key in pairs:
        n = 1
        rows = src.shape[0]
        elems = int(np.prod(src.shape))
        while elems // n * 4 > 8 << 20 or rows % n:
            n += 1
        step = rows // n
        for i in range(n):
            sl = slice(i * step, (i + 1) * step)
            p.op("gp", lambda e, s=src[sl], d=dst[sl]: e.dma_start(out=d, in_=s),
                 writes=(f"{key}:{i}",), dma="sink")
    p.barrier()


def load_chunks(ctx, dram, key, dtype=None, eng="sp", cols=S):
    p = ctx.p
    arena = ctx.ab if (dtype or dram.dtype) == BF16 else ctx.af
    out = []
    for c in range(NCHUNK):
        ap, k = arena.alloc(cols)
        p.op(eng, lambda e, d=dram[c * 128:(c + 1) * 128, :], a=ap: e.dma_start(out=a, in_=d),
             reads=(f"{key}",), writes=(k,), dma=True)
        out.append((ap, k))
    return out


def const_col(ctx, value, parts=1, name=None):
    ap, k = ctx.af.alloc(1)
    ap = ap[0:parts, :]
    ctx.p.op("dve", lambda e: e.memset(ap, float(value)), writes=(k,))
    return ap, k


def emit_norm(ctx, src_key, src_dram, dst_dram, dst_key, kind, w_dram=None):
    nc, p = ctx.nc, ctx.p
    ones1b, _ = ctx.ab.alloc(1)
    p.op("dve", lambda e: e.memset(ones1b, 1.0), writes=("ones1b",))
    onesrow_t, _ = ctx.af.alloc(128)
    onesrow = onesrow_t[0:1, :]
    p.op("dve", lambda e: e.memset(onesrow, 1.0), writes=("onesrow",))
    wk_all = ()
    if w_dram is not None:
        wcol, wk = ctx.af.alloc(NCHUNK)
        for c in range(NCHUNK):
            p.op("sp", lambda e, c=c, wcol=wcol: e.dma_start(out=wcol[:, c:c + 1], in_=w_dram[c * 128:(c + 1) * 128]),
                 writes=(f"{wk}:{c}",), dma=True)
        wk_all = tuple(f"{wk}:{c}" for c in range(NCHUNK))
    eps = LN_EPS if kind == "ln" else RMS_EPS
    epsc, epsk = const_col(ctx, eps)
    xs = [[ctx.af.alloc(TSUB) for _ in range(NCHUNK)] for _ in range(2)]
    _ra = [ctx.af.alloc(TSUB) for _ in range(6)]
    srow, qrow, var, lnv, rstd, mr = [a[0:1, :] for a, _ in _ra]
    _rk = [k for _, k in _ra]
    tb = [ctx.ab.alloc(TSUB) for _ in range(2)]
    stg_b = [ctx.ab.alloc(TSUB) for _ in range(2)]
    stg_f = [ctx.af.alloc(TSUB) for _ in range(2)]
    t32 = [ctx.af.alloc(TSUB) for _ in range(2)]
    for sub in range(NSUB):
        cs = slice(sub * TSUB, (sub + 1) * TSUB)
        par = sub % 2
        xt = xs[par]
        for c in range(NCHUNK):
            p.op("sp", lambda e, a=xt[c][0], d=src_dram[c * 128:(c + 1) * 128, cs]:
                 e.dma_start(out=a, in_=d),
                 reads=(src_key,), writes=(xt[c][1],), dma=True)
        for c in range(NCHUNK):
            xbc, xbk = tb[c % 2]
            p.op("dve", lambda e, s=xt[c][0], d=xbc: e.tensor_copy(d, s),
                 reads=(xt[c][1],), writes=(xbk,))
            if kind == "ln":
                p.op("pe", lambda e, s=xbc: e.matmul(ctx.s["ps0"].ap()[0:1, :], ones1b, s,
                     start=True, stop=True),
                     reads=(xbk, "ones1b"), writes=("ps0",))
                p.op("dve", lambda e, c=c: (e.tensor_copy(srow, ctx.s["ps0"].ap()[0:1, :]) if c == 0
                     else e.tensor_add(srow, srow, ctx.s["ps0"].ap()[0:1, :])),
                     reads=("ps0", _rk[0]), writes=(_rk[0],))
            sqc, sqk = tb[(c + 1) % 2]
            p.op("act", lambda e, s=xt[c][0], d=sqc: e.activation(out=d, in_=s, func=AF.Square),
                 reads=(xt[c][1],), writes=(sqk,))
            p.op("pe", lambda e, s=sqc: e.matmul(ctx.s["ps1"].ap()[0:1, :], ones1b, s,
                 start=True, stop=True),
                 reads=(sqk, "ones1b"), writes=("ps1",))
            p.op("dve", lambda e, c=c: (e.tensor_copy(qrow, ctx.s["ps1"].ap()[0:1, :]) if c == 0
                 else e.tensor_add(qrow, qrow, ctx.s["ps1"].ap()[0:1, :])),
                 reads=("ps1", _rk[1]), writes=(_rk[1],))
        if kind == "ln":
            p.op("dve", lambda e: e.tensor_scalar(out=srow, in0=srow, scalar1=1.0 / D,
                 scalar2=None, op0=ALU.mult), reads=(_rk[0],), writes=(_rk[0],))
            p.op("act", lambda e: e.activation(out=var, in_=srow, func=AF.Square),
                 reads=(_rk[0],), writes=(_rk[2],))
            p.op("dve", lambda e: e.scalar_tensor_tensor(out=var, in0=qrow, scalar=1.0 / D,
                 in1=var, op0=ALU.mult, op1=ALU.subtract),
                 reads=(_rk[1], _rk[2]), writes=(_rk[2],))
        else:
            p.op("dve", lambda e: e.tensor_scalar(out=var, in0=qrow, scalar1=1.0 / D,
                 scalar2=None, op0=ALU.mult), reads=(_rk[1],), writes=(_rk[2],))
        p.op("act", lambda e: e.activation(out=lnv, in_=var, func=AF.Ln, bias=epsc),
             reads=(_rk[2], epsk), writes=(_rk[3],))
        p.op("act", lambda e: e.activation(out=rstd, in_=lnv, func=AF.Exp, scale=-0.5),
             reads=(_rk[3],), writes=(_rk[4],))
        if kind == "ln":
            p.op("dve", lambda e: e.tensor_mul(mr, srow, rstd), reads=(_rk[0], _rk[4]), writes=(_rk[5],))
        p.op("pe", lambda e: e.matmul(ctx.s["ps2"].ap()[:, :], onesrow, rstd,
             start=True, stop=True), reads=(_rk[4], "onesrow"), writes=("ps2",))
        if kind == "ln":
            p.op("pe", lambda e: e.matmul(ctx.s["ps3"].ap()[:, :], onesrow, mr,
                 start=True, stop=True), reads=(_rk[5], "onesrow"), writes=("ps3",))
        for c in range(NCHUNK):
            o, ok = stg_b[c % 2] if dst_dram.dtype == BF16 else stg_f[c % 2]
            t, tk = t32[c % 2]
            if kind == "ln":
                p.op("dve", lambda e, s=xt[c][0], t=t: e.tensor_mul(t, s, ctx.s["ps2"].ap()[:, :]),
                     reads=(xt[c][1], "ps2"), writes=(tk,))
                p.op("dve", lambda e, t=t, o=o: e.tensor_sub(o, t, ctx.s["ps3"].ap()[:, :]),
                     reads=(tk, "ps3"), writes=(ok,))
            else:
                p.op("dve", lambda e, s=xt[c][0], o=o, c=c:
                     e.scalar_tensor_tensor(out=o, in0=s, scalar=wcol[:, c:c + 1],
                                            in1=ctx.s["ps2"].ap()[:, :], op0=ALU.mult, op1=ALU.mult),
                     reads=(xt[c][1], "ps2") + wk_all, writes=(ok,))
            p.op("gp", lambda e, o=o, d=dst_dram[c * 128:(c + 1) * 128, cs]: e.dma_start(out=d, in_=o),
                 reads=(ok,), writes=(f"{dst_key}:{c}:{sub}",), dma="sink")
    p.barrier()
    ctx.ab.reset()
    ctx.af.reset()


def emit_proj(ctx, in_dram, in_key, w_bf_dram, w_key, out_dram, out_key,
              transpose_w, out_dtype=BF16, out_scale=1.0, nin=NCHUNK,
              nout=NCHUNK, residual=None, res_scale=1.0, cols=S, inv_name=None):
    nc, p = ctx.nc, ctx.p
    xb = [ctx.ab.alloc(cols) for _ in range(nin)]
    for c in range(nin):
        a, k = xb[c]
        p.op("sp", lambda e, a=a, d=in_dram[c * 128:(c + 1) * 128, :]: e.dma_start(out=a, in_=d),
             reads=(in_key,), writes=(k,), dma=True)
    if inv_name is not None:
        selp, selpk = ctx.af.alloc(128)
        p.op("sp", lambda e: e.dma_start(out=selp[0:2, :], in_=ctx.d["c_selp"]),
             writes=(selpk,), dma=True)
        iv2 = [ctx.af.alloc(cols) for _ in range(2)]
        for c in range(nin):
            a, k = xb[c]
            iv, ivk = iv2[c % 2]
            p.op("sp", lambda e, iv=iv, d=ctx.d[inv_name][2 * c:2 * c + 2, :]:
                 e.dma_start(out=iv[0:2, :], in_=d),
                 reads=(f"{inv_name}w",), writes=(ivk,), dma=True)
            for sub in range(cols // TSUB):
                cs = slice(sub * TSUB, (sub + 1) * TSUB)
                p.op("pe", lambda e, iv=iv, cs=cs: e.matmul(ctx.s["ps6"].ap()[:, :],
                     selp[0:2, :], iv[0:2, cs], start=True, stop=True),
                     reads=(ivk, selpk), writes=("ps6",))
                p.op("dve", lambda e, a=a, cs=cs: e.tensor_mul(a[:, cs], a[:, cs],
                     ctx.s["ps6"].ap()[:, :]),
                     reads=(k, "ps6"), writes=(k,))
    strips = [ctx.ab.alloc(nout * 128) for _ in range(nin)]
    for c in range(nin):
        a, k = strips[c]
        if transpose_w:
            p.op("sp", lambda e, a=a, d=w_bf_dram[:, c * 128:(c + 1) * 128]:
                 e.dma_start(out=a, in_=d, transpose=True),
                 reads=(w_key,), writes=(k,), dma=True)
        else:
            p.op("sp", lambda e, a=a, d=w_bf_dram[c * 128:(c + 1) * 128, :]:
                 e.dma_start(out=a, in_=d),
                 reads=(w_key,), writes=(k,), dma=True)
    nsub = cols // TSUB
    stg = [(ctx.ab.alloc(TSUB) if out_dtype == BF16 else ctx.af.alloc(TSUB)) for _ in range(4)]
    rst = [ctx.af.alloc(TSUB) for _ in range(2)]
    for do in range(nout):
        for sub in range(nsub):
            cs = slice(sub * TSUB, (sub + 1) * TSUB)
            bank = ctx.s[f"ps{(do * nsub + sub) % 4}"]
            bkey = f"ps{(do * nsub + sub) % 4}"
            for c in range(nin):
                p.op("pe", lambda e, w=strips[c][0], x=xb[c][0], do=do, cs=cs, b=bank,
                     st=(c == 0), sp=(c == nin - 1):
                     e.matmul(b.ap()[:, :], w[:, do * 128:(do + 1) * 128], x[:, cs], start=st, stop=sp),
                     reads=(strips[c][1], xb[c][1]), writes=(bkey,))
            o, ok = stg[(do * nsub + sub) % 4]
            if residual is None:
                p.op("act", lambda e, o=o, b=bank: e.activation(out=o, in_=b.ap()[:, :],
                     func=AF.Copy, scale=out_scale),
                     reads=(bkey,), writes=(ok,))
            else:
                rdram, rkey = residual
                r, rk = rst[sub % 2]
                p.op("sp", lambda e, r=r, d=rdram[do * 128:(do + 1) * 128, cs]: e.dma_start(out=r, in_=d),
                     reads=(rkey,), writes=(rk,), dma=True)
                p.op("dve", lambda e, o=o, b=bank, r=r:
                     e.scalar_tensor_tensor(out=o, in0=b.ap()[:, :], scalar=float(out_scale),
                                            in1=r, op0=ALU.mult, op1=ALU.add),
                     reads=(bkey, rk), writes=(ok,))
            p.op("gp", lambda e, o=o, d=out_dram[do * 128:(do + 1) * 128, cs]: e.dma_start(out=d, in_=o),
                 reads=(ok,), writes=(f"{out_key}:{do}:{sub}",), dma="sink")
    p.barrier()
    ctx.ab.reset()
    ctx.af.reset()



def emit_qkv(ctx, in_dram, in_key, mats):
    nc, p = ctx.nc, ctx.p
    ps = ctx.s
    xb = [ctx.ab.alloc(S) for _ in range(NCHUNK)]
    for c in range(NCHUNK):
        a, k = xb[c]
        p.op("sp", lambda e, a=a, d=in_dram[c * 128:(c + 1) * 128, :]: e.dma_start(out=a, in_=d),
             reads=(in_key,), writes=(k,), dma=True)
    strips = [[ctx.ab.alloc(D) for _ in range(NCHUNK)] for _ in range(2)]
    stg = [ctx.ab.alloc(TSUB) for _ in range(4)]
    for mi, (w_bf, w_key, out_dram, out_key, transpose_w, out_scale) in enumerate(mats):
        sset = strips[mi % 2]
        for c in range(NCHUNK):
            a, k = sset[c]
            if transpose_w:
                p.op("sp", lambda e, a=a, d=w_bf[:, c * 128:(c + 1) * 128]:
                     e.dma_start(out=a, in_=d, transpose=True),
                     reads=(w_key,), writes=(k,), dma=True)
            else:
                p.op("sp", lambda e, a=a, d=w_bf[c * 128:(c + 1) * 128, :]:
                     e.dma_start(out=a, in_=d),
                     reads=(w_key,), writes=(k,), dma=True)
        for do in range(NCHUNK):
            for sub in range(NSUB):
                cs = slice(sub * TSUB, (sub + 1) * TSUB)
                bank = f"ps{(do * NSUB + sub) % 4}"
                for c in range(NCHUNK):
                    p.op("pe", lambda e, w=sset[c][0], x=xb[c][0], do=do, cs=cs, b=bank,
                         st=(c == 0), sp=(c == NCHUNK - 1):
                         e.matmul(ps[b].ap()[:, :], w[:, do * 128:(do + 1) * 128], x[:, cs],
                                  start=st, stop=sp),
                         reads=(sset[c][1], xb[c][1]), writes=(bank,))
                o, ok = stg[(do * NSUB + sub) % 4]
                p.op("act", lambda e, o=o, b=bank, out_scale=out_scale: e.activation(
                     out=o, in_=ps[b].ap()[:, :], func=AF.Copy, scale=out_scale),
                     reads=(bank,), writes=(ok,))
                p.op("gp", lambda e, o=o, d=out_dram[do * 128:(do + 1) * 128, cs]: e.dma_start(out=d, in_=o),
                     reads=(ok,), writes=(f"{out_key}:{do}:{sub}",), dma="sink")
    p.barrier()
    ctx.ab.reset()
    ctx.af.reset()


def emit_attn_core(ctx, q_dram, k_dram, v_dram, ctx_dram, ctx_out_key, causal,
                   masks_dram=None, dbg=None, den_name="den", inv_name="invd"):
    nc, p = ctx.nc, ctx.p
    ps = ctx.s
    onesrow_t, _ = ctx.af.alloc(128)
    onesrow = onesrow_t[0:1, :]
    p.op("dve", lambda e: e.memset(onesrow, 1.0), writes=("onesrow",))
    maskb = None
    if causal:
        maskb, mk = ctx.ab.alloc(4 * TSUB)
        p.op("sp", lambda e: e.dma_start(out=maskb, in_=masks_dram),
             writes=("maskb",), dma=True)
    hsel, hselk = ctx.ab.alloc(128)
    p.op("sp", lambda e: e.dma_start(out=hsel, in_=ctx.d["c_hsel"]),
         writes=(hselk,), dma=True)
    s16, s16k = ctx.af.alloc(16 * 128)
    p.op("sp", lambda e: e.dma_start(out=s16[0:16, :], in_=ctx.d["c_selb16"]),
         writes=(s16k,), dma=True)
    sq2 = [ctx.ab.alloc(S) for _ in range(2)]
    qkc = [ctx.ab.alloc(S) for _ in range(2)]
    nmxa, nmxk = ctx.af.alloc(8)
    biasall, bak = ctx.af.alloc(16)
    halves = ["ps0", "ps1", "ps2", "ps3", "ps4", "ps5", "ps6", "ps7"]
    for oi, dram in ((0, q_dram), (1, k_dram)):
        for c in range(NCHUNK):
            buf, bk2 = qkc[c % 2]
            p.op("sp", lambda e, buf=buf, d=dram[c * 128:(c + 1) * 128, :]:
                 e.dma_start(out=buf, in_=d), writes=(bk2,), dma=True)
            sqa, sqk = sq2[c % 2]
            p.op("dve", lambda e, s=buf, d=sqa: e.tensor_mul(d, s, s),
                 reads=(bk2,), writes=(sqk,))
            for sub in range(NSUB):
                hb = halves[sub]
                p.op("pe", lambda e, s=sqa, sub=sub, c=c, hb=hb: e.matmul(
                     ps[hb].ap()[0:16, :], hsel[:, c * 16:(c + 1) * 16],
                     s[:, sub * TSUB:(sub + 1) * TSUB],
                     start=(c == 0), stop=(c == NCHUNK - 1)),
                     reads=(sqk, hselk), writes=(hb,))
        for sub in range(NSUB):
            hb = halves[sub]
            p.op("dve", lambda e, hb=hb, oi=oi, sub=sub: (
                 e.tensor_reduce(nmxa[0:16, oi:oi + 1], ps[hb].ap()[0:16, :], axis=AX.X, op=ALU.max)
                 if sub == 0 else
                 e.tensor_reduce(nmxa[0:16, 2:3], ps[hb].ap()[0:16, :], axis=AX.X, op=ALU.max)),
                 reads=(hb, nmxk), writes=(nmxk,))
            if sub > 0:
                p.op("dve", lambda e, oi=oi: e.tensor_max(nmxa[0:16, oi:oi + 1],
                     nmxa[0:16, oi:oi + 1], nmxa[0:16, 2:3]),
                     reads=(nmxk,), writes=(nmxk,))
    p.op("dve", lambda e: e.tensor_add(nmxa[0:16, 3:4], nmxa[0:16, 0:1], nmxa[0:16, 1:2]),
         reads=(nmxk,), writes=(nmxk,))
    p.op("dve", lambda e: e.tensor_scalar(out=nmxa[0:16, 3:4], in0=nmxa[0:16, 3:4],
         scalar1=-0.5, scalar2=None, op0=ALU.mult), reads=(nmxk,), writes=(nmxk,))
    for h in range(H):
        p.op("pe", lambda e, h=h: e.matmul(ps["ps6"].ap()[:, h:h + 1],
             s16[0:16, h * 128:(h + 1) * 128], nmxa[0:16, 3:4],
             start=(h == 0), stop=(h == H - 1)),
             reads=(nmxk, s16k), writes=("ps6",))
    p.op("dve", lambda e: e.tensor_copy(biasall, ps["ps6"].ap()[:, 0:16]),
         reads=("ps6",), writes=(bak,))
    qh = [ctx.ab.alloc(S) for _ in range(2)]
    kh = [ctx.ab.alloc(S) for _ in range(2)]
    vtm = [ctx.ab.alloc(32 * 96) for _ in range(2)]
    for va, vk in vtm:
        p.op("dve", lambda e, va=va: e.memset(
             va.rearrange("p (k c) -> p k c", c=96)[:, :, 64:65], 1.0),
             writes=(vk,))
    pT = [ctx.ab.alloc(2 * TSUB) for _ in range(3)]
    cstg = [ctx.ab.alloc(TSUB) for _ in range(2)]
    _ia = [ctx.af.alloc(TSUB) for _ in range(2)]
    pending = [None]
    for h in range(H):
        hp = h % 2
        qa, qk = qh[hp]
        ka, kk = kh[hp]
        va, vk = vtm[hp]
        rs = slice(64 * h, 64 * h + 64)
        p.op("sp", lambda e, a=qa, rs=rs: e.dma_start(out=a[0:64, :], in_=q_dram[rs, :]),
             writes=(qk,), dma=True)
        p.op("sp", lambda e, a=ka, rs=rs: e.dma_start(out=a[0:64, :], in_=k_dram[rs, :]),
             writes=(kk,), dma=True)
        for kt in range(32):
            p.op("sp", lambda e, a=va, kt=kt, rs=rs: e.dma_start(
                 out=a[:, kt * 96:kt * 96 + 64], in_=v_dram[rs, kt * 128:(kt + 1) * 128],
                 transpose=True),
                 writes=(vk,), dma=True)
        biash = biasall[:, h:h + 1]
        for sub in range(NSUB):
            cs = slice(sub * TSUB, (sub + 1) * TSUB)
            qs0 = sub * TSUB
            kts = [kt for kt in range(32) if not (causal and kt * 128 > qs0 + TSUB - 1)]
            groups = [kts[i:i + 2] for i in range(0, len(kts), 2)]
            ng = len(groups)
            def emit_scores(gi, groups=groups, qa=qa, ka=ka, qk=qk, kk=kk, cs=cs, qs0=qs0):
                bank = f"pp{gi % 3}"
                for j, kt in enumerate(groups[gi]):
                    p.op("pe", lambda e, kt=kt, j=j, bank=bank, qa=qa, ka=ka, cs=cs: e.matmul(
                         ps[bank].ap()[:, j * TSUB:(j + 1) * TSUB],
                         ka[0:64, kt * 128:(kt + 1) * 128], qa[0:64, cs],
                         start=True, stop=True),
                         reads=(qk, kk), writes=(bank,))
                w = len(groups[gi]) * TSUB
                pt, ptk = pT[gi % 3]
                p.op("act", lambda e, pt=pt, bank=bank, w=w, biash=biash: e.activation(
                     out=pt[:, 0:w], in_=ps[bank].ap()[:, 0:w], func=AF.Exp, bias=biash),
                     reads=(bank, bak), writes=(ptk,))
                if causal:
                    for j, kt in enumerate(groups[gi]):
                        if kt * 128 + 127 > qs0:
                            dd = (kt * 128 - qs0) // 128
                            assert 0 <= dd <= 3 and dd * 128 == kt * 128 - qs0
                            p.op("dve", lambda e, pt=pt, j=j, dd=dd: e.tensor_mul(
                                 pt[:, j * TSUB:(j + 1) * TSUB], pt[:, j * TSUB:(j + 1) * TSUB],
                                 maskb[:, dd * TSUB:(dd + 1) * TSUB]),
                                 reads=(ptk, "maskb"), writes=(ptk,))
            def emit_ctx(gi, groups=groups, ng=ng, va=va, vk=vk):
                for j, kt in enumerate(groups[gi]):
                    pt, ptk = pT[gi % 3]
                    st = (gi == 0 and j == 0)
                    sp_ = (gi == ng - 1 and j == len(groups[gi]) - 1)
                    p.op("pe", lambda e, pt=pt, j=j, kt=kt, st=st, sp=sp_, va=va: e.matmul(
                         ps["ps4"].ap()[0:65, :], va[:, kt * 96:kt * 96 + 65],
                         pt[:, j * TSUB:(j + 1) * TSUB], start=st, stop=sp),
                         reads=(ptk, vk), writes=("ps4",))
            def make_epilogue(sub=sub, cs=cs, rs=rs, h=h):
                par = sub % 2
                def emit_epi():
                    co, cok = cstg[par]
                    p.op("dve", lambda e, co=co: e.tensor_copy(co[0:64, :], ps["ps4"].ap()[0:64, :]),
                         reads=("ps4",), writes=(cok,))
                    dn, dnk = _ia[par]
                    p.op("dve", lambda e, dn=dn: e.tensor_copy(dn[0:1, :], ps["ps4"].ap()[64:65, :]),
                         reads=("ps4",), writes=(dnk,))
                    p.op("gp", lambda e, co=co, d=ctx_dram[rs, cs]: e.dma_start(out=d, in_=co[0:64, :]),
                         reads=(cok,), writes=(f"{ctx_out_key}:{h}:{sub}",), dma="sink")
                    p.op("gp", lambda e, dn=dn, d=ctx.d[den_name][h:h + 1, cs]: e.dma_start(out=d, in_=dn[0:1, :]),
                         reads=(dnk,), writes=(f"den:{h}:{sub}",), dma="sink")
                return emit_epi
            for gi in range(ng):
                emit_scores(gi)
                if gi == 1 and pending[0] is not None:
                    pending[0]()
                    pending[0] = None
                if gi >= 2:
                    emit_ctx(gi - 2)
            if pending[0] is not None:
                pending[0]()
                pending[0] = None
            for gi in range(max(0, ng - 2), ng):
                emit_ctx(gi)
            pending[0] = make_epilogue()
    pending[0]()
    dall, dallk = ctx.af.alloc(S)
    p.op("sp", lambda e: e.dma_start(out=dall[0:16, :], in_=ctx.d[den_name]),
         reads=tuple(f"den:{h}:{s}" for h in range(H) for s in range(NSUB)),
         writes=(dallk,), dma=True)
    p.op("act", lambda e: e.activation(out=dall[0:16, :], in_=dall[0:16, :], func=AF.Ln),
         reads=(dallk,), writes=(dallk,))
    p.op("act", lambda e: e.activation(out=dall[0:16, :], in_=dall[0:16, :], func=AF.Exp, scale=-1.0),
         reads=(dallk,), writes=(dallk,))
    p.op("gp", lambda e: e.dma_start(out=ctx.d[inv_name], in_=dall[0:16, :]),
         reads=(dallk,), writes=(f"{inv_name}w",), dma="sink")
    p.barrier()
    ctx.ab.reset()
    ctx.af.reset()


def _recip_exact(ctx, dst, dstk, srcden, srck, tmp, tmpk, rows=8):
    p = ctx.p
    p.op("dve", lambda e: e.reciprocal(dst[0:rows, :], srcden[0:rows, :]),
         reads=(srck,), writes=(dstk,))
    p.op("dve", lambda e: e.tensor_mul(tmp[0:rows, :], srcden[0:rows, :], dst[0:rows, :]),
         reads=(srck, dstk), writes=(tmpk,))
    p.op("dve", lambda e: e.tensor_scalar(out=tmp[0:rows, :], in0=tmp[0:rows, :],
         scalar1=-1.0, scalar2=2.0, op0=ALU.mult, op1=ALU.add), reads=(tmpk,), writes=(tmpk,))
    p.op("dve", lambda e: e.tensor_mul(dst[0:rows, :], dst[0:rows, :], tmp[0:rows, :]),
         reads=(dstk, tmpk), writes=(dstk,))


def emit_gate0(ctx, z_chunks, gate_w_f32, we_dram, we_key):
    nc, p = ctx.nc, ctx.p
    ps = ctx.s
    strips = [ctx.af.alloc(8) for _ in range(NCHUNK)]
    for c in range(NCHUNK):
        a, k = strips[c]
        for e_i in range(E):
            p.op("sp", lambda e, a=a, e_i=e_i, d=gate_w_f32[e_i, c * 128:(c + 1) * 128]:
                 e.dma_start(out=a[:, e_i:e_i + 1], in_=d),
                 writes=(k,), dma=True)
        strips[c] = (a, (k,))
    zf32 = [ctx.af.alloc(TSUB) for _ in range(2)]
    bufs = [ctx.af.alloc(TSUB) for _ in range(5)]
    (lg, lgk), (t1, t1k), (t2, t2k), (t3, t3k), (sh, shk) = bufs
    for sub in range(NSUB):
        cs = slice(sub * TSUB, (sub + 1) * TSUB)
        for c in range(NCHUNK):
            zf, zfk = zf32[c % 2]
            p.op("dve", lambda e, zf=zf, zc=z_chunks[c][0], cs=cs: e.tensor_copy(zf, zc[:, cs]),
                 reads=(z_chunks[c][1],), writes=(zfk,))
            p.op("pe", lambda e, c=c, zf=zf: e.matmul(ps["ps0"].ap()[0:8, :],
                 strips[c][0][:, 0:8], zf,
                 start=(c == 0), stop=(c == NCHUNK - 1)),
                 reads=strips[c][1] + (zfk,), writes=("ps0",))
        p.op("dve", lambda e: e.tensor_copy(lg[0:8, :], ps["ps0"].ap()[0:8, :]),
             reads=("ps0",), writes=(lgk,))
        _tree8(ctx, t1[0:8, :], t1k, lg[0:8, :], lgk, ALU.max, sh[0:8, :], shk)
        p.op("dve", lambda e: e.tensor_tensor(t2[0:8, :], lg[0:8, :], t1[0:8, :], ALU.is_equal),
             reads=(lgk, t1k), writes=(t2k,))
        _tree8(ctx, t3[0:8, :], t3k, t2[0:8, :], t2k, ALU.add, sh[0:8, :], shk)
        p.op("dve", lambda e: e.scalar_tensor_tensor(out=t2[0:8, :], in0=t2[0:8, :],
             scalar=-1e30, in1=lg[0:8, :], op0=ALU.mult, op1=ALU.add),
             reads=(t2k, lgk), writes=(t2k,))
        _tree8(ctx, t2[0:8, :], t2k, t2[0:8, :], t2k, ALU.max, sh[0:8, :], shk)
        p.op("dve", lambda e: e.tensor_scalar(out=ctx.s["u8t"].ap()[0:8, 0:TSUB], in0=t3[0:8, :],
             scalar1=1.5, scalar2=None, op0=ALU.is_ge), reads=(t3k,), writes=("u8t",))
        p.op("dve", lambda e: e.select(out=t2[0:8, :], mask=ctx.s["u8t"].ap()[0:8, 0:TSUB],
             on_true=t1[0:8, :], on_false=t2[0:8, :]),
             reads=("u8t", t1k, t2k), writes=(t2k,))
        p.op("dve", lambda e: e.tensor_tensor(t3[0:8, :], lg[0:8, :], t2[0:8, :], ALU.is_ge),
             reads=(lgk, t2k), writes=(t3k,))
        p.op("dve", lambda e: e.tensor_sub(t2[0:8, :], t2[0:8, :], t1[0:8, :]),
             reads=(t2k, t1k), writes=(t2k,))
        p.op("act", lambda e: e.activation(out=t2[0:8, :], in_=t2[0:8, :], func=AF.Exp),
             reads=(t2k,), writes=(t2k,))
        p.op("dve", lambda e: e.tensor_scalar(out=t2[0:8, :], in0=t2[0:8, :], scalar1=1.0,
             scalar2=None, op0=ALU.add), reads=(t2k,), writes=(t2k,))
        p.op("dve", lambda e: e.tensor_sub(t1[0:8, :], lg[0:8, :], t1[0:8, :]),
             reads=(lgk, t1k), writes=(t1k,))
        p.op("act", lambda e: e.activation(out=t1[0:8, :], in_=t1[0:8, :], func=AF.Exp),
             reads=(t1k,), writes=(t1k,))
        _recip_exact(ctx, sh, shk, t2, t2k, lg, lgk)
        p.op("dve", lambda e: e.tensor_mul(t1[0:8, :], t1[0:8, :], sh[0:8, :]),
             reads=(t1k, shk), writes=(t1k,))
        p.op("dve", lambda e: e.tensor_mul(t1[0:8, :], t1[0:8, :], t3[0:8, :]),
             reads=(t1k, t3k), writes=(t1k,))
        p.op("gp", lambda e, cs=cs: e.dma_start(out=we_dram[:, cs], in_=t1[0:8, :]),
             reads=(t1k,), writes=(f"{we_key}:{sub}",), dma="sink")


def _tree8(ctx, out, out_key, in_, in_key, op, sh, shk):
    p = ctx.p
    if out_key != in_key:
        p.op("dve", lambda e: e.tensor_copy(out, in_), reads=(in_key,), writes=(out_key,))
    for r in (4, 2, 1):
        mask = [(i + r) % 8 if i < 8 else i for i in range(32)]
        p.op("dve", lambda e, mask=mask: e.stream_shuffle(out=sh, in_=out, mask=mask),
             reads=(out_key,), writes=(shk,))
        p.op("dve", lambda e, op=op: e.tensor_tensor(out, out, sh, op),
             reads=(out_key, shk), writes=(out_key,))



def emit_zero_dram(ctx, dram, key, rows, cols, dtype=F32):
    p = ctx.p
    z, zk = (ctx.af if dtype == F32 else ctx.ab).alloc(min(cols, S))
    p.op("dve", lambda e: e.memset(z, 0.0), writes=(zk,))
    for r0 in range(0, rows, 128):
        rr = min(128, rows - r0)
        for c0 in range(0, cols, z.shape[1]):
            cc = min(z.shape[1], cols - c0)
            p.op("gp", lambda e, d=dram[r0:r0 + rr, c0:c0 + cc], s=z[0:rr, 0:cc]:
                 e.dma_start(out=d, in_=s),
                 reads=(zk,), writes=(f"{key}:{r0}:{c0}",), dma="sink")


def emit_moe0(ctx, z_dram, z_key, w1_bf, w2_bf, w3_bf, gate_wT_bf, we_dram,
              moe_dram, moe_key, selb8, selb8_key):
    nc, p = ctx.nc, ctx.p
    ps = ctx.s
    zc = load_chunks(ctx, z_dram, z_key)
    emit_gate0(ctx, zc, gate_wT_bf, we_dram, "we0")
    p.barrier()
    ctx.ab.reset()
    ctx.af.reset()
    emit_zero_dram(ctx, moe_dram, moe_key, D, S)
    wer, werk = ctx.af.alloc(S)
    p.op("sp", lambda e: e.dma_start(out=wer[0:8, :], in_=we_dram), reads=("we0",),
         writes=(werk,), dma=True)
    w1s = [ctx.ab.alloc(IMOE) for _ in range(NCHUNK)]
    w3s = [ctx.ab.alloc(IMOE) for _ in range(NCHUNK)]
    zs2 = [[ctx.ab.alloc(TSUB) for _ in range(NCHUNK)] for _ in range(2)]
    g = [ctx.ab.alloc(TSUB) for _ in range(16)]
    w2s = [ctx.ab.alloc(D) for _ in range(3)]
    sbf = [ctx.ab.alloc(TSUB) for _ in range(1)]
    vmb = [ctx.ab.alloc(TSUB) for _ in range(1)]
    webc = [ctx.af.alloc(TSUB) for _ in range(2)]
    stg = [ctx.af.alloc(TSUB) for _ in range(8)]
    def load_w13(e_i):
        for qi in range(4):
            for c in range(NCHUNK):
                p.op("sp", lambda e, a=w1s[c][0], qi=qi,
                     d=w1_bf[e_i][qi * 512:(qi + 1) * 512, c * 128:(c + 1) * 128]:
                     e.dma_start(out=a[:, qi * 512:(qi + 1) * 512], in_=d, transpose=True),
                     reads=("w1bf",), writes=(f"{w1s[c][1]}:{qi}",), dma=True)
                p.op("sp", lambda e, a=w3s[c][0], qi=qi,
                     d=w3_bf[e_i][qi * 512:(qi + 1) * 512, c * 128:(c + 1) * 128]:
                     e.dma_start(out=a[:, qi * 512:(qi + 1) * 512], in_=d, transpose=True),
                     reads=("w3bf",), writes=(f"{w3s[c][1]}:{qi}",), dma=True)
    load_w13(0)
    for e_i in range(E):
        for sub in range(NSUB):
            cs = slice(sub * TSUB, (sub + 1) * TSUB)
            zp = sub % 2
            for c in range(NCHUNK):
                p.op("sp", lambda e, a=zs2[zp][c][0], d=z_dram[c * 128:(c + 1) * 128, cs]:
                     e.dma_start(out=a, in_=d),
                     reads=(z_key,), writes=(zs2[zp][c][1],), dma=True)
            zs = zs2[zp]
            wb, wbk = webc[sub % 2]
            p.op("pe", lambda e, cs=cs, e_i=e_i: e.matmul(ps["ps6"].ap()[:, :],
                 selb8[0:8, e_i * 128:(e_i + 1) * 128], wer[0:8, cs], start=True, stop=True),
                 reads=(werk, selb8_key), writes=("ps6",))
            p.op("dve", lambda e, wb=wb: e.tensor_copy(wb, ps["ps6"].ap()[:, :]),
                 reads=("ps6",), writes=(wbk,))
            for ig in range(16):
                pu = "ps0" if ig % 2 == 0 else "ps1"
                pv = "ps2" if ig % 2 == 0 else "ps3"
                for c in range(NCHUNK):
                    p.op("pe", lambda e, c=c, ig=ig, pu=pu, zs=zs: e.matmul(ps[pu].ap()[:, :],
                         w1s[c][0][:, ig * 128:(ig + 1) * 128], zs[c][0],
                         start=(c == 0), stop=(c == NCHUNK - 1)),
                         reads=(f"{w1s[c][1]}:{ig // 4}", zs[c][1]), writes=(pu,))
                sb, sbk = sbf[0]
                p.op("act", lambda e, sb=sb, pu=pu: e.activation(out=sb, in_=ps[pu].ap()[:, :], func=AF.Silu),
                     reads=(pu,), writes=(sbk,))
                for c in range(NCHUNK):
                    p.op("pe", lambda e, c=c, ig=ig, pv=pv, zs=zs: e.matmul(ps[pv].ap()[:, :],
                         w3s[c][0][:, ig * 128:(ig + 1) * 128], zs[c][0],
                         start=(c == 0), stop=(c == NCHUNK - 1)),
                         reads=(f"{w3s[c][1]}:{ig // 4}", zs[c][1]), writes=(pv,))
                vm, vmk = vmb[0]
                p.op("dve", lambda e, vm=vm, pv=pv, wb=wb: e.tensor_mul(vm, ps[pv].ap()[:, :], wb),
                     reads=(pv, wbk), writes=(vmk,))
                p.op("dve", lambda e, gt=g[ig][0], sb=sb, vm=vm: e.tensor_mul(gt, sb, vm),
                     reads=(sbk, vmk), writes=(g[ig][1],))
            if sub == NSUB - 1 and e_i + 1 < E:
                load_w13(e_i + 1)
            def w2load(ig):
                ws, wsk = w2s[ig % 3]
                p.op("sp", lambda e, ws=ws, ig=ig, d=w2_bf[e_i][:, ig * 128:(ig + 1) * 128]:
                     e.dma_start(out=ws, in_=d, transpose=True),
                     reads=("w2bf",), writes=(wsk,), dma=True)
            for ig in range(3):
                w2load(ig)
            for ig in range(16):
                ws, wsk = w2s[ig % 3]
                for ho in range(NCHUNK):
                    p.op("pe", lambda e, ws=ws, ho=ho, ig=ig: e.matmul(ps[f"ps{ho}"].ap()[:, :],
                         ws[:, ho * 128:(ho + 1) * 128], g[ig][0],
                         start=(ig == 0), stop=(ig == 15)),
                         reads=(wsk, g[ig][1]), writes=(f"ps{ho}",))
                if ig + 3 < 16:
                    w2load(ig + 3)
            for ho in (6, 0, 2, 7, 1, 3, 4, 5):
                st, stk = stg[ho]
                p.op("dve", lambda e, st=st, ho=ho: e.tensor_copy(st, ps[f"ps{ho}"].ap()[:, :]),
                     reads=(f"ps{ho}",), writes=(stk,))
                p.op("gp", lambda e, st=st, d=moe_dram[ho * 128:(ho + 1) * 128, cs]:
                     e.dma_start(out=d, in_=st, accum_op=ALU.add),
                     reads=(stk,), writes=(f"{moe_key}:{ho * 128}:{sub * TSUB}",), dma="sink")
    p.barrier()
    ctx.ab.reset()
    ctx.af.reset()


def emit_combine(ctx, x_dram, x_key, h_dram, h_key, moe_dram, moe_key,
                 out_dram, out_key):
    p = ctx.p
    bufs = [[ctx.af.alloc(2048) for _ in range(3)] for _ in range(2)]
    for c in range(NCHUNK):
        for hf in range(2):
            cs = slice(hf * 2048, (hf + 1) * 2048)
            (xa, xk), (ha, hk), (ma, mk) = bufs[(c * 2 + hf) % 2]
            p.op("sp", lambda e, a=xa, d=x_dram[c * 128:(c + 1) * 128, cs]: e.dma_start(out=a, in_=d),
                 reads=(x_key,), writes=(xk,), dma=True)
            p.op("sp", lambda e, a=ha, d=h_dram[c * 128:(c + 1) * 128, cs]: e.dma_start(out=a, in_=d),
                 reads=(h_key,), writes=(hk,), dma=True)
            p.op("sp", lambda e, a=ma, d=moe_dram[c * 128:(c + 1) * 128, cs]: e.dma_start(out=a, in_=d),
                 reads=(moe_key,), writes=(mk,), dma=True)
            p.op("dve", lambda e, ha=ha, xa=xa: e.scalar_tensor_tensor(out=ha, in0=ha,
                 scalar=0.5, in1=xa, op0=ALU.mult, op1=ALU.add),
                 reads=(hk, xk), writes=(hk,))
            p.op("dve", lambda e, ma=ma, ha=ha: e.scalar_tensor_tensor(out=ma, in0=ma,
                 scalar=0.5, in1=ha, op0=ALU.mult, op1=ALU.add),
                 reads=(mk, hk), writes=(mk,))
            p.op("gp", lambda e, a=ma, d=out_dram[c * 128:(c + 1) * 128, cs]: e.dma_start(out=d, in_=a),
                 reads=(mk,), writes=(f"{out_key}:{c}:{hf}",), dma="sink")
    p.barrier()
    ctx.ab.reset()
    ctx.af.reset()


def emit_l0attn_win(ctx, q_dram, k_dram, v_dram, ce_dram, co_dram, ce_key, co_key):
    nc, p = ctx.nc, ctx.p
    ps = ctx.s
    zb, zbk = ctx.ab.alloc(128)
    p.op("dve", lambda e: e.memset(zb, 0.0), writes=(zbk,))
    on1, onk = ctx.af.alloc(128)
    p.op("dve", lambda e: e.memset(on1, 1.0), writes=(onk,))
    for c in range(NCHUNK):
        p.op("gp", lambda e, d=co_dram[c * 128:(c + 1) * 128, 0:128]: e.dma_start(out=d, in_=zb),
             reads=(zbk,), writes=(f"{co_key}:z0:{c}",), dma="sink")
        p.op("gp", lambda e, d=co_dram[c * 128:(c + 1) * 128, 3968:4096]: e.dma_start(out=d, in_=zb),
             reads=(zbk,), writes=(f"{co_key}:z1:{c}",), dma="sink")
    p.op("gp", lambda e: e.dma_start(out=ctx.d["den_o"][0:16, 0:128], in_=on1[0:16, :]),
         reads=(onk,), writes=("den_o:z0",), dma="sink")
    p.op("gp", lambda e: e.dma_start(out=ctx.d["den_o"][0:16, 3968:4096], in_=on1[0:16, :]),
         reads=(onk,), writes=("den_o:z1",), dma="sink")
    hsel, hselk = ctx.ab.alloc(128)
    p.op("sp", lambda e: e.dma_start(out=hsel, in_=ctx.d["c_hsel"]),
         writes=(hselk,), dma=True)
    s16, s16k = ctx.af.alloc(16 * 128)
    p.op("sp", lambda e: e.dma_start(out=s16[0:16, :], in_=ctx.d["c_selb16"]),
         writes=(s16k,), dma=True)
    sq2 = [ctx.ab.alloc(S) for _ in range(2)]
    qkc = [ctx.ab.alloc(S) for _ in range(2)]
    nmxa, nmxk = ctx.af.alloc(8)
    biasall, bak = ctx.af.alloc(16)
    halves = ["ps0", "ps1", "ps2", "ps3", "ps4", "ps5", "ps6", "ps7"]
    for oi, dram in ((0, q_dram), (1, k_dram)):
        for c in range(NCHUNK):
            buf, bk2 = qkc[c % 2]
            p.op("sp", lambda e, buf=buf, d=dram[c * 128:(c + 1) * 128, :]:
                 e.dma_start(out=buf, in_=d), writes=(bk2,), dma=True)
            sqa, sqk = sq2[c % 2]
            p.op("dve", lambda e, s=buf, d=sqa: e.tensor_mul(d, s, s),
                 reads=(bk2,), writes=(sqk,))
            for sub in range(NSUB):
                hb = halves[sub]
                p.op("pe", lambda e, s=sqa, sub=sub, c=c, hb=hb: e.matmul(
                     ps[hb].ap()[0:16, :], hsel[:, c * 16:(c + 1) * 16],
                     s[:, sub * TSUB:(sub + 1) * TSUB],
                     start=(c == 0), stop=(c == NCHUNK - 1)),
                     reads=(sqk, hselk), writes=(hb,))
        for sub in range(NSUB):
            hb = halves[sub]
            p.op("dve", lambda e, hb=hb, oi=oi, sub=sub: (
                 e.tensor_reduce(nmxa[0:16, oi:oi + 1], ps[hb].ap()[0:16, :], axis=AX.X, op=ALU.max)
                 if sub == 0 else
                 e.tensor_reduce(nmxa[0:16, 2:3], ps[hb].ap()[0:16, :], axis=AX.X, op=ALU.max)),
                 reads=(hb, nmxk), writes=(nmxk,))
            if sub > 0:
                p.op("dve", lambda e, oi=oi: e.tensor_max(nmxa[0:16, oi:oi + 1],
                     nmxa[0:16, oi:oi + 1], nmxa[0:16, 2:3]),
                     reads=(nmxk,), writes=(nmxk,))
    p.op("dve", lambda e: e.tensor_add(nmxa[0:16, 3:4], nmxa[0:16, 0:1], nmxa[0:16, 1:2]),
         reads=(nmxk,), writes=(nmxk,))
    p.op("dve", lambda e: e.tensor_scalar(out=nmxa[0:16, 3:4], in0=nmxa[0:16, 3:4],
         scalar1=-0.5, scalar2=None, op0=ALU.mult), reads=(nmxk,), writes=(nmxk,))
    for h in range(H):
        p.op("pe", lambda e, h=h: e.matmul(ps["ps6"].ap()[:, h:h + 1],
             s16[0:16, h * 128:(h + 1) * 128], nmxa[0:16, 3:4],
             start=(h == 0), stop=(h == H - 1)),
             reads=(nmxk, s16k), writes=("ps6",))
    p.op("dve", lambda e: e.tensor_copy(biasall, ps["ps6"].ap()[:, 0:16]),
         reads=("ps6",), writes=(bak,))
    qh = [ctx.ab.alloc(S) for _ in range(2)]
    kh = [ctx.ab.alloc(S) for _ in range(2)]
    vtm = [ctx.ab.alloc(32 * 96) for _ in range(2)]
    for va, vk in vtm:
        p.op("dve", lambda e, va=va: e.memset(
             va.rearrange("p (k c) -> p k c", c=96)[:, :, 64:65], 1.0),
             writes=(vk,))
    pT = [ctx.ab.alloc(TSUB) for _ in range(3)]
    cstg = [ctx.ab.alloc(WIN) for _ in range(2)]
    dstg = [ctx.af.alloc(WIN) for _ in range(2)]
    pending = [None]
    for h in range(H):
        hp = h % 2
        qa, qk = qh[hp]
        ka, kk = kh[hp]
        va, vk = vtm[hp]
        rs = slice(64 * h, 64 * h + 64)
        p.op("sp", lambda e, a=qa, rs=rs: e.dma_start(out=a[0:64, :], in_=q_dram[rs, :]),
             writes=(qk,), dma=True)
        p.op("sp", lambda e, a=ka, rs=rs: e.dma_start(out=a[0:64, :], in_=k_dram[rs, :]),
             writes=(kk,), dma=True)
        for kt in range(32):
            p.op("sp", lambda e, a=va, kt=kt, rs=rs: e.dma_start(
                 out=a[:, kt * 96:kt * 96 + 64], in_=v_dram[rs, kt * 128:(kt + 1) * 128],
                 transpose=True),
                 writes=(vk,), dma=True)
        biash = biasall[:, h:h + 1]
        for n in range(NWIN):
            cs = slice(n * STRIDE, n * STRIDE + WIN)
            cbank = "ps4" if n % 2 == 0 else "ps5"
            def emit_scores(n=n, cs=cs, qa=qa, ka=ka, qk=qk, kk=kk):
                bank = f"pp{n % 3}"
                for j, kt in enumerate((n, n + 1)):
                    p.op("pe", lambda e, kt=kt, j=j, bank=bank, qa=qa, ka=ka, cs=cs: e.matmul(
                         ps[bank].ap()[:, j * WIN:(j + 1) * WIN],
                         ka[0:64, kt * 128:(kt + 1) * 128], qa[0:64, cs],
                         start=True, stop=True),
                         reads=(qk, kk), writes=(bank,))
                pt, ptk = pT[n % 3]
                p.op("act", lambda e, pt=pt, bank=bank, biash=biash: e.activation(
                     out=pt, in_=ps[bank].ap()[:, 0:2 * WIN], func=AF.Exp, bias=biash),
                     reads=(bank, bak), writes=(ptk,))
            def emit_ctx(n=n, cbank=cbank, va=va, vk=vk):
                pt, ptk = pT[n % 3]
                for j, kt in enumerate((n, n + 1)):
                    p.op("pe", lambda e, pt=pt, j=j, kt=kt, st=(j == 0), sp=(j == 1),
                         va=va, cbank=cbank: e.matmul(
                         ps[cbank].ap()[0:65, 0:WIN], va[:, kt * 96:kt * 96 + 65],
                         pt[:, j * WIN:(j + 1) * WIN], start=st, stop=sp),
                         reads=(ptk, vk), writes=(cbank,))
            def make_epi(n=n, cs=cs, rs=rs, h=h, cbank=cbank):
                par = n % 2
                tgt = ce_dram if par == 0 else co_dram
                tkey = ce_key if par == 0 else co_key
                dent = ctx.d["den_e"] if par == 0 else ctx.d["den_o"]
                def emit_epi():
                    co, cok = cstg[par]
                    p.op("dve", lambda e, co=co, cbank=cbank: e.tensor_copy(co[0:64, :], ps[cbank].ap()[0:64, 0:WIN]),
                         reads=(cbank,), writes=(cok,))
                    dn, dnk = dstg[par]
                    p.op("dve", lambda e, dn=dn, cbank=cbank: e.tensor_copy(dn[0:1, :], ps[cbank].ap()[64:65, 0:WIN]),
                         reads=(cbank,), writes=(dnk,))
                    p.op("gp", lambda e, co=co, d=tgt[rs, cs]: e.dma_start(out=d, in_=co[0:64, :]),
                         reads=(cok,), writes=(f"{tkey}:{h}:{n}",), dma="sink")
                    p.op("gp", lambda e, dn=dn, d=dent[h:h + 1, cs]: e.dma_start(out=d, in_=dn[0:1, :]),
                         reads=(dnk,), writes=(f"denw:{h}:{n}",), dma="sink")
                return emit_epi
            emit_scores()
            if pending[0] is not None:
                pending[0]()
                pending[0] = None
            emit_ctx()
            pending[0] = make_epi()
    pending[0]()
    dall, dallk = ctx.af.alloc(S)
    for dname, iname, deps in (("den_e", "inv_e", tuple(f"denw:{h}:{n}" for h in range(H) for n in range(0, NWIN, 2))),
                               ("den_o", "inv_o", tuple(f"denw:{h}:{n}" for h in range(H) for n in range(1, NWIN, 2)) + ("den_o:z0", "den_o:z1"))):
        p.op("sp", lambda e, dname=dname, dall=dall: e.dma_start(out=dall[0:16, :], in_=ctx.d[dname]),
             reads=deps, writes=(dallk,), dma=True)
        p.op("act", lambda e, dall=dall: e.activation(out=dall[0:16, :], in_=dall[0:16, :], func=AF.Ln),
             reads=(dallk,), writes=(dallk,))
        p.op("act", lambda e, dall=dall: e.activation(out=dall[0:16, :], in_=dall[0:16, :], func=AF.Exp, scale=-1.0),
             reads=(dallk,), writes=(dallk,))
        p.op("gp", lambda e, iname=iname, dall=dall: e.dma_start(out=ctx.d[iname], in_=dall[0:16, :]),
             reads=(dallk,), writes=(f"{iname}w",), dma="sink")
    p.barrier()
    ctx.ab.reset()
    ctx.af.reset()


def emit_wo_win(ctx, ce_dram, ce_key, co_dram, co_key, wo_bf, x_dram, x_key,
                invc_row_dram, out_dram, out_key):
    nc, p = ctx.nc, ctx.p
    ps = ctx.s
    onesrow_t, _ = ctx.af.alloc(128)
    onesrow = onesrow_t[0:1, :]
    p.op("dve", lambda e: e.memset(onesrow, 1.0), writes=("onesrow",))
    invr, invk = ctx.af.alloc(S)
    p.op("sp", lambda e: e.dma_start(out=invr[0:1, :], in_=invc_row_dram),
         writes=(invk,), dma=True)
    p.op("dve", lambda e: e.tensor_scalar(out=invr[0:1, :], in0=invr[0:1, :],
         scalar1=1e-8, scalar2=None, op0=ALU.add), reads=(invk,), writes=(invk,))
    p.op("dve", lambda e: e.reciprocal(invr[0:1, :], invr[0:1, :]), reads=(invk,), writes=(invk,))
    strips = [ctx.ab.alloc(D) for _ in range(NCHUNK)]
    for c in range(NCHUNK):
        p.op("sp", lambda e, a=strips[c][0], d=wo_bf[c * 128:(c + 1) * 128, :]:
             e.dma_start(out=a, in_=d), writes=(strips[c][1],), dma=True)
    cbufs = [[ctx.ab.alloc(TSUB) for _ in range(NCHUNK)] for _ in range(2)]
    obufs = [[ctx.ab.alloc(TSUB) for _ in range(NCHUNK)] for _ in range(2)]
    ibc = [ctx.af.alloc(TSUB) for _ in range(2)]
    xst = [ctx.af.alloc(TSUB) for _ in range(2)]
    ost = [ctx.af.alloc(TSUB) for _ in range(2)]
    selp, selpk = ctx.af.alloc(128)
    p.op("sp", lambda e: e.dma_start(out=selp[0:2, :], in_=ctx.d["c_selp"]),
         writes=(selpk,), dma=True)
    iv2 = [[ctx.af.alloc(TSUB) for _ in range(2)] for _ in range(2)]
    for sub in range(NSUB):
        cs = slice(sub * TSUB, (sub + 1) * TSUB)
        par = sub % 2
        for c in range(NCHUNK):
            p.op("sp", lambda e, a=cbufs[par][c][0], d=ce_dram[c * 128:(c + 1) * 128, cs]:
                 e.dma_start(out=a, in_=d), reads=(ce_key,), writes=(cbufs[par][c][1],), dma=True)
            p.op("sp", lambda e, a=obufs[par][c][0], d=co_dram[c * 128:(c + 1) * 128, cs]:
                 e.dma_start(out=a, in_=d), reads=(co_key,), writes=(obufs[par][c][1],), dma=True)
        for c in range(NCHUNK):
            for si, (bufpair, iname, ikey) in enumerate(((cbufs[par][c], "inv_e", "inv_ew"),
                                                         (obufs[par][c], "inv_o", "inv_ow"))):
                a, akey = bufpair
                i2, i2k = iv2[si][c % 2]
                p.op("sp", lambda e, i2=i2, d=ctx.d[iname][2 * c:2 * c + 2, cs]:
                     e.dma_start(out=i2[0:2, 0:TSUB], in_=d),
                     reads=(ikey,), writes=(i2k,), dma=True)
                p.op("pe", lambda e, i2=i2: e.matmul(ctx.s["ps6"].ap()[:, :],
                     selp[0:2, :], i2[0:2, 0:TSUB], start=True, stop=True),
                     reads=(i2k, selpk), writes=("ps6",))
                p.op("dve", lambda e, a=a: e.tensor_mul(a, a, ctx.s["ps6"].ap()[:, :]),
                     reads=(akey, "ps6"), writes=(akey,))
        ib, ibk = ibc[par]
        p.op("pe", lambda e, cs=cs: e.matmul(ps["ps6"].ap()[:, :], onesrow, invr[0:1, cs],
             start=True, stop=True), reads=(invk, "onesrow"), writes=("ps6",))
        p.op("dve", lambda e, ib=ib: e.tensor_copy(ib, ps["ps6"].ap()[:, :]),
             reads=("ps6",), writes=(ibk,))
        for do in range(NCHUNK):
            bank = f"ps{do % 4}"
            for c in range(NCHUNK):
                p.op("pe", lambda e, c=c, do=do, bank=bank, par=par: e.matmul(ps[bank].ap()[:, :],
                     strips[c][0][:, do * 128:(do + 1) * 128], cbufs[par][c][0],
                     start=(c == 0), stop=False),
                     reads=(strips[c][1], cbufs[par][c][1]), writes=(bank,))
            for c in range(NCHUNK):
                p.op("pe", lambda e, c=c, do=do, bank=bank, par=par: e.matmul(ps[bank].ap()[:, :],
                     strips[c][0][:, do * 128:(do + 1) * 128], obufs[par][c][0],
                     start=False, stop=(c == NCHUNK - 1)),
                     reads=(strips[c][1], obufs[par][c][1]), writes=(bank,))
            xa, xk2 = xst[do % 2]
            p.op("sp", lambda e, a=xa, d=x_dram[do * 128:(do + 1) * 128, cs]:
                 e.dma_start(out=a, in_=d), reads=(x_key,), writes=(xk2,), dma=True)
            oa, ok2 = ost[do % 2]
            p.op("dve", lambda e, oa=oa, bank=bank, ib=ib: e.tensor_mul(oa, ps[bank].ap()[:, :], ib),
                 reads=(bank, ibk), writes=(ok2,))
            p.op("dve", lambda e, oa=oa, xa=xa: e.tensor_add(oa, oa, xa),
                 reads=(ok2, xk2), writes=(ok2,))
            p.op("gp", lambda e, oa=oa, d=out_dram[do * 128:(do + 1) * 128, cs]:
                 e.dma_start(out=d, in_=oa), reads=(ok2,), writes=(f"{out_key}:{do}:{sub}",), dma="sink")
    p.barrier()
    ctx.ab.reset()
    ctx.af.reset()


def emit_gate1(ctx, z_dram, z_key, gate_w_bf, gate_b_dram, masked_dram, masked_key):
    nc, p = ctx.nc, ctx.p
    ps = ctx.s
    strips = [ctx.af.alloc(8) for _ in range(NCHUNK)]
    for c in range(NCHUNK):
        a, k = strips[c]
        p.op("sp", lambda e, a=a, d=gate_w_bf[c * 128:(c + 1) * 128, :]:
             e.dma_start(out=a, in_=d), writes=(k,), dma=True)
    bcol, bck = ctx.af.alloc(1)
    p.op("sp", lambda e: e.dma_start(out=bcol[0:8, :], in_=gate_b_dram),
         writes=(bck,), dma=True)
    zc = [ctx.ab.alloc(TSUB) for _ in range(NCHUNK)]
    zf32 = [ctx.af.alloc(TSUB) for _ in range(2)]
    bufs = [ctx.af.alloc(TSUB) for _ in range(5)]
    (lg, lgk), (t1, t1k), (t2, t2k), (t3, t3k), (sh, shk) = bufs
    for sub in range(NSUB):
        cs = slice(sub * TSUB, (sub + 1) * TSUB)
        for c in range(NCHUNK):
            p.op("sp", lambda e, a=zc[c][0], d=z_dram[c * 128:(c + 1) * 128, cs]:
                 e.dma_start(out=a, in_=d), reads=(z_key,), writes=(zc[c][1],), dma=True)
        for c in range(NCHUNK):
            zf, zfk = zf32[c % 2]
            p.op("dve", lambda e, zf=zf, zc=zc[c][0]: e.tensor_copy(zf, zc),
                 reads=(zc[c][1],), writes=(zfk,))
            p.op("pe", lambda e, c=c, zf=zf: e.matmul(ps["ps0"].ap()[0:8, :],
                 strips[c][0][:, 0:8], zf,
                 start=(c == 0), stop=(c == NCHUNK - 1)),
                 reads=(strips[c][1], zfk), writes=("ps0",))
        p.op("dve", lambda e: e.tensor_scalar(out=lg[0:8, :], in0=ps["ps0"].ap()[0:8, :],
             scalar1=bcol[0:8, :], scalar2=None, op0=ALU.add),
             reads=("ps0", bck), writes=(lgk,))
        _tree8(ctx, t1[0:8, :], t1k, lg[0:8, :], lgk, ALU.max, sh[0:8, :], shk)
        p.op("dve", lambda e: e.tensor_sub(t2[0:8, :], lg[0:8, :], t1[0:8, :]),
             reads=(lgk, t1k), writes=(t2k,))
        p.op("act", lambda e: e.activation(out=t2[0:8, :], in_=t2[0:8, :], func=AF.Exp),
             reads=(t2k,), writes=(t2k,))
        _tree8(ctx, t3[0:8, :], t3k, t2[0:8, :], t2k, ALU.add, sh[0:8, :], shk)
        _recip_exact(ctx, sh, shk, t3, t3k, lg, lgk)
        p.op("dve", lambda e: e.tensor_mul(lg[0:8, :], t2[0:8, :], sh[0:8, :]),
             reads=(t2k, shk), writes=(lgk,))
        _tree8(ctx, t1[0:8, :], t1k, lg[0:8, :], lgk, ALU.max, sh[0:8, :], shk)
        p.op("dve", lambda e: e.tensor_tensor(t2[0:8, :], lg[0:8, :], t1[0:8, :], ALU.is_equal),
             reads=(lgk, t1k), writes=(t2k,))
        _tree8(ctx, t3[0:8, :], t3k, t2[0:8, :], t2k, ALU.add, sh[0:8, :], shk)
        p.op("dve", lambda e: e.scalar_tensor_tensor(out=t2[0:8, :], in0=t2[0:8, :],
             scalar=-1e30, in1=lg[0:8, :], op0=ALU.mult, op1=ALU.add),
             reads=(t2k, lgk), writes=(t2k,))
        _tree8(ctx, t2[0:8, :], t2k, t2[0:8, :], t2k, ALU.max, sh[0:8, :], shk)
        p.op("dve", lambda e: e.tensor_scalar(out=ctx.s["u8t"].ap()[0:8, 0:TSUB], in0=t3[0:8, :],
             scalar1=1.5, scalar2=None, op0=ALU.is_ge), reads=(t3k,), writes=("u8t",))
        p.op("dve", lambda e: e.select(out=t2[0:8, :], mask=ctx.s["u8t"].ap()[0:8, 0:TSUB],
             on_true=t1[0:8, :], on_false=t2[0:8, :]),
             reads=("u8t", t1k, t2k), writes=(t2k,))
        p.op("dve", lambda e: e.tensor_tensor(t3[0:8, :], lg[0:8, :], t2[0:8, :], ALU.is_ge),
             reads=(lgk, t2k), writes=(t3k,))
        p.op("dve", lambda e: e.tensor_mul(t1[0:8, :], lg[0:8, :], t3[0:8, :]),
             reads=(lgk, t3k), writes=(t1k,))
        p.op("gp", lambda e, cs=cs: e.dma_start(out=masked_dram[:, cs], in_=t1[0:8, :]),
             reads=(t1k,), writes=(f"{masked_key}:{sub}",), dma="sink")
    p.barrier()
    ctx.ab.reset()
    ctx.af.reset()


def emit_l1ffn(ctx, z_dram, z_key, mall_dram, mown_dram, w1_bf, b1_dram,
               w2_bf, b2_bf, x_dram, x_key, out_dram, out_key, selb8, selb8_key):
    nc, p = ctx.nc, ctx.p
    ps = ctx.s
    den = [ctx.af.alloc(TSUB) for _ in range(2)]
    acc, acck = ctx.af.alloc(TSUB)
    own, ownk = ctx.af.alloc(TSUB)
    r32, r32k = ctx.af.alloc(1024)
    r8, r8k = ctx.af.alloc(S)
    for sub in range(NSUB):
        cs = slice(sub * TSUB, (sub + 1) * TSUB)
        a0, k0 = den[0]
        a1, k1 = den[1]
        for b in range(8):
            dst = (a0, k0) if b % 2 == 0 else (a1, k1)
            p.op("sp", lambda e, d=mall_dram[b, :, cs], a=dst[0]: e.dma_start(out=a[0:8, :], in_=d),
                 writes=(dst[1],), dma=True)
            if b == 0:
                p.op("dve", lambda e, a=dst[0]: e.tensor_copy(acc[0:8, :], a[0:8, :]),
                     reads=(dst[1],), writes=(acck,))
            else:
                p.op("dve", lambda e, a=dst[0]: e.tensor_add(acc[0:8, :], acc[0:8, :], a[0:8, :]),
                     reads=(dst[1], acck), writes=(acck,))
        p.op("dve", lambda e: e.tensor_scalar(out=acc[0:8, :], in0=acc[0:8, :],
             scalar1=L1F_EPS, scalar2=None, op0=ALU.add), reads=(acck,), writes=(acck,))
        p.op("sp", lambda e, d=mown_dram[:, cs]: e.dma_start(out=own[0:8, :], in_=d),
             writes=(ownk,), dma=True)
        _recip_exact(ctx, den[0][0], den[0][1], acc, acck, den[1][0], den[1][1])
        p.op("dve", lambda e: e.tensor_mul(own[0:8, :], own[0:8, :], den[0][0][0:8, :]),
             reads=(ownk, den[0][1]), writes=(ownk,))
        p.op("dve", lambda e, cs=cs: e.tensor_scalar(out=r8[0:8, cs], in0=own[0:8, :],
             scalar1=8.0, scalar2=None, op0=ALU.mult), reads=(ownk,), writes=(r8k,))
    p.op("gp", lambda e: e.dma_start(out=ctx.d["r8_scratch"], in_=r8[0:8, :]),
         reads=(r8k,), writes=("r8d",), dma=True)
    p.op("sp", lambda e: e.dma_start(out=r32[0:32, :],
         in_=ctx.d["r8_scratch"].rearrange("ee (c t) -> c ee t", c=4)),
         reads=("r8d",), writes=(r32k,), dma=True)
    m32, m32k = ctx.af.alloc(1024)
    cols = [ctx.af.alloc(1) for _ in range(9)]
    (lo, lok), (hi, hik), (mid, midk), (cnt, cntk), (shc, shck), (tcol, tck), \
        (ucol, uck), (d1, d1k), (d2, d2k) = cols
    p.op("dve", lambda e: e.memset(lo[0:32, :], -1.0), writes=(lok,))
    p.op("dve", lambda e: e.memset(hi[0:32, :], 9.0), writes=(hik,))
    u8c = ctx.s["u8c"].ap()
    for it in range(60):
        p.op("dve", lambda e: e.tensor_add(mid[0:32, :], lo[0:32, :], hi[0:32, :]),
             reads=(lok, hik), writes=(midk,))
        p.op("dve", lambda e: e.tensor_scalar(out=mid[0:32, :], in0=mid[0:32, :],
             scalar1=0.5, scalar2=None, op0=ALU.mult), reads=(midk,), writes=(midk,))
        p.op("dve", lambda e: e.tensor_scalar(out=m32[0:32, :], in0=r32[0:32, :],
             scalar1=mid[0:32, :], scalar2=None, op0=ALU.is_gt, op1=ALU.add,
             accum_out=cnt[0:32, :]),
             reads=(r32k, midk), writes=(m32k, cntk))
        for rot in (8, 16):
            mask = [(i + rot) % 32 for i in range(32)]
            p.op("dve", lambda e, mask=mask: e.stream_shuffle(out=shc[0:32, :], in_=cnt[0:32, :], mask=mask),
                 reads=(cntk,), writes=(shck,))
            p.op("dve", lambda e: e.tensor_add(cnt[0:32, :], cnt[0:32, :], shc[0:32, :]),
                 reads=(cntk, shck), writes=(cntk,))
        p.op("dve", lambda e: e.tensor_scalar(out=ucol[0:32, :], in0=cnt[0:32, :],
             scalar1=float(KCAP), scalar2=None, op0=ALU.is_ge), reads=(cntk,), writes=(uck,))
        p.op("dve", lambda e: e.tensor_sub(d1[0:32, :], mid[0:32, :], lo[0:32, :]),
             reads=(midk, lok), writes=(d1k,))
        p.op("dve", lambda e: e.tensor_sub(d2[0:32, :], hi[0:32, :], mid[0:32, :]),
             reads=(hik, midk), writes=(d2k,))
        p.op("dve", lambda e: e.scalar_tensor_tensor(out=lo[0:32, :], in0=d1[0:32, :],
             scalar=ucol[0:32, :], in1=lo[0:32, :], op0=ALU.mult, op1=ALU.add),
             reads=(d1k, uck, lok), writes=(lok,))
        p.op("dve", lambda e: e.scalar_tensor_tensor(out=hi[0:32, :], in0=d2[0:32, :],
             scalar=ucol[0:32, :], in1=mid[0:32, :], op0=ALU.mult, op1=ALU.add),
             reads=(d2k, uck, midk), writes=(hik,))
    if ctx.d.get("dbg_bis") is not None:
        dbgb = ctx.d["dbg_bis"]
        p.op("gp", lambda e: e.dma_start(out=dbgb[0:32, 0:1], in_=lo[0:32, :], allow_slow_non_contiguous=True),
             reads=(lok,), writes=("dbgb1",), dma="sink")
        p.op("gp", lambda e: e.dma_start(out=dbgb[0:32, 1:2], in_=hi[0:32, :], allow_slow_non_contiguous=True),
             reads=(hik,), writes=("dbgb2",), dma="sink")
        p.op("gp", lambda e: e.dma_start(out=dbgb[0:32, 2:3], in_=cnt[0:32, :], allow_slow_non_contiguous=True),
             reads=(cntk,), writes=("dbgb3",), dma="sink")

    p.op("dve", lambda e: e.tensor_scalar(out=m32[0:32, :], in0=r32[0:32, :],
         scalar1=hi[0:32, :], scalar2=None, op0=ALU.is_gt, op1=ALU.add,
         accum_out=cnt[0:32, :]),
         reads=(r32k, hik), writes=(m32k, cntk))
    for rot in (8, 16):
        mask = [(i + rot) % 32 for i in range(32)]
        p.op("dve", lambda e, mask=mask: e.stream_shuffle(out=shc[0:32, :], in_=cnt[0:32, :], mask=mask),
             reads=(cntk,), writes=(shck,))
        p.op("dve", lambda e: e.tensor_add(cnt[0:32, :], cnt[0:32, :], shc[0:32, :]),
             reads=(cntk, shck), writes=(cntk,))
    p.op("dve", lambda e: e.tensor_scalar(out=tcol[0:32, :], in0=cnt[0:32, :],
         scalar1=-1.0, scalar2=float(KCAP), op0=ALU.mult, op1=ALU.add),
         reads=(cntk,), writes=(tck,))
    gt8, gt8k = ctx.af.alloc(S)
    tie8, tie8k = ctx.af.alloc(S)
    sel8 = ctx.s["sel8"].ap()
    sel8k = "sel8t"
    p.op("dve", lambda e: e.tensor_scalar(out=gt8[0:8, :], in0=r8[0:8, :],
         scalar1=hi[0:8, :], scalar2=None, op0=ALU.is_gt), reads=(r8k, hik), writes=(gt8k,))
    p.op("dve", lambda e: e.tensor_scalar(out=tie8[0:8, :], in0=r8[0:8, :],
         scalar1=lo[0:8, :], scalar2=None, op0=ALU.is_gt), reads=(r8k, lok), writes=(tie8k,))
    p.op("dve", lambda e: e.tensor_sub(tie8[0:8, :], tie8[0:8, :], gt8[0:8, :]),
         reads=(tie8k, gt8k), writes=(tie8k,))
    p.op("dve", lambda e: e.tensor_tensor_scan(out=sel8[0:8, :], data0=tie8[0:8, :],
         data1=tie8[0:8, :], initial=0.0, op0=ALU.add, op1=ALU.bypass),
         reads=(tie8k,), writes=(sel8k,))
    p.op("dve", lambda e: e.tensor_scalar(out=sel8[0:8, :], in0=sel8[0:8, :],
         scalar1=tcol[0:8, :], scalar2=None, op0=ALU.is_le), reads=(sel8k, tck), writes=(sel8k,))
    p.op("dve", lambda e: e.tensor_mul(sel8[0:8, :], sel8[0:8, :], tie8[0:8, :]),
         reads=(sel8k, tie8k), writes=(sel8k,))
    p.op("dve", lambda e: e.tensor_add(sel8[0:8, :], sel8[0:8, :], gt8[0:8, :]),
         reads=(sel8k, gt8k), writes=(sel8k,))
    selb, selbk = ctx.ab.alloc(S)
    p.op("dve", lambda e: e.tensor_copy(selb[0:8, :], sel8[0:8, :]),
         reads=(sel8k,), writes=(selbk,))
    p.barrier()
    ctx.af.reset()
    b1c, b1ck = ctx.af.alloc(2 * E)
    for e_i in range(E):
        for hh in range(2):
            p.op("sp", lambda e, e_i=e_i, hh=hh: e.dma_start(
                 out=b1c[:, e_i * 2 + hh:e_i * 2 + hh + 1], in_=b1_dram[e_i, hh * 128:(hh + 1) * 128]),
                 writes=(b1ck,), dma=True)
    b2s, b2sk = ctx.ab.alloc(D)
    p.op("sp", lambda e: e.dma_start(out=b2s[0:8, :], in_=b2_bf), writes=(b2sk,), dma=True)
    zc = [ctx.ab.alloc(TSUB) for _ in range(NCHUNK)]
    w1st = [ctx.ab.alloc(SUBH) for _ in range(NCHUNK)]
    w2st = [ctx.ab.alloc(D) for _ in range(4)]
    hgall, hgk = ctx.ab.alloc(E * 2 * TSUB)
    selbc = [ctx.af.alloc(TSUB) for _ in range(2)]
    gtmp = [ctx.af.alloc(TSUB) for _ in range(2)]
    xst = [ctx.af.alloc(TSUB) for _ in range(2)]
    ost = [ctx.af.alloc(TSUB) for _ in range(2)]
    for sub in range(NSUB):
        cs = slice(sub * TSUB, (sub + 1) * TSUB)
        for c in range(NCHUNK):
            p.op("sp", lambda e, a=zc[c][0], d=z_dram[c * 128:(c + 1) * 128, cs]:
                 e.dma_start(out=a, in_=d), reads=(z_key,), writes=(zc[c][1],), dma=True)
        for e_i in range(E):
            for c in range(NCHUNK):
                p.op("sp", lambda e, a=w1st[c][0], d=w1_bf[e_i][c * 128:(c + 1) * 128, :]:
                     e.dma_start(out=a, in_=d), reads=("w1bf",), writes=(w1st[c][1],), dma=True)
            sbc, sbck = selbc[e_i % 2]
            p.op("pe", lambda e, cs=cs, e_i=e_i: e.matmul(ps["ps6"].ap()[:, :],
                 selb8[0:8, e_i * 128:(e_i + 1) * 128], sel8[0:8, cs], start=True, stop=True),
                 reads=(sel8k, selb8_key), writes=("ps6",))
            p.op("dve", lambda e, sbc=sbc: e.tensor_copy(sbc, ps["ps6"].ap()[:, :]),
                 reads=("ps6",), writes=(sbck,))
            for hh in range(2):
                pb = f"ps{hh % 2}"
                for c in range(NCHUNK):
                    p.op("pe", lambda e, c=c, hh=hh, pb=pb: e.matmul(ps[pb].ap()[:, :],
                         w1st[c][0][:, hh * 128:(hh + 1) * 128], zc[c][0],
                         start=(c == 0), stop=(c == NCHUNK - 1)),
                         reads=(w1st[c][1], zc[c][1]), writes=(pb,))
                gg, ggk = gtmp[hh % 2]
                p.op("act", lambda e, gg=gg, pb=pb, e_i=e_i, hh=hh: e.activation(
                     out=gg, in_=ps[pb].ap()[:, :], func=AF.Gelu,
                     bias=b1c[:, e_i * 2 + hh:e_i * 2 + hh + 1]),
                     reads=(pb, b1ck), writes=(ggk,))
                p.op("dve", lambda e, gg=gg, sbc=sbc, e_i=e_i, hh=hh: e.tensor_mul(
                     hgall[:, (e_i * 2 + hh) * TSUB:(e_i * 2 + hh + 1) * TSUB], gg, sbc),
                     reads=(ggk, sbck), writes=(f"{hgk}:{e_i}:{hh}",))
        first = True
        for e_i in range(E):
            for hh in range(2):
                ws, wsk = w2st[(e_i * 2 + hh) % 4]
                p.op("sp", lambda e, ws=ws, d=w2_bf[e_i][hh * 128:(hh + 1) * 128, :]:
                     e.dma_start(out=ws, in_=d), reads=("w2bf",), writes=(wsk,), dma=True)
                for ho in range(NCHUNK):
                    p.op("pe", lambda e, ws=ws, ho=ho, st=first, e_i=e_i, hh=hh: e.matmul(
                         ps[f"ps{ho}"].ap()[:, :], ws[:, ho * 128:(ho + 1) * 128],
                         hgall[:, (e_i * 2 + hh) * TSUB:(e_i * 2 + hh + 1) * TSUB],
                         start=st, stop=False),
                         reads=(wsk, f"{hgk}:{e_i}:{hh}"), writes=(f"ps{ho}",))
                first = False
        for ho in range(NCHUNK):
            p.op("pe", lambda e, ho=ho, cs=cs: e.matmul(ps[f"ps{ho}"].ap()[:, :],
                 b2s[0:8, ho * 128:(ho + 1) * 128], selb[0:8, cs], start=False, stop=True),
                 reads=(b2sk, selbk), writes=(f"ps{ho}",))
        for ho in range(NCHUNK):
            xa, xk2 = xst[ho % 2]
            p.op("sp", lambda e, a=xa, d=x_dram[ho * 128:(ho + 1) * 128, cs]:
                 e.dma_start(out=a, in_=d), reads=(x_key,), writes=(xk2,), dma=True)
            oa, ok2 = ost[ho % 2]
            p.op("dve", lambda e, oa=oa, ho=ho, xa=xa: e.scalar_tensor_tensor(out=oa,
                 in0=ps[f"ps{ho}"].ap()[:, :], scalar=0.5, in1=xa, op0=ALU.mult, op1=ALU.add),
                 reads=(f"ps{ho}", xk2), writes=(ok2,))
            p.op("gp", lambda e, oa=oa, d=out_dram[ho * 128:(ho + 1) * 128, cs]:
                 e.dma_start(out=d, in_=oa), reads=(ok2,), writes=(f"{out_key}:{ho}:{sub}",), dma="sink")
    p.barrier()
    ctx.ab.reset()
    ctx.af.reset()


import contextlib

AB_ELEMS = 53248
AF_ELEMS = 20480


def make_ctx(nc, stack):
    ab_t = stack.enter_context(nc.sbuf_tensor("arena_bf", [128, AB_ELEMS], BF16))
    af_t = stack.enter_context(nc.sbuf_tensor("arena_f32", [128, AF_ELEMS], F32))
    u8t = stack.enter_context(nc.sbuf_tensor("u8t", [8, TSUB], U8))
    sel8_t = stack.enter_context(nc.sbuf_tensor("sel8t", [8, S], F32))
    u8c = stack.enter_context(nc.sbuf_tensor("u8c", [32, 1], U8))
    small = {"u8t": u8t, "u8c": u8c, "sel8": sel8_t}

    class _V:
        def __init__(self, ap):
            self._ap = ap

        def ap(self):
            return self._ap

    pp = [stack.enter_context(nc.psum_tensor(f"pp{i}", [128, 1024], F32)) for i in range(3)]
    pd = [stack.enter_context(nc.psum_tensor(f"pd{i}", [128, 512], F32)) for i in range(2)]
    small["pp0"] = _V(pp[0].ap())
    small["pp1"] = _V(pp[1].ap())
    small["pp2"] = _V(pp[2].ap())
    small["ps0"] = _V(pp[0].ap()[:, 0:512])
    small["ps1"] = _V(pp[0].ap()[:, 512:1024])
    small["ps2"] = _V(pp[1].ap()[:, 0:512])
    small["ps3"] = _V(pp[1].ap()[:, 512:1024])
    small["ps6"] = _V(pp[2].ap()[:, 0:512])
    small["ps7"] = _V(pp[2].ap()[:, 512:1024])
    small["ps4"] = _V(pd[0].ap())
    small["ps5"] = _V(pd[1].ap())
    p = Prog(nc)
    return Ctx(nc, p, Arena(ab_t, AB_ELEMS, "ab"), Arena(af_t, AF_ELEMS, "af"), small, {})


def _load_selb8(ctx, dram):
    selb, _ = ctx.s["selb"], None
    ctx.p.op("sp", lambda e: e.dma_start(out=selb.ap()[0:8, :], in_=dram),
             writes=("selb8",), dma=True)
    return selb.ap(), "selb8"


def emit_l0ffn(ctx, x_in, x_in_key, x_out, x_out_key, selb8, selb8k):
    d = ctx.d
    emit_norm(ctx, x_in_key, x_in, d["y"], "y", "ln")
    emit_norm(ctx, "y", d["y"], d["zb"], "zb", "rms", w_dram=d["ln1_w"])
    emit_qkv(ctx, d["zb"], "zb", [
        (d["fq_b"], "fq_b", d["q"], "q", True, 0.3535533905932738),
        (d["fk_b"], "fk_b", d["k"], "k", True, 0.3535533905932738),
        (d["fv_b"], "fv_b", d["v"], "v", True, 1.0)])
    emit_attn_core(ctx, d["q"], d["k"], d["v"], d["cx"], "cx", causal=False)
    emit_proj(ctx, d["cx"], "cx", d["fo_b"], "fo_b", d["h"], "h", True,
              out_dtype=F32, residual=(d["y"], "y"), inv_name="invd")
    emit_norm(ctx, "h", d["h"], d["zb"], "zb2", "rms", w_dram=d["ln2_w"])
    emit_moe0(ctx, d["zb"], "zb2", d["w1_b"], d["w2_b"], d["w3_b"], d["gate_w"],
              d["we"], d["moe"], "moe", selb8, selb8k)
    emit_combine(ctx, x_in, x_in_key, d["h"], "h", d["moe"], "moe", x_out, x_out_key)


def emit_l0attn(ctx, x_in, x_in_key, x_out, x_out_key):
    d = ctx.d
    emit_norm(ctx, x_in_key, x_in, d["zb"], "zb", "ln")
    emit_qkv(ctx, d["zb"], "zb", [
        (d["aq_b"], "aq_b", d["q"], "q", False, 0.3535533905932738),
        (d["ak_b"], "ak_b", d["k"], "k", False, 0.3535533905932738),
        (d["av_b"], "av_b", d["v"], "v", False, 1.0)])
    emit_l0attn_win(ctx, d["q"], d["k"], d["v"], d["cx"], d["cx2"], "ce", "co")
    emit_wo_win(ctx, d["cx"], "ce", d["cx2"], "co", d["ao_b"], x_in, x_in_key,
                d["c_count"], x_out, x_out_key)


def build_launch1(wdt, dbg=False):
    nc = bass.Bass("TRN2", target_bir_lowering=False, debug=False)
    d = {}
    def di(name, shape, dt=F32, kind="ExternalInput"):
        d[name] = nc.dram_tensor(name, list(shape), dt, kind=kind).ap()
    di("x0", (D, S))
    di("ln1_w", (D,))
    di("ln2_w", (D,))
    for n in ("fq", "fk", "fv", "fo"):
        di(n, (D, D))
    di("gate_w", (E, D))
    di("w1", (E, IMOE, D))
    di("w2", (E, D, IMOE))
    di("w3", (E, IMOE, D))
    for n in ("aq", "ak", "av", "ao"):
        di(n, (D, D))
    di("c_selb", (E, E * 128))
    di("c_count", (1, S))
    di("c_hsel", (128, 128), BF16)
    di("c_selb16", (16, 16 * 128))
    di("c_selp", (2, 128))
    di("den", (16, S), F32, "Internal")
    di("invd", (16, S), F32, "Internal")
    di("den_e", (16, S), F32, "Internal")
    di("den_o", (16, S), F32, "Internal")
    di("inv_e", (16, S), F32, "Internal")
    di("inv_o", (16, S), F32, "Internal")
    di("gw", (D, E))
    di("gwb", (E,))
    for n in ("fq", "fk", "fv", "fo", "aq", "ak", "av", "ao"):
        di(n + "_b", (D, D), BF16, "Internal")

    di("w1_b", (E, IMOE, D), BF16, "Internal")
    di("w2_b", (E, D, IMOE), BF16, "Internal")
    di("w3_b", (E, IMOE, D), BF16, "Internal")
    for n in ("y", "h", "moe"):
        di(n, (D, S), F32, "Internal")
    for n in ("xa", "xb"):
        di(n, (D, S), F32, "Internal")
    for n in ("zb", "q", "k", "v", "cx", "cx2"):
        di(n, (D, S), BF16, "Internal")
    di("we", (E, S), F32, "Internal")
    di("x_out", (D, S), F32, "ExternalOutput")
    di("masked", (E, S), F32, "ExternalOutput")
    with contextlib.ExitStack() as stack:
        selb_t = stack.enter_context(nc.sbuf_tensor("selb", [8, E * 128], F32))
        ctx = make_ctx(nc, stack)
        ctx.s["selb"] = selb_t
        ctx.d = d
        prep = [(d[n], d[n + "_b"], n + "_b") for n in
                ("fq", "fk", "fv", "fo", "aq", "ak", "av", "ao")]

        for e_i in range(E):
            prep.append((d["w1"][e_i], d["w1_b"][e_i], f"w1b{e_i}"))
            prep.append((d["w2"][e_i], d["w2_b"][e_i], f"w2b{e_i}"))
            prep.append((d["w3"][e_i], d["w3_b"][e_i], f"w3b{e_i}"))
        emit_weight_prep(ctx, prep)
        selb8, selb8k = _load_selb8(ctx, d["c_selb"])
        d["w1bf_idx"] = None
        emit_l0ffn(ctx, d["x0"], "x0", d["xa"], "xa", selb8, selb8k)
        if dbg:
            for n in ("y", "h", "moe", "we", "xa"):
                dd = nc.dram_tensor("dbg_" + n, list(d[n].shape), F32, kind="ExternalOutput").ap()
                ctx.p.op("gp", lambda e, s=d[n], t=dd: e.dma_start(out=t, in_=s),
                         writes=(f"dbg{n}",), dma="sink")
            ctx.p.barrier()
        emit_l0attn(ctx, d["xa"], "xa", d["xb"], "xb")
        if dbg:
            for n in ("xb", "q", "k", "v"):
                sh2 = list(d[n].shape)
                dd = nc.dram_tensor("dbg2_" + n, sh2, d[n].dtype, kind="ExternalOutput").ap()
                ctx.p.op("gp", lambda e, s=d[n], t=dd: e.dma_start(out=t, in_=s),
                         writes=(f"dbg2{n}",), dma="sink")
            ctx.p.barrier()
        emit_l0ffn(ctx, d["xb"], "xb", d["x_out"], "x_out", selb8, selb8k)
        emit_norm(ctx, "x_out", d["x_out"], d["zb"], "zb4", "ln")
        emit_gate1(ctx, d["zb"], "zb4", d["gw"], d["gwb"], d["masked"], "masked")
        ctx.p.finalize()
    return nc


def build_launch2():
    nc = bass.Bass("TRN2", target_bir_lowering=False, debug=False)
    d = {}
    def di(name, shape, dt=F32, kind="ExternalInput"):
        d[name] = nc.dram_tensor(name, list(shape), dt, kind=kind).ap()
    di("x0", (D, S))
    di("mall", (8, E, S))
    di("mown", (E, S))
    for n in ("aq", "ak", "av", "ao"):
        di(n, (D, D))
    di("gw", (D, E))
    di("gwb", (E,))
    di("w1", (E, D, SUBH))
    di("b1", (E, SUBH))
    di("w2", (E, SUBH, D))
    di("b2", (E, D))
    di("c_selb", (E, E * 128))
    di("c_masks", (128, 4 * TSUB), BF16)
    di("c_hsel", (128, 128), BF16)
    di("c_selb16", (16, 16 * 128))
    di("c_selp", (2, 128))
    di("den", (16, S), F32, "Internal")
    di("invd", (16, S), F32, "Internal")
    di("den_e", (16, S), F32, "Internal")
    di("den_o", (16, S), F32, "Internal")
    di("inv_e", (16, S), F32, "Internal")
    di("inv_o", (16, S), F32, "Internal")
    for n in ("aq", "ak", "av", "ao"):
        di(n + "_b", (D, D), BF16, "Internal")
    di("w1_b", (E, D, SUBH), BF16, "Internal")
    di("w2_b", (E, SUBH, D), BF16, "Internal")
    di("b2_b", (E, D), BF16, "Internal")
    for n in ("xa", "xb"):
        di(n, (D, S), F32, "Internal")
    for n in ("zb", "q", "k", "v", "cx"):
        di(n, (D, S), BF16, "Internal")
    di("r8_scratch", (E, S), F32, "Internal")
    di("x_out", (D, S), F32, "ExternalOutput")
    di("masked", (E, S), F32, "ExternalOutput")
    with contextlib.ExitStack() as stack:
        selb_t = stack.enter_context(nc.sbuf_tensor("selb", [8, E * 128], F32))
        ctx = make_ctx(nc, stack)
        ctx.s["selb"] = selb_t
        ctx.d = d
        prep = [(d[n], d[n + "_b"], n + "_b") for n in ("aq", "ak", "av", "ao")]
        prep.append((d["b2"], d["b2_b"], "b2_b"))
        for e_i in range(E):
            prep.append((d["w1"][e_i], d["w1_b"][e_i], f"w1b{e_i}"))
            prep.append((d["w2"][e_i], d["w2_b"][e_i], f"w2b{e_i}"))
        emit_weight_prep(ctx, prep)
        selb8, selb8k = _load_selb8(ctx, d["c_selb"])
        emit_norm(ctx, "x0", d["x0"], d["zb"], "zb4", "ln")
        emit_l1ffn(ctx, d["zb"], "zb4", d["mall"], d["mown"], d["w1_b"], d["b1"],
                   d["w2_b"], d["b2_b"], d["x0"], "x0", d["xa"], "xa", selb8, selb8k)
        emit_norm(ctx, "xa", d["xa"], d["zb"], "zb5", "ln")
        emit_qkv(ctx, d["zb"], "zb5", [
            (d["aq_b"], "aq_b", d["q"], "q", True, 0.3535533905932738),
            (d["ak_b"], "ak_b", d["k"], "k", True, 0.3535533905932738),
            (d["av_b"], "av_b", d["v"], "v", True, 1.0)])
        emit_attn_core(ctx, d["q"], d["k"], d["v"], d["cx"], "cx", causal=True,
                       masks_dram=d["c_masks"])
        emit_proj(ctx, d["cx"], "cx", d["ao_b"], "ao_b", d["x_out"], "x_out", True,
                  out_dtype=F32, residual=(d["xa"], "xa"), inv_name="invd")
        emit_norm(ctx, "x_out", d["x_out"], d["zb"], "zb6", "ln")
        emit_gate1(ctx, d["zb"], "zb6", d["gw"], d["gwb"], d["masked"], "masked")
        ctx.p.finalize()
    return nc


def build_launch3():
    nc = bass.Bass("TRN2", target_bir_lowering=False, debug=False)
    d = {}
    def di(name, shape, dt=F32, kind="ExternalInput"):
        d[name] = nc.dram_tensor(name, list(shape), dt, kind=kind).ap()
    di("x0", (D, S))
    di("mall", (8, E, S))
    di("mown", (E, S))
    di("w1", (E, D, SUBH))
    di("b1", (E, SUBH))
    di("w2", (E, SUBH, D))
    di("b2", (E, D))
    di("c_selb", (E, E * 128))
    di("w1_b", (E, D, SUBH), BF16, "Internal")
    di("w2_b", (E, SUBH, D), BF16, "Internal")
    di("b2_b", (E, D), BF16, "Internal")
    di("zb", (D, S), BF16, "Internal")
    di("r8_scratch", (E, S), F32, "Internal")
    di("x_out", (D, S), F32, "ExternalOutput")
    with contextlib.ExitStack() as stack:
        selb_t = stack.enter_context(nc.sbuf_tensor("selb", [8, E * 128], F32))
        ctx = make_ctx(nc, stack)
        ctx.s["selb"] = selb_t
        ctx.d = d
        prep = [(d["b2"], d["b2_b"], "b2_b")]
        for e_i in range(E):
            prep.append((d["w1"][e_i], d["w1_b"][e_i], f"w1b{e_i}"))
            prep.append((d["w2"][e_i], d["w2_b"][e_i], f"w2b{e_i}"))
        emit_weight_prep(ctx, prep)
        selb8, selb8k = _load_selb8(ctx, d["c_selb"])
        emit_norm(ctx, "x0", d["x0"], d["zb"], "zb6", "ln")
        emit_l1ffn(ctx, d["zb"], "zb6", d["mall"], d["mown"], d["w1_b"], d["b1"],
                   d["w2_b"], d["b2_b"], d["x0"], "x0", d["x_out"], "x_out", selb8, selb8k)
        ctx.p.finalize()
    return nc


def _consts():
    selb = np.zeros((E, E * 128), np.float32)
    for e_i in range(E):
        selb[e_i, e_i * 128:(e_i + 1) * 128] = 1.0
    cnt = np.full((1, S), 2.0, np.float32)
    cnt[0, 0:128] = 1.0
    cnt[0, S - 128:S] = 1.0
    hsel = np.zeros((128, 128), np.float32)
    for c in range(8):
        for dd in range(128):
            hsel[dd, c * 16 + 2 * c + dd // 64] = 1.0
    selb16 = np.zeros((16, 16 * 128), np.float32)
    for h in range(16):
        selb16[h, h * 128:(h + 1) * 128] = 1.0
    selp = np.zeros((2, 128), np.float32)
    selp[0, 0:64] = 1.0
    selp[1, 64:128] = 1.0
    masks = np.zeros((128, 4 * TSUB), np.float32)
    for dd in range(4):
        i = np.arange(128)[:, None]
        j = np.arange(TSUB)[None, :]
        masks[:, dd * TSUB:(dd + 1) * TSUB] = (j >= i + dd * 128).astype(np.float32)
    return selb, cnt, masks.astype(ml_dtypes.bfloat16), hsel.astype(ml_dtypes.bfloat16), selb16, selp


_NC_CACHE = {}


def kernel(**inputs):
    inp = {k: np.asarray(v) for k, v in inputs.items()}
    x_full = inp["L0_attn_hidden_states"]
    selb, cnt, masks, hsel, selb16, selp = _consts()
    if "l1" not in _NC_CACHE:
        _NC_CACHE["l1"] = build_launch1(None)
        _NC_CACHE["l2"] = build_launch2()
        _NC_CACHE["l3"] = build_launch3()
    nc1, nc2, nc3 = _NC_CACHE["l1"], _NC_CACHE["l2"], _NC_CACHE["l3"]
    common1 = {
        "ln1_w": inp["L0_ffn_ln1_w"], "ln2_w": inp["L0_ffn_ln2_w"],
        "fq": inp["L0_ffn_wq"], "fk": inp["L0_ffn_wk"], "fv": inp["L0_ffn_wv"],
        "fo": inp["L0_ffn_wo"], "gate_w": inp["L0_ffn_gate_w"],
        "w1": inp["L0_ffn_w1"], "w2": inp["L0_ffn_w2"], "w3": inp["L0_ffn_w3"],
        "aq": inp["L0_attn_Wq"], "ak": inp["L0_attn_Wk"], "av": inp["L0_attn_Wv"],
        "ao": inp["L0_attn_Wo"],
        "gw": inp["L1_ffn_w_gate_w"], "gwb": inp["L1_ffn_w_gate_b"],
        "c_selb": selb, "c_count": cnt, "c_hsel": hsel, "c_selb16": selb16, "c_selp": selp,
    }
    in1 = []
    for b in range(8):
        m = dict(common1)
        m["x0"] = np.ascontiguousarray(x_full[b].T)
        in1.append(m)
    r1 = run_bass_kernel_spmd(nc1, in1, core_ids=list(range(8)))
    x3 = [r1.results[b]["x_out"] for b in range(8)]
    masked1 = np.stack([r1.results[b]["masked"] for b in range(8)])
    common2 = {
        "aq": inp["L1_attn_WQ"], "ak": inp["L1_attn_WK"], "av": inp["L1_attn_WV"],
        "ao": inp["L1_attn_WO"],
        "gw": inp["L1_ffn_w_gate_w"], "gwb": inp["L1_ffn_w_gate_b"],
        "w1": inp["L1_ffn_W1"], "b1": inp["L1_ffn_b1"],
        "w2": inp["L1_ffn_W2"], "b2": inp["L1_ffn_b2"],
        "c_selb": selb, "c_masks": masks, "mall": masked1, "c_hsel": hsel, "c_selb16": selb16, "c_selp": selp,
    }
    in2 = []
    for b in range(8):
        m = dict(common2)
        m["x0"] = x3[b]
        m["mown"] = masked1[b]
        in2.append(m)
    r2 = run_bass_kernel_spmd(nc2, in2, core_ids=list(range(8)))
    x5 = [r2.results[b]["x_out"] for b in range(8)]
    masked2 = np.stack([r2.results[b]["masked"] for b in range(8)])
    common3 = {
        "w1": inp["L1_ffn_W1"], "b1": inp["L1_ffn_b1"],
        "w2": inp["L1_ffn_W2"], "b2": inp["L1_ffn_b2"],
        "c_selb": selb, "mall": masked2,
    }
    in3 = []
    for b in range(8):
        m = dict(common3)
        m["x0"] = x5[b]
        m["mown"] = masked2[b]
        in3.append(m)
    r3 = run_bass_kernel_spmd(nc3, in3, core_ids=list(range(8)))
    out = np.stack([np.ascontiguousarray(r3.results[b]["x_out"].T) for b in range(8)])
    return out.astype(np.float32)



# version 06
# speedup vs baseline: 1.1583x; 1.1583x over previous
import numpy as np
import ml_dtypes
import concourse.bass as bass
import concourse.mybir as mybir
from concourse.bass_utils import run_bass_kernel_spmd

F32 = mybir.dt.float32
BF16 = mybir.dt.bfloat16
U8 = mybir.dt.uint8
AF = mybir.ActivationFunctionType
ALU = mybir.AluOpType
AX = mybir.AxisListType

D = 1024
S = 4096
H = 16
DH = 64
NCHUNK = 8
NSUB = 8
TSUB = 512
E = 8
IMOE = 2048
SUBH = 256
LN_EPS = 1e-5
RMS_EPS = 1e-5
L1F_EPS = 1e-6
WIN = 256
STRIDE = 128
NWIN = 31
KCAP = 2048

ENGINES = ("pe", "act", "dve", "sp", "gp")


class Prog:

    POOL = 100

    def __init__(self, nc):
        self.nc = nc
        self.ops = {e: [] for e in ENGINES}
        self.cnt = {e: 0 for e in ENGINES}
        self.pool_cnt = {}
        self.key_sem = {}
        self.pool_next = {"sp": 0, "gp": 0, "act": 0}
        self.pool_used = set()
        self.last_write = {}
        self.readers = {}
        self.water = {}
        self.nbar = 0

    def _dma_sem(self, eng, key):
        k = (eng, key)
        if k not in self.key_sem:
            label = f"{eng}{self.pool_next[eng] % self.POOL}"
            self.pool_next[eng] += 1
            assert self.pool_next[eng] <= self.POOL, f"dma sem pool exhausted on {eng}"
            self.key_sem[k] = label
            self.pool_used.add(label)
        return self.key_sem[k]

    _PSUM_ALIAS = {"pp0": ("ps0", "ps1"), "pp1": ("ps2", "ps3"), "pp2": ("ps6", "ps7")}

    def _expand(self, keys):
        out = []
        for k in keys:
            if k in self._PSUM_ALIAS:
                out.extend(self._PSUM_ALIAS[k])
            else:
                out.append(k)
        return tuple(out)

    def op(self, eng, emit, reads=(), writes=(), dma=False):
        reads = self._expand(reads)
        writes = self._expand(writes)
        deps = []
        for r in reads:
            w = self.last_write.get(r)
            if w is not None:
                deps.append(w)
        for r in writes:
            w = self.last_write.get(r)
            if w is not None:
                deps.append(w)
            deps.extend(self.readers.get(r, ()))
        need = {}
        for (lbl, tc) in deps:
            if lbl == eng == "pe":
                continue
            if need.get(lbl, -1) < tc:
                need[lbl] = tc
        waits = []
        for lbl, tc in sorted(need.items()):
            k = (eng, lbl)
            if self.water.get(k, -1) >= tc:
                continue
            self.water[k] = tc
            waits.append((lbl, tc))
        if dma:
            assert eng in ("sp", "gp", "act")
            assert writes, "dma op needs a destination key"
            if dma == "sink":
                if reads:
                    lbl = self._dma_sem(eng, ("src", reads[0]))
                else:
                    lbl = f"{eng}_sink"
                    self.pool_used.add(lbl)
            else:
                lbl = self._dma_sem(eng, writes[0])
            self.pool_cnt[lbl] = self.pool_cnt.get(lbl, 0) + 16
            after = (lbl, self.pool_cnt[lbl])
            self.ops[eng].append((waits, emit, 16, lbl))
        else:
            self.cnt[eng] += 1
            after = (eng, self.cnt[eng])
            self.ops[eng].append((waits, emit, 1, eng))
        for r in reads:
            self.readers.setdefault(r, []).append(after)
        for r in writes:
            self.last_write[r] = after
            self.readers[r] = []
        return after

    def barrier(self):
        self.nbar += 1
        n = self.nbar
        gpw = []
        for e in ENGINES:
            if self.cnt[e] > 0:
                gpw.append((e, self.cnt[e]))
        for lbl, c in sorted(self.pool_cnt.items()):
            if c > 0:
                k = ("gp", lbl)
                if self.water.get(k, -1) < c:
                    self.water[k] = c
                    gpw.append((lbl, c))
        self.ops["gp"].append((gpw, "BARRIER_INC", 0, None))
        for e in ENGINES:
            self.ops[e].append(([("bar", n)], None, 0, None))
        self.last_write = {}
        self.readers = {}
        self.water = {}
        self.key_sem = {}
        self.pool_next = {"sp": 0, "gp": 0, "act": 0}

    def finalize(self):
        self.barrier()
        nc = self.nc
        labels = set(ENGINES) | self.pool_used | {"bar"}
        sems = {l: nc.semaphore(f"s_{l}").__enter__() for l in sorted(labels)}
        with nc.Block() as block:
            sections = {
                "pe": block.tensor, "act": block.scalar, "dve": block.vector,
                "sp": block.sync, "gp": block.gpsimd,
            }
            for eng in ENGINES:
                def make(eng):
                    def body(handle):
                        barn = [0]
                        for waits, emit, inc, lbl in self.ops[eng]:
                            for tl, tc in waits:
                                handle.wait_ge(sems[tl], tc)
                            if emit == "BARRIER_INC":
                                barn[0] += 1
                                handle.sem_inc(sems["bar"], 1)
                            elif emit is not None:
                                inst = emit(handle)
                                inst.then_inc(sems[lbl], inc)
                    return body
                sections[eng](make(eng))


class Arena:

    def __init__(self, tensor, total, name):
        self.t = tensor
        self.total = total
        self.off = 0
        self.name = name
        self.uid = 0

    def alloc(self, n):
        n32 = (n + 31) & ~31
        assert self.off + n32 <= self.total, f"{self.name} arena overflow: {self.off}+{n32}>{self.total}"
        ap = self.t.ap()[:, self.off:self.off + n]
        key = f"{self.name}:{self.off}:{self.uid}"
        self.uid += 1
        self.off += n32
        return ap, key

    def reset(self):
        self.off = 0


class Ctx:

    def __init__(self, nc, p, ab, af, small, dram):
        self.nc = nc
        self.p = p
        self.ab = ab
        self.af = af
        self.s = small
        self.d = dram



def emit_weight_prep(ctx, pairs):
    p = ctx.p
    for src, dst, key in pairs:
        n = 1
        rows = src.shape[0]
        elems = int(np.prod(src.shape))
        while elems // n * 4 > 8 << 20 or rows % n:
            n += 1
        step = rows // n
        for i in range(n):
            sl = slice(i * step, (i + 1) * step)
            p.op("gp", lambda e, s=src[sl], d=dst[sl]: e.dma_start(out=d, in_=s),
                 writes=(f"{key}:{i}",), dma="sink")
    p.barrier()


def load_chunks(ctx, dram, key, dtype=None, eng="sp", cols=S):
    p = ctx.p
    arena = ctx.ab if (dtype or dram.dtype) == BF16 else ctx.af
    out = []
    for c in range(NCHUNK):
        ap, k = arena.alloc(cols)
        p.op(eng, lambda e, d=dram[c * 128:(c + 1) * 128, :], a=ap: e.dma_start(out=a, in_=d),
             reads=(f"{key}",), writes=(k,), dma=True)
        out.append((ap, k))
    return out


def const_col(ctx, value, parts=1, name=None):
    ap, k = ctx.af.alloc(1)
    ap = ap[0:parts, :]
    ctx.p.op("dve", lambda e: e.memset(ap, float(value)), writes=(k,))
    return ap, k


def emit_norm(ctx, src_key, src_dram, dst_dram, dst_key, kind, w_dram=None):
    nc, p = ctx.nc, ctx.p
    ps = ctx.s
    halves = ["ps0", "ps1", "ps2", "ps3", "ps4", "ps5", "ps6", "ps7"]
    ones1b, _ = ctx.ab.alloc(1)
    p.op("dve", lambda e: e.memset(ones1b, 1.0), writes=("ones1b",))
    onesrow_t, _ = ctx.af.alloc(128)
    onesrow = onesrow_t[0:1, :]
    p.op("dve", lambda e: e.memset(onesrow, 1.0), writes=("onesrow",))
    wk_all = ()
    if w_dram is not None:
        wcol, wk = ctx.af.alloc(NCHUNK)
        for c in range(NCHUNK):
            p.op("sp", lambda e, c=c, wcol=wcol: e.dma_start(out=wcol[:, c:c + 1], in_=w_dram[c * 128:(c + 1) * 128]),
                 writes=(wk,), dma=True)
        wk_all = (wk,)
    eps = LN_EPS if kind == "ln" else RMS_EPS
    epsc, epsk = const_col(ctx, eps)
    xl = [ctx.af.alloc(S) for _ in range(2)]
    xb = [ctx.ab.alloc(S) for _ in range(NCHUNK)]
    sqt2 = [ctx.ab.alloc(S) for _ in range(2)]
    (srow_t, srk), (qrow_t, qrk) = ctx.af.alloc(S), ctx.af.alloc(S)
    srow = srow_t[0:1, :]
    qrow = qrow_t[0:1, :]
    for c in range(NCHUNK):
        xa, xak = xl[c % 2]
        p.op("sp", lambda e, a=xa, d=src_dram[c * 128:(c + 1) * 128, :]: e.dma_start(out=a, in_=d),
             reads=(src_key,), writes=(xak,), dma=True)
        p.op("dve", lambda e, s=xa, d=xb[c][0]: e.tensor_copy(d, s),
             reads=(xak,), writes=(xb[c][1],))
        if kind == "ln":
            for sub in range(NSUB):
                p.op("pe", lambda e, s=xb[c][0], sub=sub, c=c, hb=halves[sub]: e.matmul(
                     ps[hb].ap()[0:1, :], ones1b, s[:, sub * TSUB:(sub + 1) * TSUB],
                     start=(c == 0), stop=(c == NCHUNK - 1)),
                     reads=(xb[c][1], "ones1b"), writes=(halves[sub],))
    if kind == "ln":
        for sub in range(NSUB):
            p.op("dve", lambda e, sub=sub, hb=halves[sub]: e.tensor_copy(
                 srow[0:1, sub * TSUB:(sub + 1) * TSUB], ps[hb].ap()[0:1, :]),
                 reads=(halves[sub],), writes=(srk,))
    for c in range(NCHUNK):
        sqt, sqk = sqt2[c % 2]
        p.op("dve", lambda e, s=xb[c][0], d=sqt: e.tensor_mul(d, s, s),
             reads=(xb[c][1],), writes=(sqk,))
        for sub in range(NSUB):
            p.op("pe", lambda e, s=sqt, sub=sub, c=c, hb=halves[sub]: e.matmul(
                 ps[hb].ap()[0:1, :], ones1b, s[:, sub * TSUB:(sub + 1) * TSUB],
                 start=(c == 0), stop=(c == NCHUNK - 1)),
                 reads=(sqk, "ones1b"), writes=(halves[sub],))
    for sub in range(NSUB):
        p.op("dve", lambda e, sub=sub, hb=halves[sub]: e.tensor_copy(
             qrow[0:1, sub * TSUB:(sub + 1) * TSUB], ps[hb].ap()[0:1, :]),
             reads=(halves[sub],), writes=(qrk,))
    if kind == "ln":
        p.op("dve", lambda e: e.tensor_scalar(out=srow, in0=srow, scalar1=1.0 / D,
             scalar2=None, op0=ALU.mult), reads=(srk,), writes=(srk,))
        p.op("dve", lambda e: e.tensor_scalar(out=qrow, in0=qrow, scalar1=1.0 / D,
             scalar2=None, op0=ALU.mult), reads=(qrk,), writes=(qrk,))
        p.op("dve", lambda e: e.tensor_mul(sqt2[0][0][0:1, :], srow, srow), reads=(srk,), writes=(sqt2[0][1],))
        p.op("dve", lambda e: e.tensor_sub(qrow, qrow, sqt2[0][0][0:1, :]), reads=(qrk, sqt2[0][1]), writes=(qrk,))
    else:
        p.op("dve", lambda e: e.tensor_scalar(out=qrow, in0=qrow, scalar1=1.0 / D,
             scalar2=None, op0=ALU.mult), reads=(qrk,), writes=(qrk,))
    p.op("act", lambda e: e.activation(out=qrow, in_=qrow, func=AF.Ln, bias=epsc),
         reads=(qrk, epsk), writes=(qrk,))
    p.op("act", lambda e: e.activation(out=qrow, in_=qrow, func=AF.Exp, scale=-0.5),
         reads=(qrk,), writes=(qrk,))
    if kind == "ln":
        p.op("dve", lambda e: e.tensor_mul(srow, srow, qrow), reads=(srk, qrk), writes=(srk,))
    stg_b = [ctx.ab.alloc(TSUB) for _ in range(2)]
    stg_f = [ctx.af.alloc(TSUB) for _ in range(2)]
    xt2 = [ctx.af.alloc(TSUB) for _ in range(2)]
    t32 = [ctx.af.alloc(TSUB) for _ in range(2)]
    for sub in range(NSUB):
        cs = slice(sub * TSUB, (sub + 1) * TSUB)
        rb = f"ps{(2 * sub) % 4}"
        mb = f"ps{(2 * sub) % 4 + 1}"
        p.op("pe", lambda e, cs=cs, rb=rb: e.matmul(ps[rb].ap()[:, :], onesrow,
             qrow[0:1, cs], start=True, stop=True),
             reads=(qrk, "onesrow"), writes=(rb,))
        if kind == "ln":
            p.op("pe", lambda e, cs=cs, mb=mb: e.matmul(ps[mb].ap()[:, :], onesrow,
                 srow[0:1, cs], start=True, stop=True),
                 reads=(srk, "onesrow"), writes=(mb,))
        for c in range(NCHUNK):
            xa, xak = xt2[(sub * NCHUNK + c) % 2]
            p.op("sp", lambda e, a=xa, d=src_dram[c * 128:(c + 1) * 128, cs]: e.dma_start(out=a, in_=d),
                 reads=(src_key,), writes=(xak,), dma=True)
            o, ok = stg_b[c % 2] if dst_dram.dtype == BF16 else stg_f[c % 2]
            if kind == "ln":
                t, tk = t32[c % 2]
                p.op("dve", lambda e, s=xa, t=t, rb=rb: e.tensor_mul(t, s, ps[rb].ap()[:, :]),
                     reads=(xak, rb), writes=(tk,))
                p.op("dve", lambda e, t=t, o=o, mb=mb: e.tensor_sub(o, t, ps[mb].ap()[:, :]),
                     reads=(tk, mb), writes=(ok,))
            else:
                p.op("dve", lambda e, s=xa, o=o, c=c, rb=rb:
                     e.scalar_tensor_tensor(out=o, in0=s, scalar=wcol[:, c:c + 1],
                                            in1=ps[rb].ap()[:, :], op0=ALU.mult, op1=ALU.mult),
                     reads=(xak, rb) + wk_all, writes=(ok,))
            p.op("gp", lambda e, o=o, d=dst_dram[c * 128:(c + 1) * 128, cs]: e.dma_start(out=d, in_=o),
                 reads=(ok,), writes=(f"{dst_key}:{c}:{sub}",), dma="sink")
    p.barrier()
    ctx.ab.reset()
    ctx.af.reset()


def emit_proj(ctx, in_dram, in_key, w_bf_dram, w_key, out_dram, out_key,
              transpose_w, out_dtype=BF16, out_scale=1.0, nin=NCHUNK,
              nout=NCHUNK, residual=None, res_scale=1.0, cols=S, inv_name=None):
    nc, p = ctx.nc, ctx.p
    xb = [ctx.ab.alloc(cols) for _ in range(nin)]
    for c in range(nin):
        a, k = xb[c]
        p.op("sp", lambda e, a=a, d=in_dram[c * 128:(c + 1) * 128, :]: e.dma_start(out=a, in_=d),
             reads=(in_key,), writes=(k,), dma=True)
    if inv_name is not None:
        selp, selpk = ctx.af.alloc(128)
        p.op("sp", lambda e: e.dma_start(out=selp[0:2, :], in_=ctx.d["c_selp"]),
             writes=(selpk,), dma=True)
        iv2 = [ctx.af.alloc(cols) for _ in range(2)]
        for c in range(nin):
            a, k = xb[c]
            iv, ivk = iv2[c % 2]
            p.op("sp", lambda e, iv=iv, d=ctx.d[inv_name][2 * c:2 * c + 2, :]:
                 e.dma_start(out=iv[0:2, :], in_=d),
                 reads=(f"{inv_name}w",), writes=(ivk,), dma=True)
            for sub in range(cols // TSUB):
                cs = slice(sub * TSUB, (sub + 1) * TSUB)
                p.op("pe", lambda e, iv=iv, cs=cs: e.matmul(ctx.s["ps6"].ap()[:, :],
                     selp[0:2, :], iv[0:2, cs], start=True, stop=True),
                     reads=(ivk, selpk), writes=("ps6",))
                p.op("dve", lambda e, a=a, cs=cs: e.tensor_mul(a[:, cs], a[:, cs],
                     ctx.s["ps6"].ap()[:, :]),
                     reads=(k, "ps6"), writes=(k,))
    strips = [ctx.ab.alloc(nout * 128) for _ in range(nin)]
    for c in range(nin):
        a, k = strips[c]
        if transpose_w:
            p.op("sp", lambda e, a=a, d=w_bf_dram[:, c * 128:(c + 1) * 128]:
                 e.dma_start(out=a, in_=d, transpose=True),
                 reads=(w_key,), writes=(k,), dma=True)
        else:
            p.op("sp", lambda e, a=a, d=w_bf_dram[c * 128:(c + 1) * 128, :]:
                 e.dma_start(out=a, in_=d),
                 reads=(w_key,), writes=(k,), dma=True)
    nsub = cols // TSUB
    stg = [(ctx.ab.alloc(TSUB) if out_dtype == BF16 else ctx.af.alloc(TSUB)) for _ in range(4)]
    rst = [ctx.af.alloc(TSUB) for _ in range(2)]
    for do in range(nout):
        for sub in range(nsub):
            cs = slice(sub * TSUB, (sub + 1) * TSUB)
            bank = ctx.s[f"ps{(do * nsub + sub) % 4}"]
            bkey = f"ps{(do * nsub + sub) % 4}"
            for c in range(nin):
                p.op("pe", lambda e, w=strips[c][0], x=xb[c][0], do=do, cs=cs, b=bank,
                     st=(c == 0), sp=(c == nin - 1):
                     e.matmul(b.ap()[:, :], w[:, do * 128:(do + 1) * 128], x[:, cs], start=st, stop=sp),
                     reads=(strips[c][1], xb[c][1]), writes=(bkey,))
            o, ok = stg[(do * nsub + sub) % 4]
            if residual is None:
                p.op("act", lambda e, o=o, b=bank: e.activation(out=o, in_=b.ap()[:, :],
                     func=AF.Copy, scale=out_scale),
                     reads=(bkey,), writes=(ok,))
            else:
                rdram, rkey = residual
                r, rk = rst[sub % 2]
                p.op("sp", lambda e, r=r, d=rdram[do * 128:(do + 1) * 128, cs]: e.dma_start(out=r, in_=d),
                     reads=(rkey,), writes=(rk,), dma=True)
                p.op("dve", lambda e, o=o, b=bank, r=r:
                     e.scalar_tensor_tensor(out=o, in0=b.ap()[:, :], scalar=float(out_scale),
                                            in1=r, op0=ALU.mult, op1=ALU.add),
                     reads=(bkey, rk), writes=(ok,))
            p.op("gp", lambda e, o=o, d=out_dram[do * 128:(do + 1) * 128, cs]: e.dma_start(out=d, in_=o),
                 reads=(ok,), writes=(f"{out_key}:{do}:{sub}",), dma="sink")
    p.barrier()
    ctx.ab.reset()
    ctx.af.reset()



def emit_qkv(ctx, in_dram, in_key, mats):
    nc, p = ctx.nc, ctx.p
    ps = ctx.s
    xb = [ctx.ab.alloc(S) for _ in range(NCHUNK)]
    for c in range(NCHUNK):
        a, k = xb[c]
        p.op("sp", lambda e, a=a, d=in_dram[c * 128:(c + 1) * 128, :]: e.dma_start(out=a, in_=d),
             reads=(in_key,), writes=(k,), dma=True)
    strips = [[ctx.ab.alloc(D) for _ in range(NCHUNK)] for _ in range(2)]
    stg = [ctx.ab.alloc(TSUB) for _ in range(4)]
    for mi, (w_bf, w_key, out_dram, out_key, transpose_w, out_scale) in enumerate(mats):
        sset = strips[mi % 2]
        for c in range(NCHUNK):
            a, k = sset[c]
            if transpose_w:
                p.op("sp", lambda e, a=a, d=w_bf[:, c * 128:(c + 1) * 128]:
                     e.dma_start(out=a, in_=d, transpose=True),
                     reads=(w_key,), writes=(k,), dma=True)
            else:
                p.op("sp", lambda e, a=a, d=w_bf[c * 128:(c + 1) * 128, :]:
                     e.dma_start(out=a, in_=d),
                     reads=(w_key,), writes=(k,), dma=True)
        for do in range(NCHUNK):
            for sub in range(NSUB):
                cs = slice(sub * TSUB, (sub + 1) * TSUB)
                bank = f"ps{(do * NSUB + sub) % 4}"
                for c in range(NCHUNK):
                    p.op("pe", lambda e, w=sset[c][0], x=xb[c][0], do=do, cs=cs, b=bank,
                         st=(c == 0), sp=(c == NCHUNK - 1):
                         e.matmul(ps[b].ap()[:, :], w[:, do * 128:(do + 1) * 128], x[:, cs],
                                  start=st, stop=sp),
                         reads=(sset[c][1], xb[c][1]), writes=(bank,))
                o, ok = stg[(do * NSUB + sub) % 4]
                p.op("act", lambda e, o=o, b=bank, out_scale=out_scale: e.activation(
                     out=o, in_=ps[b].ap()[:, :], func=AF.Copy, scale=out_scale),
                     reads=(bank,), writes=(ok,))
                p.op("gp", lambda e, o=o, d=out_dram[do * 128:(do + 1) * 128, cs]: e.dma_start(out=d, in_=o),
                     reads=(ok,), writes=(f"{out_key}:{do}:{sub}",), dma="sink")
    p.barrier()
    ctx.ab.reset()
    ctx.af.reset()


def emit_attn_core(ctx, q_dram, k_dram, v_dram, ctx_dram, ctx_out_key, causal,
                   masks_dram=None, dbg=None, den_name="den", inv_name="invd"):
    nc, p = ctx.nc, ctx.p
    ps = ctx.s
    onesrow_t, _ = ctx.af.alloc(128)
    onesrow = onesrow_t[0:1, :]
    p.op("dve", lambda e: e.memset(onesrow, 1.0), writes=("onesrow",))
    maskb = None
    if causal:
        maskb, mk = ctx.ab.alloc(4 * TSUB)
        p.op("sp", lambda e: e.dma_start(out=maskb, in_=masks_dram),
             writes=("maskb",), dma=True)
    hsel, hselk = ctx.ab.alloc(128)
    p.op("sp", lambda e: e.dma_start(out=hsel, in_=ctx.d["c_hsel"]),
         writes=(hselk,), dma=True)
    s16, s16k = ctx.af.alloc(16 * 128)
    p.op("sp", lambda e: e.dma_start(out=s16[0:16, :], in_=ctx.d["c_selb16"]),
         writes=(s16k,), dma=True)
    sq2 = [ctx.ab.alloc(S) for _ in range(2)]
    qkc = [ctx.ab.alloc(S) for _ in range(2)]
    nmxa, nmxk = ctx.af.alloc(8)
    biasall, bak = ctx.af.alloc(16)
    halves = ["ps0", "ps1", "ps2", "ps3", "ps4", "ps5", "ps6", "ps7"]
    for oi, dram in ((0, q_dram), (1, k_dram)):
        for c in range(NCHUNK):
            buf, bk2 = qkc[c % 2]
            p.op("sp", lambda e, buf=buf, d=dram[c * 128:(c + 1) * 128, :]:
                 e.dma_start(out=buf, in_=d), writes=(bk2,), dma=True)
            sqa, sqk = sq2[c % 2]
            p.op("dve", lambda e, s=buf, d=sqa: e.tensor_mul(d, s, s),
                 reads=(bk2,), writes=(sqk,))
            for sub in range(NSUB):
                hb = halves[sub]
                p.op("pe", lambda e, s=sqa, sub=sub, c=c, hb=hb: e.matmul(
                     ps[hb].ap()[0:16, :], hsel[:, c * 16:(c + 1) * 16],
                     s[:, sub * TSUB:(sub + 1) * TSUB],
                     start=(c == 0), stop=(c == NCHUNK - 1)),
                     reads=(sqk, hselk), writes=(hb,))
        for sub in range(NSUB):
            hb = halves[sub]
            p.op("dve", lambda e, hb=hb, oi=oi, sub=sub: (
                 e.tensor_reduce(nmxa[0:16, oi:oi + 1], ps[hb].ap()[0:16, :], axis=AX.X, op=ALU.max)
                 if sub == 0 else
                 e.tensor_reduce(nmxa[0:16, 2:3], ps[hb].ap()[0:16, :], axis=AX.X, op=ALU.max)),
                 reads=(hb, nmxk), writes=(nmxk,))
            if sub > 0:
                p.op("dve", lambda e, oi=oi: e.tensor_max(nmxa[0:16, oi:oi + 1],
                     nmxa[0:16, oi:oi + 1], nmxa[0:16, 2:3]),
                     reads=(nmxk,), writes=(nmxk,))
    p.op("dve", lambda e: e.tensor_add(nmxa[0:16, 3:4], nmxa[0:16, 0:1], nmxa[0:16, 1:2]),
         reads=(nmxk,), writes=(nmxk,))
    p.op("dve", lambda e: e.tensor_scalar(out=nmxa[0:16, 3:4], in0=nmxa[0:16, 3:4],
         scalar1=-0.5, scalar2=None, op0=ALU.mult), reads=(nmxk,), writes=(nmxk,))
    for h in range(H):
        p.op("pe", lambda e, h=h: e.matmul(ps["ps6"].ap()[:, h:h + 1],
             s16[0:16, h * 128:(h + 1) * 128], nmxa[0:16, 3:4],
             start=(h == 0), stop=(h == H - 1)),
             reads=(nmxk, s16k), writes=("ps6",))
    p.op("dve", lambda e: e.tensor_copy(biasall, ps["ps6"].ap()[:, 0:16]),
         reads=("ps6",), writes=(bak,))
    qh = [ctx.ab.alloc(S) for _ in range(2)]
    kh = [ctx.ab.alloc(S) for _ in range(2)]
    vtm = [ctx.ab.alloc(32 * 96) for _ in range(2)]
    for va, vk in vtm:
        p.op("dve", lambda e, va=va: e.memset(
             va.rearrange("p (k c) -> p k c", c=96)[:, :, 64:65], 1.0),
             writes=(vk,))
    pT = [ctx.ab.alloc(2 * TSUB) for _ in range(3)]
    cstg = [ctx.ab.alloc(TSUB) for _ in range(2)]
    _ia = [ctx.af.alloc(TSUB) for _ in range(2)]
    pending = [None]
    for h in range(H):
        hp = h % 2
        qa, qk = qh[hp]
        ka, kk = kh[hp]
        va, vk = vtm[hp]
        rs = slice(64 * h, 64 * h + 64)
        p.op("sp", lambda e, a=qa, rs=rs: e.dma_start(out=a[0:64, :], in_=q_dram[rs, :]),
             writes=(qk,), dma=True)
        p.op("sp", lambda e, a=ka, rs=rs: e.dma_start(out=a[0:64, :], in_=k_dram[rs, :]),
             writes=(kk,), dma=True)
        for kt in range(32):
            p.op("sp", lambda e, a=va, kt=kt, rs=rs: e.dma_start(
                 out=a[:, kt * 96:kt * 96 + 64], in_=v_dram[rs, kt * 128:(kt + 1) * 128],
                 transpose=True),
                 writes=(vk,), dma=True)
        biash = biasall[:, h:h + 1]
        for sub in range(NSUB):
            cs = slice(sub * TSUB, (sub + 1) * TSUB)
            qs0 = sub * TSUB
            kts = [kt for kt in range(32) if not (causal and kt * 128 > qs0 + TSUB - 1)]
            groups = [kts[i:i + 2] for i in range(0, len(kts), 2)]
            ng = len(groups)
            def emit_scores(gi, groups=groups, qa=qa, ka=ka, qk=qk, kk=kk, cs=cs, qs0=qs0):
                bank = f"pp{gi % 3}"
                for j, kt in enumerate(groups[gi]):
                    p.op("pe", lambda e, kt=kt, j=j, bank=bank, qa=qa, ka=ka, cs=cs: e.matmul(
                         ps[bank].ap()[:, j * TSUB:(j + 1) * TSUB],
                         ka[0:64, kt * 128:(kt + 1) * 128], qa[0:64, cs],
                         start=True, stop=True),
                         reads=(qk, kk), writes=(bank,))
                w = len(groups[gi]) * TSUB
                pt, ptk = pT[gi % 3]
                p.op("act", lambda e, pt=pt, bank=bank, w=w, biash=biash: e.activation(
                     out=pt[:, 0:w], in_=ps[bank].ap()[:, 0:w], func=AF.Exp, bias=biash),
                     reads=(bank, bak), writes=(ptk,))
                if causal:
                    for j, kt in enumerate(groups[gi]):
                        if kt * 128 + 127 > qs0:
                            dd = (kt * 128 - qs0) // 128
                            assert 0 <= dd <= 3 and dd * 128 == kt * 128 - qs0
                            p.op("dve", lambda e, pt=pt, j=j, dd=dd: e.tensor_mul(
                                 pt[:, j * TSUB:(j + 1) * TSUB], pt[:, j * TSUB:(j + 1) * TSUB],
                                 maskb[:, dd * TSUB:(dd + 1) * TSUB]),
                                 reads=(ptk, "maskb"), writes=(ptk,))
            def emit_ctx(gi, groups=groups, ng=ng, va=va, vk=vk):
                for j, kt in enumerate(groups[gi]):
                    pt, ptk = pT[gi % 3]
                    st = (gi == 0 and j == 0)
                    sp_ = (gi == ng - 1 and j == len(groups[gi]) - 1)
                    p.op("pe", lambda e, pt=pt, j=j, kt=kt, st=st, sp=sp_, va=va: e.matmul(
                         ps["ps4"].ap()[0:65, :], va[:, kt * 96:kt * 96 + 65],
                         pt[:, j * TSUB:(j + 1) * TSUB], start=st, stop=sp),
                         reads=(ptk, vk), writes=("ps4",))
            def make_epilogue(sub=sub, cs=cs, rs=rs, h=h):
                par = sub % 2
                def emit_epi():
                    co, cok = cstg[par]
                    p.op("dve", lambda e, co=co: e.tensor_copy(co[0:64, :], ps["ps4"].ap()[0:64, :]),
                         reads=("ps4",), writes=(cok,))
                    dn, dnk = _ia[par]
                    p.op("dve", lambda e, dn=dn: e.tensor_copy(dn[0:1, :], ps["ps4"].ap()[64:65, :]),
                         reads=("ps4",), writes=(dnk,))
                    p.op("gp", lambda e, co=co, d=ctx_dram[rs, cs]: e.dma_start(out=d, in_=co[0:64, :]),
                         reads=(cok,), writes=(f"{ctx_out_key}:{h}:{sub}",), dma="sink")
                    p.op("gp", lambda e, dn=dn, d=ctx.d[den_name][h:h + 1, cs]: e.dma_start(out=d, in_=dn[0:1, :]),
                         reads=(dnk,), writes=(f"den:{h}:{sub}",), dma="sink")
                return emit_epi
            for gi in range(ng):
                emit_scores(gi)
                if gi == 1 and pending[0] is not None:
                    pending[0]()
                    pending[0] = None
                if gi >= 2:
                    emit_ctx(gi - 2)
            if pending[0] is not None:
                pending[0]()
                pending[0] = None
            for gi in range(max(0, ng - 2), ng):
                emit_ctx(gi)
            pending[0] = make_epilogue()
    pending[0]()
    dall, dallk = ctx.af.alloc(S)
    p.op("sp", lambda e: e.dma_start(out=dall[0:16, :], in_=ctx.d[den_name]),
         reads=tuple(f"den:{h}:{s}" for h in range(H) for s in range(NSUB)),
         writes=(dallk,), dma=True)
    p.op("act", lambda e: e.activation(out=dall[0:16, :], in_=dall[0:16, :], func=AF.Ln),
         reads=(dallk,), writes=(dallk,))
    p.op("act", lambda e: e.activation(out=dall[0:16, :], in_=dall[0:16, :], func=AF.Exp, scale=-1.0),
         reads=(dallk,), writes=(dallk,))
    p.op("gp", lambda e: e.dma_start(out=ctx.d[inv_name], in_=dall[0:16, :]),
         reads=(dallk,), writes=(f"{inv_name}w",), dma="sink")
    p.barrier()
    ctx.ab.reset()
    ctx.af.reset()


def _recip_exact(ctx, dst, dstk, srcden, srck, tmp, tmpk, rows=8):
    p = ctx.p
    p.op("dve", lambda e: e.reciprocal(dst[0:rows, :], srcden[0:rows, :]),
         reads=(srck,), writes=(dstk,))
    p.op("dve", lambda e: e.tensor_mul(tmp[0:rows, :], srcden[0:rows, :], dst[0:rows, :]),
         reads=(srck, dstk), writes=(tmpk,))
    p.op("dve", lambda e: e.tensor_scalar(out=tmp[0:rows, :], in0=tmp[0:rows, :],
         scalar1=-1.0, scalar2=2.0, op0=ALU.mult, op1=ALU.add), reads=(tmpk,), writes=(tmpk,))
    p.op("dve", lambda e: e.tensor_mul(dst[0:rows, :], dst[0:rows, :], tmp[0:rows, :]),
         reads=(dstk, tmpk), writes=(dstk,))


def emit_gate0(ctx, z_chunks, gate_w_f32, we_dram, we_key):
    nc, p = ctx.nc, ctx.p
    ps = ctx.s
    strips = [ctx.af.alloc(8) for _ in range(NCHUNK)]
    for c in range(NCHUNK):
        a, k = strips[c]
        for e_i in range(E):
            p.op("sp", lambda e, a=a, e_i=e_i, d=gate_w_f32[e_i, c * 128:(c + 1) * 128]:
                 e.dma_start(out=a[:, e_i:e_i + 1], in_=d),
                 writes=(k,), dma=True)
        strips[c] = (a, (k,))
    zf32 = [ctx.af.alloc(TSUB) for _ in range(2)]
    bufs = [ctx.af.alloc(TSUB) for _ in range(5)]
    (lg, lgk), (t1, t1k), (t2, t2k), (t3, t3k), (sh, shk) = bufs
    for sub in range(NSUB):
        cs = slice(sub * TSUB, (sub + 1) * TSUB)
        for c in range(NCHUNK):
            zf, zfk = zf32[c % 2]
            p.op("dve", lambda e, zf=zf, zc=z_chunks[c][0], cs=cs: e.tensor_copy(zf, zc[:, cs]),
                 reads=(z_chunks[c][1],), writes=(zfk,))
            p.op("pe", lambda e, c=c, zf=zf: e.matmul(ps["ps0"].ap()[0:8, :],
                 strips[c][0][:, 0:8], zf,
                 start=(c == 0), stop=(c == NCHUNK - 1)),
                 reads=strips[c][1] + (zfk,), writes=("ps0",))
        p.op("dve", lambda e: e.tensor_copy(lg[0:8, :], ps["ps0"].ap()[0:8, :]),
             reads=("ps0",), writes=(lgk,))
        _tree8(ctx, t1[0:8, :], t1k, lg[0:8, :], lgk, ALU.max, sh[0:8, :], shk)
        p.op("dve", lambda e: e.tensor_tensor(t2[0:8, :], lg[0:8, :], t1[0:8, :], ALU.is_equal),
             reads=(lgk, t1k), writes=(t2k,))
        _tree8(ctx, t3[0:8, :], t3k, t2[0:8, :], t2k, ALU.add, sh[0:8, :], shk)
        p.op("dve", lambda e: e.scalar_tensor_tensor(out=t2[0:8, :], in0=t2[0:8, :],
             scalar=-1e30, in1=lg[0:8, :], op0=ALU.mult, op1=ALU.add),
             reads=(t2k, lgk), writes=(t2k,))
        _tree8(ctx, t2[0:8, :], t2k, t2[0:8, :], t2k, ALU.max, sh[0:8, :], shk)
        p.op("dve", lambda e: e.tensor_scalar(out=ctx.s["u8t"].ap()[0:8, 0:TSUB], in0=t3[0:8, :],
             scalar1=1.5, scalar2=None, op0=ALU.is_ge), reads=(t3k,), writes=("u8t",))
        p.op("dve", lambda e: e.select(out=t2[0:8, :], mask=ctx.s["u8t"].ap()[0:8, 0:TSUB],
             on_true=t1[0:8, :], on_false=t2[0:8, :]),
             reads=("u8t", t1k, t2k), writes=(t2k,))
        p.op("dve", lambda e: e.tensor_tensor(t3[0:8, :], lg[0:8, :], t2[0:8, :], ALU.is_ge),
             reads=(lgk, t2k), writes=(t3k,))
        p.op("dve", lambda e: e.tensor_sub(t2[0:8, :], t2[0:8, :], t1[0:8, :]),
             reads=(t2k, t1k), writes=(t2k,))
        p.op("act", lambda e: e.activation(out=t2[0:8, :], in_=t2[0:8, :], func=AF.Exp),
             reads=(t2k,), writes=(t2k,))
        p.op("dve", lambda e: e.tensor_scalar(out=t2[0:8, :], in0=t2[0:8, :], scalar1=1.0,
             scalar2=None, op0=ALU.add), reads=(t2k,), writes=(t2k,))
        p.op("dve", lambda e: e.tensor_sub(t1[0:8, :], lg[0:8, :], t1[0:8, :]),
             reads=(lgk, t1k), writes=(t1k,))
        p.op("act", lambda e: e.activation(out=t1[0:8, :], in_=t1[0:8, :], func=AF.Exp),
             reads=(t1k,), writes=(t1k,))
        _recip_exact(ctx, sh, shk, t2, t2k, lg, lgk)
        p.op("dve", lambda e: e.tensor_mul(t1[0:8, :], t1[0:8, :], sh[0:8, :]),
             reads=(t1k, shk), writes=(t1k,))
        p.op("dve", lambda e: e.tensor_mul(t1[0:8, :], t1[0:8, :], t3[0:8, :]),
             reads=(t1k, t3k), writes=(t1k,))
        p.op("gp", lambda e, cs=cs: e.dma_start(out=we_dram[:, cs], in_=t1[0:8, :]),
             reads=(t1k,), writes=(f"{we_key}:{sub}",), dma="sink")


def _tree8(ctx, out, out_key, in_, in_key, op, sh, shk):
    p = ctx.p
    if out_key != in_key:
        p.op("dve", lambda e: e.tensor_copy(out, in_), reads=(in_key,), writes=(out_key,))
    for r in (4, 2, 1):
        mask = [(i + r) % 8 if i < 8 else i for i in range(32)]
        p.op("dve", lambda e, mask=mask: e.stream_shuffle(out=sh, in_=out, mask=mask),
             reads=(out_key,), writes=(shk,))
        p.op("dve", lambda e, op=op: e.tensor_tensor(out, out, sh, op),
             reads=(out_key, shk), writes=(out_key,))



def emit_zero_dram(ctx, dram, key, rows, cols, dtype=F32):
    p = ctx.p
    z, zk = (ctx.af if dtype == F32 else ctx.ab).alloc(min(cols, S))
    p.op("dve", lambda e: e.memset(z, 0.0), writes=(zk,))
    for r0 in range(0, rows, 128):
        rr = min(128, rows - r0)
        for c0 in range(0, cols, z.shape[1]):
            cc = min(z.shape[1], cols - c0)
            p.op("gp", lambda e, d=dram[r0:r0 + rr, c0:c0 + cc], s=z[0:rr, 0:cc]:
                 e.dma_start(out=d, in_=s),
                 reads=(zk,), writes=(f"{key}:{r0}:{c0}",), dma="sink")


def emit_moe0(ctx, z_dram, z_key, w1_bf, w2_bf, w3_bf, gate_wT_bf, we_dram,
              moe_dram, moe_key, selb8, selb8_key):
    nc, p = ctx.nc, ctx.p
    ps = ctx.s
    zc = load_chunks(ctx, z_dram, z_key)
    emit_gate0(ctx, zc, gate_wT_bf, we_dram, "we0")
    p.barrier()
    ctx.ab.reset()
    ctx.af.reset()
    emit_zero_dram(ctx, moe_dram, moe_key, D, S)
    wer, werk = ctx.af.alloc(S)
    p.op("sp", lambda e: e.dma_start(out=wer[0:8, :], in_=we_dram), reads=("we0",),
         writes=(werk,), dma=True)
    w1s = [ctx.ab.alloc(IMOE) for _ in range(NCHUNK)]
    w3s = [ctx.ab.alloc(IMOE) for _ in range(NCHUNK)]
    zs2 = [[ctx.ab.alloc(TSUB) for _ in range(NCHUNK)] for _ in range(2)]
    g = [ctx.ab.alloc(TSUB) for _ in range(16)]
    w2s = [ctx.ab.alloc(D) for _ in range(3)]
    sbf = [ctx.ab.alloc(TSUB) for _ in range(1)]
    vmb = [ctx.ab.alloc(TSUB) for _ in range(1)]
    webc = [ctx.af.alloc(TSUB) for _ in range(2)]
    stg = [ctx.af.alloc(TSUB) for _ in range(8)]
    def load_w13(e_i):
        for qi in range(4):
            for c in range(NCHUNK):
                p.op("sp", lambda e, a=w1s[c][0], qi=qi,
                     d=w1_bf[e_i][qi * 512:(qi + 1) * 512, c * 128:(c + 1) * 128]:
                     e.dma_start(out=a[:, qi * 512:(qi + 1) * 512], in_=d, transpose=True),
                     reads=("w1bf",), writes=(f"{w1s[c][1]}:{qi}",), dma=True)
                p.op("sp", lambda e, a=w3s[c][0], qi=qi,
                     d=w3_bf[e_i][qi * 512:(qi + 1) * 512, c * 128:(c + 1) * 128]:
                     e.dma_start(out=a[:, qi * 512:(qi + 1) * 512], in_=d, transpose=True),
                     reads=("w3bf",), writes=(f"{w3s[c][1]}:{qi}",), dma=True)
    load_w13(0)
    for e_i in range(E):
        for sub in range(NSUB):
            cs = slice(sub * TSUB, (sub + 1) * TSUB)
            zp = sub % 2
            for c in range(NCHUNK):
                p.op("sp", lambda e, a=zs2[zp][c][0], d=z_dram[c * 128:(c + 1) * 128, cs]:
                     e.dma_start(out=a, in_=d),
                     reads=(z_key,), writes=(zs2[zp][c][1],), dma=True)
            zs = zs2[zp]
            wb, wbk = webc[sub % 2]
            p.op("pe", lambda e, cs=cs, e_i=e_i: e.matmul(ps["ps6"].ap()[:, :],
                 selb8[0:8, e_i * 128:(e_i + 1) * 128], wer[0:8, cs], start=True, stop=True),
                 reads=(werk, selb8_key), writes=("ps6",))
            p.op("dve", lambda e, wb=wb: e.tensor_copy(wb, ps["ps6"].ap()[:, :]),
                 reads=("ps6",), writes=(wbk,))
            for ig in range(16):
                pu = "ps0" if ig % 2 == 0 else "ps1"
                pv = "ps2" if ig % 2 == 0 else "ps3"
                for c in range(NCHUNK):
                    p.op("pe", lambda e, c=c, ig=ig, pu=pu, zs=zs: e.matmul(ps[pu].ap()[:, :],
                         w1s[c][0][:, ig * 128:(ig + 1) * 128], zs[c][0],
                         start=(c == 0), stop=(c == NCHUNK - 1)),
                         reads=(f"{w1s[c][1]}:{ig // 4}", zs[c][1]), writes=(pu,))
                sb, sbk = sbf[0]
                p.op("act", lambda e, sb=sb, pu=pu: e.activation(out=sb, in_=ps[pu].ap()[:, :], func=AF.Silu),
                     reads=(pu,), writes=(sbk,))
                for c in range(NCHUNK):
                    p.op("pe", lambda e, c=c, ig=ig, pv=pv, zs=zs: e.matmul(ps[pv].ap()[:, :],
                         w3s[c][0][:, ig * 128:(ig + 1) * 128], zs[c][0],
                         start=(c == 0), stop=(c == NCHUNK - 1)),
                         reads=(f"{w3s[c][1]}:{ig // 4}", zs[c][1]), writes=(pv,))
                vm, vmk = vmb[0]
                p.op("dve", lambda e, vm=vm, pv=pv, wb=wb: e.tensor_mul(vm, ps[pv].ap()[:, :], wb),
                     reads=(pv, wbk), writes=(vmk,))
                p.op("dve", lambda e, gt=g[ig][0], sb=sb, vm=vm: e.tensor_mul(gt, sb, vm),
                     reads=(sbk, vmk), writes=(g[ig][1],))
            if sub == NSUB - 1 and e_i + 1 < E:
                load_w13(e_i + 1)
            def w2load(ig):
                ws, wsk = w2s[ig % 3]
                p.op("sp", lambda e, ws=ws, ig=ig, d=w2_bf[e_i][:, ig * 128:(ig + 1) * 128]:
                     e.dma_start(out=ws, in_=d, transpose=True),
                     reads=("w2bf",), writes=(wsk,), dma=True)
            for ig in range(3):
                w2load(ig)
            for ig in range(16):
                ws, wsk = w2s[ig % 3]
                for ho in range(NCHUNK):
                    p.op("pe", lambda e, ws=ws, ho=ho, ig=ig: e.matmul(ps[f"ps{ho}"].ap()[:, :],
                         ws[:, ho * 128:(ho + 1) * 128], g[ig][0],
                         start=(ig == 0), stop=(ig == 15)),
                         reads=(wsk, g[ig][1]), writes=(f"ps{ho}",))
                if ig + 3 < 16:
                    w2load(ig + 3)
            for ho in (6, 0, 2, 7, 1, 3, 4, 5):
                st, stk = stg[ho]
                p.op("dve", lambda e, st=st, ho=ho: e.tensor_copy(st, ps[f"ps{ho}"].ap()[:, :]),
                     reads=(f"ps{ho}",), writes=(stk,))
                p.op("gp", lambda e, st=st, d=moe_dram[ho * 128:(ho + 1) * 128, cs]:
                     e.dma_start(out=d, in_=st, accum_op=ALU.add),
                     reads=(stk,), writes=(f"{moe_key}:{ho * 128}:{sub * TSUB}",), dma="sink")
    p.barrier()
    ctx.ab.reset()
    ctx.af.reset()


def emit_combine(ctx, x_dram, x_key, h_dram, h_key, moe_dram, moe_key,
                 out_dram, out_key):
    p = ctx.p
    bufs = [[ctx.af.alloc(2048) for _ in range(3)] for _ in range(2)]
    for c in range(NCHUNK):
        for hf in range(2):
            cs = slice(hf * 2048, (hf + 1) * 2048)
            (xa, xk), (ha, hk), (ma, mk) = bufs[(c * 2 + hf) % 2]
            p.op("sp", lambda e, a=xa, d=x_dram[c * 128:(c + 1) * 128, cs]: e.dma_start(out=a, in_=d),
                 reads=(x_key,), writes=(xk,), dma=True)
            p.op("sp", lambda e, a=ha, d=h_dram[c * 128:(c + 1) * 128, cs]: e.dma_start(out=a, in_=d),
                 reads=(h_key,), writes=(hk,), dma=True)
            p.op("sp", lambda e, a=ma, d=moe_dram[c * 128:(c + 1) * 128, cs]: e.dma_start(out=a, in_=d),
                 reads=(moe_key,), writes=(mk,), dma=True)
            p.op("dve", lambda e, ha=ha, xa=xa: e.scalar_tensor_tensor(out=ha, in0=ha,
                 scalar=0.5, in1=xa, op0=ALU.mult, op1=ALU.add),
                 reads=(hk, xk), writes=(hk,))
            p.op("dve", lambda e, ma=ma, ha=ha: e.scalar_tensor_tensor(out=ma, in0=ma,
                 scalar=0.5, in1=ha, op0=ALU.mult, op1=ALU.add),
                 reads=(mk, hk), writes=(mk,))
            p.op("gp", lambda e, a=ma, d=out_dram[c * 128:(c + 1) * 128, cs]: e.dma_start(out=d, in_=a),
                 reads=(mk,), writes=(f"{out_key}:{c}:{hf}",), dma="sink")
    p.barrier()
    ctx.ab.reset()
    ctx.af.reset()


def emit_l0attn_win(ctx, q_dram, k_dram, v_dram, ce_dram, co_dram, ce_key, co_key):
    nc, p = ctx.nc, ctx.p
    ps = ctx.s
    zb, zbk = ctx.ab.alloc(128)
    p.op("dve", lambda e: e.memset(zb, 0.0), writes=(zbk,))
    on1, onk = ctx.af.alloc(128)
    p.op("dve", lambda e: e.memset(on1, 1.0), writes=(onk,))
    for c in range(NCHUNK):
        p.op("gp", lambda e, d=co_dram[c * 128:(c + 1) * 128, 0:128]: e.dma_start(out=d, in_=zb),
             reads=(zbk,), writes=(f"{co_key}:z0:{c}",), dma="sink")
        p.op("gp", lambda e, d=co_dram[c * 128:(c + 1) * 128, 3968:4096]: e.dma_start(out=d, in_=zb),
             reads=(zbk,), writes=(f"{co_key}:z1:{c}",), dma="sink")
    p.op("gp", lambda e: e.dma_start(out=ctx.d["den_o"][0:16, 0:128], in_=on1[0:16, :]),
         reads=(onk,), writes=("den_o:z0",), dma="sink")
    p.op("gp", lambda e: e.dma_start(out=ctx.d["den_o"][0:16, 3968:4096], in_=on1[0:16, :]),
         reads=(onk,), writes=("den_o:z1",), dma="sink")
    hsel, hselk = ctx.ab.alloc(128)
    p.op("sp", lambda e: e.dma_start(out=hsel, in_=ctx.d["c_hsel"]),
         writes=(hselk,), dma=True)
    s16, s16k = ctx.af.alloc(16 * 128)
    p.op("sp", lambda e: e.dma_start(out=s16[0:16, :], in_=ctx.d["c_selb16"]),
         writes=(s16k,), dma=True)
    sq2 = [ctx.ab.alloc(S) for _ in range(2)]
    qkc = [ctx.ab.alloc(S) for _ in range(2)]
    nmxa, nmxk = ctx.af.alloc(8)
    biasall, bak = ctx.af.alloc(16)
    halves = ["ps0", "ps1", "ps2", "ps3", "ps4", "ps5", "ps6", "ps7"]
    for oi, dram in ((0, q_dram), (1, k_dram)):
        for c in range(NCHUNK):
            buf, bk2 = qkc[c % 2]
            p.op("sp", lambda e, buf=buf, d=dram[c * 128:(c + 1) * 128, :]:
                 e.dma_start(out=buf, in_=d), writes=(bk2,), dma=True)
            sqa, sqk = sq2[c % 2]
            p.op("dve", lambda e, s=buf, d=sqa: e.tensor_mul(d, s, s),
                 reads=(bk2,), writes=(sqk,))
            for sub in range(NSUB):
                hb = halves[sub]
                p.op("pe", lambda e, s=sqa, sub=sub, c=c, hb=hb: e.matmul(
                     ps[hb].ap()[0:16, :], hsel[:, c * 16:(c + 1) * 16],
                     s[:, sub * TSUB:(sub + 1) * TSUB],
                     start=(c == 0), stop=(c == NCHUNK - 1)),
                     reads=(sqk, hselk), writes=(hb,))
        for sub in range(NSUB):
            hb = halves[sub]
            p.op("dve", lambda e, hb=hb, oi=oi, sub=sub: (
                 e.tensor_reduce(nmxa[0:16, oi:oi + 1], ps[hb].ap()[0:16, :], axis=AX.X, op=ALU.max)
                 if sub == 0 else
                 e.tensor_reduce(nmxa[0:16, 2:3], ps[hb].ap()[0:16, :], axis=AX.X, op=ALU.max)),
                 reads=(hb, nmxk), writes=(nmxk,))
            if sub > 0:
                p.op("dve", lambda e, oi=oi: e.tensor_max(nmxa[0:16, oi:oi + 1],
                     nmxa[0:16, oi:oi + 1], nmxa[0:16, 2:3]),
                     reads=(nmxk,), writes=(nmxk,))
    p.op("dve", lambda e: e.tensor_add(nmxa[0:16, 3:4], nmxa[0:16, 0:1], nmxa[0:16, 1:2]),
         reads=(nmxk,), writes=(nmxk,))
    p.op("dve", lambda e: e.tensor_scalar(out=nmxa[0:16, 3:4], in0=nmxa[0:16, 3:4],
         scalar1=-0.5, scalar2=None, op0=ALU.mult), reads=(nmxk,), writes=(nmxk,))
    for h in range(H):
        p.op("pe", lambda e, h=h: e.matmul(ps["ps6"].ap()[:, h:h + 1],
             s16[0:16, h * 128:(h + 1) * 128], nmxa[0:16, 3:4],
             start=(h == 0), stop=(h == H - 1)),
             reads=(nmxk, s16k), writes=("ps6",))
    p.op("dve", lambda e: e.tensor_copy(biasall, ps["ps6"].ap()[:, 0:16]),
         reads=("ps6",), writes=(bak,))
    qh = [ctx.ab.alloc(S) for _ in range(2)]
    kh = [ctx.ab.alloc(S) for _ in range(2)]
    vtm = [ctx.ab.alloc(32 * 96) for _ in range(2)]
    for va, vk in vtm:
        p.op("dve", lambda e, va=va: e.memset(
             va.rearrange("p (k c) -> p k c", c=96)[:, :, 64:65], 1.0),
             writes=(vk,))
    pT = [ctx.ab.alloc(TSUB) for _ in range(3)]
    cstg = [ctx.ab.alloc(WIN) for _ in range(2)]
    dstg = [ctx.af.alloc(WIN) for _ in range(2)]
    pending = [None]
    for h in range(H):
        hp = h % 2
        qa, qk = qh[hp]
        ka, kk = kh[hp]
        va, vk = vtm[hp]
        rs = slice(64 * h, 64 * h + 64)
        p.op("sp", lambda e, a=qa, rs=rs: e.dma_start(out=a[0:64, :], in_=q_dram[rs, :]),
             writes=(qk,), dma=True)
        p.op("sp", lambda e, a=ka, rs=rs: e.dma_start(out=a[0:64, :], in_=k_dram[rs, :]),
             writes=(kk,), dma=True)
        for kt in range(32):
            p.op("sp", lambda e, a=va, kt=kt, rs=rs: e.dma_start(
                 out=a[:, kt * 96:kt * 96 + 64], in_=v_dram[rs, kt * 128:(kt + 1) * 128],
                 transpose=True),
                 writes=(vk,), dma=True)
        biash = biasall[:, h:h + 1]
        for n in range(NWIN):
            cs = slice(n * STRIDE, n * STRIDE + WIN)
            cbank = "ps4" if n % 2 == 0 else "ps5"
            def emit_scores(n=n, cs=cs, qa=qa, ka=ka, qk=qk, kk=kk):
                bank = f"pp{n % 3}"
                for j, kt in enumerate((n, n + 1)):
                    p.op("pe", lambda e, kt=kt, j=j, bank=bank, qa=qa, ka=ka, cs=cs: e.matmul(
                         ps[bank].ap()[:, j * WIN:(j + 1) * WIN],
                         ka[0:64, kt * 128:(kt + 1) * 128], qa[0:64, cs],
                         start=True, stop=True),
                         reads=(qk, kk), writes=(bank,))
                pt, ptk = pT[n % 3]
                p.op("act", lambda e, pt=pt, bank=bank, biash=biash: e.activation(
                     out=pt, in_=ps[bank].ap()[:, 0:2 * WIN], func=AF.Exp, bias=biash),
                     reads=(bank, bak), writes=(ptk,))
            def emit_ctx(n=n, cbank=cbank, va=va, vk=vk):
                pt, ptk = pT[n % 3]
                for j, kt in enumerate((n, n + 1)):
                    p.op("pe", lambda e, pt=pt, j=j, kt=kt, st=(j == 0), sp=(j == 1),
                         va=va, cbank=cbank: e.matmul(
                         ps[cbank].ap()[0:65, 0:WIN], va[:, kt * 96:kt * 96 + 65],
                         pt[:, j * WIN:(j + 1) * WIN], start=st, stop=sp),
                         reads=(ptk, vk), writes=(cbank,))
            def make_epi(n=n, cs=cs, rs=rs, h=h, cbank=cbank):
                par = n % 2
                tgt = ce_dram if par == 0 else co_dram
                tkey = ce_key if par == 0 else co_key
                dent = ctx.d["den_e"] if par == 0 else ctx.d["den_o"]
                def emit_epi():
                    co, cok = cstg[par]
                    p.op("dve", lambda e, co=co, cbank=cbank: e.tensor_copy(co[0:64, :], ps[cbank].ap()[0:64, 0:WIN]),
                         reads=(cbank,), writes=(cok,))
                    dn, dnk = dstg[par]
                    p.op("dve", lambda e, dn=dn, cbank=cbank: e.tensor_copy(dn[0:1, :], ps[cbank].ap()[64:65, 0:WIN]),
                         reads=(cbank,), writes=(dnk,))
                    p.op("gp", lambda e, co=co, d=tgt[rs, cs]: e.dma_start(out=d, in_=co[0:64, :]),
                         reads=(cok,), writes=(f"{tkey}:{h}:{n}",), dma="sink")
                    p.op("gp", lambda e, dn=dn, d=dent[h:h + 1, cs]: e.dma_start(out=d, in_=dn[0:1, :]),
                         reads=(dnk,), writes=(f"denw:{h}:{n}",), dma="sink")
                return emit_epi
            emit_scores()
            if pending[0] is not None:
                pending[0]()
                pending[0] = None
            emit_ctx()
            pending[0] = make_epi()
    pending[0]()
    dall, dallk = ctx.af.alloc(S)
    for dname, iname, deps in (("den_e", "inv_e", tuple(f"denw:{h}:{n}" for h in range(H) for n in range(0, NWIN, 2))),
                               ("den_o", "inv_o", tuple(f"denw:{h}:{n}" for h in range(H) for n in range(1, NWIN, 2)) + ("den_o:z0", "den_o:z1"))):
        p.op("sp", lambda e, dname=dname, dall=dall: e.dma_start(out=dall[0:16, :], in_=ctx.d[dname]),
             reads=deps, writes=(dallk,), dma=True)
        p.op("act", lambda e, dall=dall: e.activation(out=dall[0:16, :], in_=dall[0:16, :], func=AF.Ln),
             reads=(dallk,), writes=(dallk,))
        p.op("act", lambda e, dall=dall: e.activation(out=dall[0:16, :], in_=dall[0:16, :], func=AF.Exp, scale=-1.0),
             reads=(dallk,), writes=(dallk,))
        p.op("gp", lambda e, iname=iname, dall=dall: e.dma_start(out=ctx.d[iname], in_=dall[0:16, :]),
             reads=(dallk,), writes=(f"{iname}w",), dma="sink")
    p.barrier()
    ctx.ab.reset()
    ctx.af.reset()


def emit_wo_win(ctx, ce_dram, ce_key, co_dram, co_key, wo_bf, x_dram, x_key,
                invc_row_dram, out_dram, out_key):
    nc, p = ctx.nc, ctx.p
    ps = ctx.s
    onesrow_t, _ = ctx.af.alloc(128)
    onesrow = onesrow_t[0:1, :]
    p.op("dve", lambda e: e.memset(onesrow, 1.0), writes=("onesrow",))
    invr, invk = ctx.af.alloc(S)
    p.op("sp", lambda e: e.dma_start(out=invr[0:1, :], in_=invc_row_dram),
         writes=(invk,), dma=True)
    p.op("dve", lambda e: e.tensor_scalar(out=invr[0:1, :], in0=invr[0:1, :],
         scalar1=1e-8, scalar2=None, op0=ALU.add), reads=(invk,), writes=(invk,))
    p.op("dve", lambda e: e.reciprocal(invr[0:1, :], invr[0:1, :]), reads=(invk,), writes=(invk,))
    strips = [ctx.ab.alloc(D) for _ in range(NCHUNK)]
    for c in range(NCHUNK):
        p.op("sp", lambda e, a=strips[c][0], d=wo_bf[c * 128:(c + 1) * 128, :]:
             e.dma_start(out=a, in_=d), writes=(strips[c][1],), dma=True)
    cbufs = [[ctx.ab.alloc(TSUB) for _ in range(NCHUNK)] for _ in range(2)]
    obufs = [[ctx.ab.alloc(TSUB) for _ in range(NCHUNK)] for _ in range(2)]
    ibc = [ctx.af.alloc(TSUB) for _ in range(2)]
    xst = [ctx.af.alloc(TSUB) for _ in range(2)]
    ost = [ctx.af.alloc(TSUB) for _ in range(2)]
    selp, selpk = ctx.af.alloc(128)
    p.op("sp", lambda e: e.dma_start(out=selp[0:2, :], in_=ctx.d["c_selp"]),
         writes=(selpk,), dma=True)
    iv2 = [[ctx.af.alloc(TSUB) for _ in range(2)] for _ in range(2)]
    for sub in range(NSUB):
        cs = slice(sub * TSUB, (sub + 1) * TSUB)
        par = sub % 2
        for c in range(NCHUNK):
            p.op("sp", lambda e, a=cbufs[par][c][0], d=ce_dram[c * 128:(c + 1) * 128, cs]:
                 e.dma_start(out=a, in_=d), reads=(ce_key,), writes=(cbufs[par][c][1],), dma=True)
            p.op("sp", lambda e, a=obufs[par][c][0], d=co_dram[c * 128:(c + 1) * 128, cs]:
                 e.dma_start(out=a, in_=d), reads=(co_key,), writes=(obufs[par][c][1],), dma=True)
        for c in range(NCHUNK):
            for si, (bufpair, iname, ikey) in enumerate(((cbufs[par][c], "inv_e", "inv_ew"),
                                                         (obufs[par][c], "inv_o", "inv_ow"))):
                a, akey = bufpair
                i2, i2k = iv2[si][c % 2]
                p.op("sp", lambda e, i2=i2, d=ctx.d[iname][2 * c:2 * c + 2, cs]:
                     e.dma_start(out=i2[0:2, 0:TSUB], in_=d),
                     reads=(ikey,), writes=(i2k,), dma=True)
                p.op("pe", lambda e, i2=i2: e.matmul(ctx.s["ps6"].ap()[:, :],
                     selp[0:2, :], i2[0:2, 0:TSUB], start=True, stop=True),
                     reads=(i2k, selpk), writes=("ps6",))
                p.op("dve", lambda e, a=a: e.tensor_mul(a, a, ctx.s["ps6"].ap()[:, :]),
                     reads=(akey, "ps6"), writes=(akey,))
        ib, ibk = ibc[par]
        p.op("pe", lambda e, cs=cs: e.matmul(ps["ps6"].ap()[:, :], onesrow, invr[0:1, cs],
             start=True, stop=True), reads=(invk, "onesrow"), writes=("ps6",))
        p.op("dve", lambda e, ib=ib: e.tensor_copy(ib, ps["ps6"].ap()[:, :]),
             reads=("ps6",), writes=(ibk,))
        for do in range(NCHUNK):
            bank = f"ps{do % 4}"
            for c in range(NCHUNK):
                p.op("pe", lambda e, c=c, do=do, bank=bank, par=par: e.matmul(ps[bank].ap()[:, :],
                     strips[c][0][:, do * 128:(do + 1) * 128], cbufs[par][c][0],
                     start=(c == 0), stop=False),
                     reads=(strips[c][1], cbufs[par][c][1]), writes=(bank,))
            for c in range(NCHUNK):
                p.op("pe", lambda e, c=c, do=do, bank=bank, par=par: e.matmul(ps[bank].ap()[:, :],
                     strips[c][0][:, do * 128:(do + 1) * 128], obufs[par][c][0],
                     start=False, stop=(c == NCHUNK - 1)),
                     reads=(strips[c][1], obufs[par][c][1]), writes=(bank,))
            xa, xk2 = xst[do % 2]
            p.op("sp", lambda e, a=xa, d=x_dram[do * 128:(do + 1) * 128, cs]:
                 e.dma_start(out=a, in_=d), reads=(x_key,), writes=(xk2,), dma=True)
            oa, ok2 = ost[do % 2]
            p.op("dve", lambda e, oa=oa, bank=bank, ib=ib: e.tensor_mul(oa, ps[bank].ap()[:, :], ib),
                 reads=(bank, ibk), writes=(ok2,))
            p.op("dve", lambda e, oa=oa, xa=xa: e.tensor_add(oa, oa, xa),
                 reads=(ok2, xk2), writes=(ok2,))
            p.op("gp", lambda e, oa=oa, d=out_dram[do * 128:(do + 1) * 128, cs]:
                 e.dma_start(out=d, in_=oa), reads=(ok2,), writes=(f"{out_key}:{do}:{sub}",), dma="sink")
    p.barrier()
    ctx.ab.reset()
    ctx.af.reset()


def emit_gate1(ctx, z_dram, z_key, gate_w_bf, gate_b_dram, masked_dram, masked_key):
    nc, p = ctx.nc, ctx.p
    ps = ctx.s
    strips = [ctx.af.alloc(8) for _ in range(NCHUNK)]
    for c in range(NCHUNK):
        a, k = strips[c]
        p.op("sp", lambda e, a=a, d=gate_w_bf[c * 128:(c + 1) * 128, :]:
             e.dma_start(out=a, in_=d), writes=(k,), dma=True)
    bcol, bck = ctx.af.alloc(1)
    p.op("sp", lambda e: e.dma_start(out=bcol[0:8, :], in_=gate_b_dram),
         writes=(bck,), dma=True)
    zc = [ctx.ab.alloc(TSUB) for _ in range(NCHUNK)]
    zf32 = [ctx.af.alloc(TSUB) for _ in range(2)]
    bufs = [ctx.af.alloc(TSUB) for _ in range(5)]
    (lg, lgk), (t1, t1k), (t2, t2k), (t3, t3k), (sh, shk) = bufs
    for sub in range(NSUB):
        cs = slice(sub * TSUB, (sub + 1) * TSUB)
        for c in range(NCHUNK):
            p.op("sp", lambda e, a=zc[c][0], d=z_dram[c * 128:(c + 1) * 128, cs]:
                 e.dma_start(out=a, in_=d), reads=(z_key,), writes=(zc[c][1],), dma=True)
        for c in range(NCHUNK):
            zf, zfk = zf32[c % 2]
            p.op("dve", lambda e, zf=zf, zc=zc[c][0]: e.tensor_copy(zf, zc),
                 reads=(zc[c][1],), writes=(zfk,))
            p.op("pe", lambda e, c=c, zf=zf: e.matmul(ps["ps0"].ap()[0:8, :],
                 strips[c][0][:, 0:8], zf,
                 start=(c == 0), stop=(c == NCHUNK - 1)),
                 reads=(strips[c][1], zfk), writes=("ps0",))
        p.op("dve", lambda e: e.tensor_scalar(out=lg[0:8, :], in0=ps["ps0"].ap()[0:8, :],
             scalar1=bcol[0:8, :], scalar2=None, op0=ALU.add),
             reads=("ps0", bck), writes=(lgk,))
        _tree8(ctx, t1[0:8, :], t1k, lg[0:8, :], lgk, ALU.max, sh[0:8, :], shk)
        p.op("dve", lambda e: e.tensor_sub(t2[0:8, :], lg[0:8, :], t1[0:8, :]),
             reads=(lgk, t1k), writes=(t2k,))
        p.op("act", lambda e: e.activation(out=t2[0:8, :], in_=t2[0:8, :], func=AF.Exp),
             reads=(t2k,), writes=(t2k,))
        _tree8(ctx, t3[0:8, :], t3k, t2[0:8, :], t2k, ALU.add, sh[0:8, :], shk)
        _recip_exact(ctx, sh, shk, t3, t3k, lg, lgk)
        p.op("dve", lambda e: e.tensor_mul(lg[0:8, :], t2[0:8, :], sh[0:8, :]),
             reads=(t2k, shk), writes=(lgk,))
        _tree8(ctx, t1[0:8, :], t1k, lg[0:8, :], lgk, ALU.max, sh[0:8, :], shk)
        p.op("dve", lambda e: e.tensor_tensor(t2[0:8, :], lg[0:8, :], t1[0:8, :], ALU.is_equal),
             reads=(lgk, t1k), writes=(t2k,))
        _tree8(ctx, t3[0:8, :], t3k, t2[0:8, :], t2k, ALU.add, sh[0:8, :], shk)
        p.op("dve", lambda e: e.scalar_tensor_tensor(out=t2[0:8, :], in0=t2[0:8, :],
             scalar=-1e30, in1=lg[0:8, :], op0=ALU.mult, op1=ALU.add),
             reads=(t2k, lgk), writes=(t2k,))
        _tree8(ctx, t2[0:8, :], t2k, t2[0:8, :], t2k, ALU.max, sh[0:8, :], shk)
        p.op("dve", lambda e: e.tensor_scalar(out=ctx.s["u8t"].ap()[0:8, 0:TSUB], in0=t3[0:8, :],
             scalar1=1.5, scalar2=None, op0=ALU.is_ge), reads=(t3k,), writes=("u8t",))
        p.op("dve", lambda e: e.select(out=t2[0:8, :], mask=ctx.s["u8t"].ap()[0:8, 0:TSUB],
             on_true=t1[0:8, :], on_false=t2[0:8, :]),
             reads=("u8t", t1k, t2k), writes=(t2k,))
        p.op("dve", lambda e: e.tensor_tensor(t3[0:8, :], lg[0:8, :], t2[0:8, :], ALU.is_ge),
             reads=(lgk, t2k), writes=(t3k,))
        p.op("dve", lambda e: e.tensor_mul(t1[0:8, :], lg[0:8, :], t3[0:8, :]),
             reads=(lgk, t3k), writes=(t1k,))
        p.op("gp", lambda e, cs=cs: e.dma_start(out=masked_dram[:, cs], in_=t1[0:8, :]),
             reads=(t1k,), writes=(f"{masked_key}:{sub}",), dma="sink")
    p.barrier()
    ctx.ab.reset()
    ctx.af.reset()


def emit_l1ffn(ctx, z_dram, z_key, mall_dram, mown_dram, w1_bf, b1_dram,
               w2_bf, b2_bf, x_dram, x_key, out_dram, out_key, selb8, selb8_key):
    nc, p = ctx.nc, ctx.p
    ps = ctx.s
    den = [ctx.af.alloc(TSUB) for _ in range(2)]
    acc, acck = ctx.af.alloc(TSUB)
    own, ownk = ctx.af.alloc(TSUB)
    r32, r32k = ctx.af.alloc(1024)
    r8, r8k = ctx.af.alloc(S)
    for sub in range(NSUB):
        cs = slice(sub * TSUB, (sub + 1) * TSUB)
        a0, k0 = den[0]
        a1, k1 = den[1]
        for b in range(8):
            dst = (a0, k0) if b % 2 == 0 else (a1, k1)
            p.op("sp", lambda e, d=mall_dram[b, :, cs], a=dst[0]: e.dma_start(out=a[0:8, :], in_=d),
                 writes=(dst[1],), dma=True)
            if b == 0:
                p.op("dve", lambda e, a=dst[0]: e.tensor_copy(acc[0:8, :], a[0:8, :]),
                     reads=(dst[1],), writes=(acck,))
            else:
                p.op("dve", lambda e, a=dst[0]: e.tensor_add(acc[0:8, :], acc[0:8, :], a[0:8, :]),
                     reads=(dst[1], acck), writes=(acck,))
        p.op("dve", lambda e: e.tensor_scalar(out=acc[0:8, :], in0=acc[0:8, :],
             scalar1=L1F_EPS, scalar2=None, op0=ALU.add), reads=(acck,), writes=(acck,))
        p.op("sp", lambda e, d=mown_dram[:, cs]: e.dma_start(out=own[0:8, :], in_=d),
             writes=(ownk,), dma=True)
        _recip_exact(ctx, den[0][0], den[0][1], acc, acck, den[1][0], den[1][1])
        p.op("dve", lambda e: e.tensor_mul(own[0:8, :], own[0:8, :], den[0][0][0:8, :]),
             reads=(ownk, den[0][1]), writes=(ownk,))
        p.op("dve", lambda e, cs=cs: e.tensor_scalar(out=r8[0:8, cs], in0=own[0:8, :],
             scalar1=8.0, scalar2=None, op0=ALU.mult), reads=(ownk,), writes=(r8k,))
    p.op("gp", lambda e: e.dma_start(out=ctx.d["r8_scratch"], in_=r8[0:8, :]),
         reads=(r8k,), writes=("r8d",), dma=True)
    p.op("sp", lambda e: e.dma_start(out=r32[0:32, :],
         in_=ctx.d["r8_scratch"].rearrange("ee (c t) -> c ee t", c=4)),
         reads=("r8d",), writes=(r32k,), dma=True)
    m32, m32k = ctx.af.alloc(1024)
    cols = [ctx.af.alloc(1) for _ in range(9)]
    (lo, lok), (hi, hik), (mid, midk), (cnt, cntk), (shc, shck), (tcol, tck), \
        (ucol, uck), (d1, d1k), (d2, d2k) = cols
    p.op("dve", lambda e: e.memset(lo[0:32, :], -1.0), writes=(lok,))
    p.op("dve", lambda e: e.memset(hi[0:32, :], 9.0), writes=(hik,))
    u8c = ctx.s["u8c"].ap()
    for it in range(60):
        p.op("dve", lambda e: e.tensor_add(mid[0:32, :], lo[0:32, :], hi[0:32, :]),
             reads=(lok, hik), writes=(midk,))
        p.op("dve", lambda e: e.tensor_scalar(out=mid[0:32, :], in0=mid[0:32, :],
             scalar1=0.5, scalar2=None, op0=ALU.mult), reads=(midk,), writes=(midk,))
        p.op("dve", lambda e: e.tensor_scalar(out=m32[0:32, :], in0=r32[0:32, :],
             scalar1=mid[0:32, :], scalar2=None, op0=ALU.is_gt, op1=ALU.add,
             accum_out=cnt[0:32, :]),
             reads=(r32k, midk), writes=(m32k, cntk))
        for rot in (8, 16):
            mask = [(i + rot) % 32 for i in range(32)]
            p.op("dve", lambda e, mask=mask: e.stream_shuffle(out=shc[0:32, :], in_=cnt[0:32, :], mask=mask),
                 reads=(cntk,), writes=(shck,))
            p.op("dve", lambda e: e.tensor_add(cnt[0:32, :], cnt[0:32, :], shc[0:32, :]),
                 reads=(cntk, shck), writes=(cntk,))
        p.op("dve", lambda e: e.tensor_scalar(out=ucol[0:32, :], in0=cnt[0:32, :],
             scalar1=float(KCAP), scalar2=None, op0=ALU.is_ge), reads=(cntk,), writes=(uck,))
        p.op("dve", lambda e: e.tensor_sub(d1[0:32, :], mid[0:32, :], lo[0:32, :]),
             reads=(midk, lok), writes=(d1k,))
        p.op("dve", lambda e: e.tensor_sub(d2[0:32, :], hi[0:32, :], mid[0:32, :]),
             reads=(hik, midk), writes=(d2k,))
        p.op("dve", lambda e: e.scalar_tensor_tensor(out=lo[0:32, :], in0=d1[0:32, :],
             scalar=ucol[0:32, :], in1=lo[0:32, :], op0=ALU.mult, op1=ALU.add),
             reads=(d1k, uck, lok), writes=(lok,))
        p.op("dve", lambda e: e.scalar_tensor_tensor(out=hi[0:32, :], in0=d2[0:32, :],
             scalar=ucol[0:32, :], in1=mid[0:32, :], op0=ALU.mult, op1=ALU.add),
             reads=(d2k, uck, midk), writes=(hik,))
    if ctx.d.get("dbg_bis") is not None:
        dbgb = ctx.d["dbg_bis"]
        p.op("gp", lambda e: e.dma_start(out=dbgb[0:32, 0:1], in_=lo[0:32, :], allow_slow_non_contiguous=True),
             reads=(lok,), writes=("dbgb1",), dma="sink")
        p.op("gp", lambda e: e.dma_start(out=dbgb[0:32, 1:2], in_=hi[0:32, :], allow_slow_non_contiguous=True),
             reads=(hik,), writes=("dbgb2",), dma="sink")
        p.op("gp", lambda e: e.dma_start(out=dbgb[0:32, 2:3], in_=cnt[0:32, :], allow_slow_non_contiguous=True),
             reads=(cntk,), writes=("dbgb3",), dma="sink")

    p.op("dve", lambda e: e.tensor_scalar(out=m32[0:32, :], in0=r32[0:32, :],
         scalar1=hi[0:32, :], scalar2=None, op0=ALU.is_gt, op1=ALU.add,
         accum_out=cnt[0:32, :]),
         reads=(r32k, hik), writes=(m32k, cntk))
    for rot in (8, 16):
        mask = [(i + rot) % 32 for i in range(32)]
        p.op("dve", lambda e, mask=mask: e.stream_shuffle(out=shc[0:32, :], in_=cnt[0:32, :], mask=mask),
             reads=(cntk,), writes=(shck,))
        p.op("dve", lambda e: e.tensor_add(cnt[0:32, :], cnt[0:32, :], shc[0:32, :]),
             reads=(cntk, shck), writes=(cntk,))
    p.op("dve", lambda e: e.tensor_scalar(out=tcol[0:32, :], in0=cnt[0:32, :],
         scalar1=-1.0, scalar2=float(KCAP), op0=ALU.mult, op1=ALU.add),
         reads=(cntk,), writes=(tck,))
    gt8, gt8k = ctx.af.alloc(S)
    tie8, tie8k = ctx.af.alloc(S)
    sel8 = ctx.s["sel8"].ap()
    sel8k = "sel8t"
    p.op("dve", lambda e: e.tensor_scalar(out=gt8[0:8, :], in0=r8[0:8, :],
         scalar1=hi[0:8, :], scalar2=None, op0=ALU.is_gt), reads=(r8k, hik), writes=(gt8k,))
    p.op("dve", lambda e: e.tensor_scalar(out=tie8[0:8, :], in0=r8[0:8, :],
         scalar1=lo[0:8, :], scalar2=None, op0=ALU.is_gt), reads=(r8k, lok), writes=(tie8k,))
    p.op("dve", lambda e: e.tensor_sub(tie8[0:8, :], tie8[0:8, :], gt8[0:8, :]),
         reads=(tie8k, gt8k), writes=(tie8k,))
    p.op("dve", lambda e: e.tensor_tensor_scan(out=sel8[0:8, :], data0=tie8[0:8, :],
         data1=tie8[0:8, :], initial=0.0, op0=ALU.add, op1=ALU.bypass),
         reads=(tie8k,), writes=(sel8k,))
    p.op("dve", lambda e: e.tensor_scalar(out=sel8[0:8, :], in0=sel8[0:8, :],
         scalar1=tcol[0:8, :], scalar2=None, op0=ALU.is_le), reads=(sel8k, tck), writes=(sel8k,))
    p.op("dve", lambda e: e.tensor_mul(sel8[0:8, :], sel8[0:8, :], tie8[0:8, :]),
         reads=(sel8k, tie8k), writes=(sel8k,))
    p.op("dve", lambda e: e.tensor_add(sel8[0:8, :], sel8[0:8, :], gt8[0:8, :]),
         reads=(sel8k, gt8k), writes=(sel8k,))
    selb, selbk = ctx.ab.alloc(S)
    p.op("dve", lambda e: e.tensor_copy(selb[0:8, :], sel8[0:8, :]),
         reads=(sel8k,), writes=(selbk,))
    p.barrier()
    ctx.af.reset()
    b1c, b1ck = ctx.af.alloc(2 * E)
    for e_i in range(E):
        for hh in range(2):
            p.op("sp", lambda e, e_i=e_i, hh=hh: e.dma_start(
                 out=b1c[:, e_i * 2 + hh:e_i * 2 + hh + 1], in_=b1_dram[e_i, hh * 128:(hh + 1) * 128]),
                 writes=(b1ck,), dma=True)
    b2s, b2sk = ctx.ab.alloc(D)
    p.op("sp", lambda e: e.dma_start(out=b2s[0:8, :], in_=b2_bf), writes=(b2sk,), dma=True)
    zc = [ctx.ab.alloc(TSUB) for _ in range(NCHUNK)]
    w1st = [ctx.ab.alloc(SUBH) for _ in range(NCHUNK)]
    w2st = [ctx.ab.alloc(D) for _ in range(4)]
    hgall, hgk = ctx.ab.alloc(E * 2 * TSUB)
    selbc = [ctx.af.alloc(TSUB) for _ in range(2)]
    gtmp = [ctx.af.alloc(TSUB) for _ in range(2)]
    xst = [ctx.af.alloc(TSUB) for _ in range(2)]
    ost = [ctx.af.alloc(TSUB) for _ in range(2)]
    for sub in range(NSUB):
        cs = slice(sub * TSUB, (sub + 1) * TSUB)
        for c in range(NCHUNK):
            p.op("sp", lambda e, a=zc[c][0], d=z_dram[c * 128:(c + 1) * 128, cs]:
                 e.dma_start(out=a, in_=d), reads=(z_key,), writes=(zc[c][1],), dma=True)
        for e_i in range(E):
            for c in range(NCHUNK):
                p.op("sp", lambda e, a=w1st[c][0], d=w1_bf[e_i][c * 128:(c + 1) * 128, :]:
                     e.dma_start(out=a, in_=d), reads=("w1bf",), writes=(w1st[c][1],), dma=True)
            sbc, sbck = selbc[e_i % 2]
            p.op("pe", lambda e, cs=cs, e_i=e_i: e.matmul(ps["ps6"].ap()[:, :],
                 selb8[0:8, e_i * 128:(e_i + 1) * 128], sel8[0:8, cs], start=True, stop=True),
                 reads=(sel8k, selb8_key), writes=("ps6",))
            p.op("dve", lambda e, sbc=sbc: e.tensor_copy(sbc, ps["ps6"].ap()[:, :]),
                 reads=("ps6",), writes=(sbck,))
            for hh in range(2):
                pb = f"ps{hh % 2}"
                for c in range(NCHUNK):
                    p.op("pe", lambda e, c=c, hh=hh, pb=pb: e.matmul(ps[pb].ap()[:, :],
                         w1st[c][0][:, hh * 128:(hh + 1) * 128], zc[c][0],
                         start=(c == 0), stop=(c == NCHUNK - 1)),
                         reads=(w1st[c][1], zc[c][1]), writes=(pb,))
                gg, ggk = gtmp[hh % 2]
                p.op("act", lambda e, gg=gg, pb=pb, e_i=e_i, hh=hh: e.activation(
                     out=gg, in_=ps[pb].ap()[:, :], func=AF.Gelu,
                     bias=b1c[:, e_i * 2 + hh:e_i * 2 + hh + 1]),
                     reads=(pb, b1ck), writes=(ggk,))
                p.op("dve", lambda e, gg=gg, sbc=sbc, e_i=e_i, hh=hh: e.tensor_mul(
                     hgall[:, (e_i * 2 + hh) * TSUB:(e_i * 2 + hh + 1) * TSUB], gg, sbc),
                     reads=(ggk, sbck), writes=(f"{hgk}:{e_i}:{hh}",))
        first = True
        for e_i in range(E):
            for hh in range(2):
                ws, wsk = w2st[(e_i * 2 + hh) % 4]
                p.op("sp", lambda e, ws=ws, d=w2_bf[e_i][hh * 128:(hh + 1) * 128, :]:
                     e.dma_start(out=ws, in_=d), reads=("w2bf",), writes=(wsk,), dma=True)
                for ho in range(NCHUNK):
                    p.op("pe", lambda e, ws=ws, ho=ho, st=first, e_i=e_i, hh=hh: e.matmul(
                         ps[f"ps{ho}"].ap()[:, :], ws[:, ho * 128:(ho + 1) * 128],
                         hgall[:, (e_i * 2 + hh) * TSUB:(e_i * 2 + hh + 1) * TSUB],
                         start=st, stop=False),
                         reads=(wsk, f"{hgk}:{e_i}:{hh}"), writes=(f"ps{ho}",))
                first = False
        for ho in range(NCHUNK):
            p.op("pe", lambda e, ho=ho, cs=cs: e.matmul(ps[f"ps{ho}"].ap()[:, :],
                 b2s[0:8, ho * 128:(ho + 1) * 128], selb[0:8, cs], start=False, stop=True),
                 reads=(b2sk, selbk), writes=(f"ps{ho}",))
        for ho in range(NCHUNK):
            xa, xk2 = xst[ho % 2]
            p.op("sp", lambda e, a=xa, d=x_dram[ho * 128:(ho + 1) * 128, cs]:
                 e.dma_start(out=a, in_=d), reads=(x_key,), writes=(xk2,), dma=True)
            oa, ok2 = ost[ho % 2]
            p.op("dve", lambda e, oa=oa, ho=ho, xa=xa: e.scalar_tensor_tensor(out=oa,
                 in0=ps[f"ps{ho}"].ap()[:, :], scalar=0.5, in1=xa, op0=ALU.mult, op1=ALU.add),
                 reads=(f"ps{ho}", xk2), writes=(ok2,))
            p.op("gp", lambda e, oa=oa, d=out_dram[ho * 128:(ho + 1) * 128, cs]:
                 e.dma_start(out=d, in_=oa), reads=(ok2,), writes=(f"{out_key}:{ho}:{sub}",), dma="sink")
    p.barrier()
    ctx.ab.reset()
    ctx.af.reset()


import contextlib

AB_ELEMS = 53248
AF_ELEMS = 20480


def make_ctx(nc, stack):
    ab_t = stack.enter_context(nc.sbuf_tensor("arena_bf", [128, AB_ELEMS], BF16))
    af_t = stack.enter_context(nc.sbuf_tensor("arena_f32", [128, AF_ELEMS], F32))
    u8t = stack.enter_context(nc.sbuf_tensor("u8t", [8, TSUB], U8))
    sel8_t = stack.enter_context(nc.sbuf_tensor("sel8t", [8, S], F32))
    u8c = stack.enter_context(nc.sbuf_tensor("u8c", [32, 1], U8))
    small = {"u8t": u8t, "u8c": u8c, "sel8": sel8_t}

    class _V:
        def __init__(self, ap):
            self._ap = ap

        def ap(self):
            return self._ap

    pp = [stack.enter_context(nc.psum_tensor(f"pp{i}", [128, 1024], F32)) for i in range(3)]
    pd = [stack.enter_context(nc.psum_tensor(f"pd{i}", [128, 512], F32)) for i in range(2)]
    small["pp0"] = _V(pp[0].ap())
    small["pp1"] = _V(pp[1].ap())
    small["pp2"] = _V(pp[2].ap())
    small["ps0"] = _V(pp[0].ap()[:, 0:512])
    small["ps1"] = _V(pp[0].ap()[:, 512:1024])
    small["ps2"] = _V(pp[1].ap()[:, 0:512])
    small["ps3"] = _V(pp[1].ap()[:, 512:1024])
    small["ps6"] = _V(pp[2].ap()[:, 0:512])
    small["ps7"] = _V(pp[2].ap()[:, 512:1024])
    small["ps4"] = _V(pd[0].ap())
    small["ps5"] = _V(pd[1].ap())
    p = Prog(nc)
    return Ctx(nc, p, Arena(ab_t, AB_ELEMS, "ab"), Arena(af_t, AF_ELEMS, "af"), small, {})


def _load_selb8(ctx, dram):
    selb, _ = ctx.s["selb"], None
    ctx.p.op("sp", lambda e: e.dma_start(out=selb.ap()[0:8, :], in_=dram),
             writes=("selb8",), dma=True)
    return selb.ap(), "selb8"


def emit_l0ffn(ctx, x_in, x_in_key, x_out, x_out_key, selb8, selb8k):
    d = ctx.d
    emit_norm(ctx, x_in_key, x_in, d["y"], "y", "ln")
    emit_norm(ctx, "y", d["y"], d["zb"], "zb", "rms", w_dram=d["ln1_w"])
    emit_qkv(ctx, d["zb"], "zb", [
        (d["fq_b"], "fq_b", d["q"], "q", True, 0.3535533905932738),
        (d["fk_b"], "fk_b", d["k"], "k", True, 0.3535533905932738),
        (d["fv_b"], "fv_b", d["v"], "v", True, 1.0)])
    emit_attn_core(ctx, d["q"], d["k"], d["v"], d["cx"], "cx", causal=False)
    emit_proj(ctx, d["cx"], "cx", d["fo_b"], "fo_b", d["h"], "h", True,
              out_dtype=F32, residual=(d["y"], "y"), inv_name="invd")
    emit_norm(ctx, "h", d["h"], d["zb"], "zb2", "rms", w_dram=d["ln2_w"])
    emit_moe0(ctx, d["zb"], "zb2", d["w1_b"], d["w2_b"], d["w3_b"], d["gate_w"],
              d["we"], d["moe"], "moe", selb8, selb8k)
    emit_combine(ctx, x_in, x_in_key, d["h"], "h", d["moe"], "moe", x_out, x_out_key)


def emit_l0attn(ctx, x_in, x_in_key, x_out, x_out_key):
    d = ctx.d
    emit_norm(ctx, x_in_key, x_in, d["zb"], "zb", "ln")
    emit_qkv(ctx, d["zb"], "zb", [
        (d["aq_b"], "aq_b", d["q"], "q", False, 0.3535533905932738),
        (d["ak_b"], "ak_b", d["k"], "k", False, 0.3535533905932738),
        (d["av_b"], "av_b", d["v"], "v", False, 1.0)])
    emit_l0attn_win(ctx, d["q"], d["k"], d["v"], d["cx"], d["cx2"], "ce", "co")
    emit_wo_win(ctx, d["cx"], "ce", d["cx2"], "co", d["ao_b"], x_in, x_in_key,
                d["c_count"], x_out, x_out_key)


def build_launch1(wdt, dbg=False):
    nc = bass.Bass("TRN2", target_bir_lowering=False, debug=False)
    d = {}
    def di(name, shape, dt=F32, kind="ExternalInput"):
        d[name] = nc.dram_tensor(name, list(shape), dt, kind=kind).ap()
    di("x0", (D, S))
    di("ln1_w", (D,))
    di("ln2_w", (D,))
    for n in ("fq", "fk", "fv", "fo"):
        di(n, (D, D))
    di("gate_w", (E, D))
    di("w1", (E, IMOE, D))
    di("w2", (E, D, IMOE))
    di("w3", (E, IMOE, D))
    for n in ("aq", "ak", "av", "ao"):
        di(n, (D, D))
    di("c_selb", (E, E * 128))
    di("c_count", (1, S))
    di("c_hsel", (128, 128), BF16)
    di("c_selb16", (16, 16 * 128))
    di("c_selp", (2, 128))
    di("den", (16, S), F32, "Internal")
    di("invd", (16, S), F32, "Internal")
    di("den_e", (16, S), F32, "Internal")
    di("den_o", (16, S), F32, "Internal")
    di("inv_e", (16, S), F32, "Internal")
    di("inv_o", (16, S), F32, "Internal")
    di("gw", (D, E))
    di("gwb", (E,))
    for n in ("fq", "fk", "fv", "fo", "aq", "ak", "av", "ao"):
        di(n + "_b", (D, D), BF16, "Internal")

    di("w1_b", (E, IMOE, D), BF16, "Internal")
    di("w2_b", (E, D, IMOE), BF16, "Internal")
    di("w3_b", (E, IMOE, D), BF16, "Internal")
    for n in ("y", "h", "moe"):
        di(n, (D, S), F32, "Internal")
    for n in ("xa", "xb"):
        di(n, (D, S), F32, "Internal")
    for n in ("zb", "q", "k", "v", "cx", "cx2"):
        di(n, (D, S), BF16, "Internal")
    di("we", (E, S), F32, "Internal")
    di("x_out", (D, S), F32, "ExternalOutput")
    di("masked", (E, S), F32, "ExternalOutput")
    with contextlib.ExitStack() as stack:
        selb_t = stack.enter_context(nc.sbuf_tensor("selb", [8, E * 128], F32))
        ctx = make_ctx(nc, stack)
        ctx.s["selb"] = selb_t
        ctx.d = d
        prep = [(d[n], d[n + "_b"], n + "_b") for n in
                ("fq", "fk", "fv", "fo", "aq", "ak", "av", "ao")]

        for e_i in range(E):
            prep.append((d["w1"][e_i], d["w1_b"][e_i], f"w1b{e_i}"))
            prep.append((d["w2"][e_i], d["w2_b"][e_i], f"w2b{e_i}"))
            prep.append((d["w3"][e_i], d["w3_b"][e_i], f"w3b{e_i}"))
        emit_weight_prep(ctx, prep)
        selb8, selb8k = _load_selb8(ctx, d["c_selb"])
        d["w1bf_idx"] = None
        emit_l0ffn(ctx, d["x0"], "x0", d["xa"], "xa", selb8, selb8k)
        if dbg:
            for n in ("y", "h", "moe", "we", "xa"):
                dd = nc.dram_tensor("dbg_" + n, list(d[n].shape), F32, kind="ExternalOutput").ap()
                ctx.p.op("gp", lambda e, s=d[n], t=dd: e.dma_start(out=t, in_=s),
                         writes=(f"dbg{n}",), dma="sink")
            ctx.p.barrier()
        emit_l0attn(ctx, d["xa"], "xa", d["xb"], "xb")
        if dbg:
            for n in ("xb", "q", "k", "v"):
                sh2 = list(d[n].shape)
                dd = nc.dram_tensor("dbg2_" + n, sh2, d[n].dtype, kind="ExternalOutput").ap()
                ctx.p.op("gp", lambda e, s=d[n], t=dd: e.dma_start(out=t, in_=s),
                         writes=(f"dbg2{n}",), dma="sink")
            ctx.p.barrier()
        emit_l0ffn(ctx, d["xb"], "xb", d["x_out"], "x_out", selb8, selb8k)
        emit_norm(ctx, "x_out", d["x_out"], d["zb"], "zb4", "ln")
        emit_gate1(ctx, d["zb"], "zb4", d["gw"], d["gwb"], d["masked"], "masked")
        ctx.p.finalize()
    return nc


def build_launch2():
    nc = bass.Bass("TRN2", target_bir_lowering=False, debug=False)
    d = {}
    def di(name, shape, dt=F32, kind="ExternalInput"):
        d[name] = nc.dram_tensor(name, list(shape), dt, kind=kind).ap()
    di("x0", (D, S))
    di("mall", (8, E, S))
    di("mown", (E, S))
    for n in ("aq", "ak", "av", "ao"):
        di(n, (D, D))
    di("gw", (D, E))
    di("gwb", (E,))
    di("w1", (E, D, SUBH))
    di("b1", (E, SUBH))
    di("w2", (E, SUBH, D))
    di("b2", (E, D))
    di("c_selb", (E, E * 128))
    di("c_masks", (128, 4 * TSUB), BF16)
    di("c_hsel", (128, 128), BF16)
    di("c_selb16", (16, 16 * 128))
    di("c_selp", (2, 128))
    di("den", (16, S), F32, "Internal")
    di("invd", (16, S), F32, "Internal")
    di("den_e", (16, S), F32, "Internal")
    di("den_o", (16, S), F32, "Internal")
    di("inv_e", (16, S), F32, "Internal")
    di("inv_o", (16, S), F32, "Internal")
    for n in ("aq", "ak", "av", "ao"):
        di(n + "_b", (D, D), BF16, "Internal")
    di("w1_b", (E, D, SUBH), BF16, "Internal")
    di("w2_b", (E, SUBH, D), BF16, "Internal")
    di("b2_b", (E, D), BF16, "Internal")
    for n in ("xa", "xb"):
        di(n, (D, S), F32, "Internal")
    for n in ("zb", "q", "k", "v", "cx"):
        di(n, (D, S), BF16, "Internal")
    di("r8_scratch", (E, S), F32, "Internal")
    di("x_out", (D, S), F32, "ExternalOutput")
    di("masked", (E, S), F32, "ExternalOutput")
    with contextlib.ExitStack() as stack:
        selb_t = stack.enter_context(nc.sbuf_tensor("selb", [8, E * 128], F32))
        ctx = make_ctx(nc, stack)
        ctx.s["selb"] = selb_t
        ctx.d = d
        prep = [(d[n], d[n + "_b"], n + "_b") for n in ("aq", "ak", "av", "ao")]
        prep.append((d["b2"], d["b2_b"], "b2_b"))
        for e_i in range(E):
            prep.append((d["w1"][e_i], d["w1_b"][e_i], f"w1b{e_i}"))
            prep.append((d["w2"][e_i], d["w2_b"][e_i], f"w2b{e_i}"))
        emit_weight_prep(ctx, prep)
        selb8, selb8k = _load_selb8(ctx, d["c_selb"])
        emit_norm(ctx, "x0", d["x0"], d["zb"], "zb4", "ln")
        emit_l1ffn(ctx, d["zb"], "zb4", d["mall"], d["mown"], d["w1_b"], d["b1"],
                   d["w2_b"], d["b2_b"], d["x0"], "x0", d["xa"], "xa", selb8, selb8k)
        emit_norm(ctx, "xa", d["xa"], d["zb"], "zb5", "ln")
        emit_qkv(ctx, d["zb"], "zb5", [
            (d["aq_b"], "aq_b", d["q"], "q", True, 0.3535533905932738),
            (d["ak_b"], "ak_b", d["k"], "k", True, 0.3535533905932738),
            (d["av_b"], "av_b", d["v"], "v", True, 1.0)])
        emit_attn_core(ctx, d["q"], d["k"], d["v"], d["cx"], "cx", causal=True,
                       masks_dram=d["c_masks"])
        emit_proj(ctx, d["cx"], "cx", d["ao_b"], "ao_b", d["x_out"], "x_out", True,
                  out_dtype=F32, residual=(d["xa"], "xa"), inv_name="invd")
        emit_norm(ctx, "x_out", d["x_out"], d["zb"], "zb6", "ln")
        emit_gate1(ctx, d["zb"], "zb6", d["gw"], d["gwb"], d["masked"], "masked")
        ctx.p.finalize()
    return nc


def build_launch3():
    nc = bass.Bass("TRN2", target_bir_lowering=False, debug=False)
    d = {}
    def di(name, shape, dt=F32, kind="ExternalInput"):
        d[name] = nc.dram_tensor(name, list(shape), dt, kind=kind).ap()
    di("x0", (D, S))
    di("mall", (8, E, S))
    di("mown", (E, S))
    di("w1", (E, D, SUBH))
    di("b1", (E, SUBH))
    di("w2", (E, SUBH, D))
    di("b2", (E, D))
    di("c_selb", (E, E * 128))
    di("w1_b", (E, D, SUBH), BF16, "Internal")
    di("w2_b", (E, SUBH, D), BF16, "Internal")
    di("b2_b", (E, D), BF16, "Internal")
    di("zb", (D, S), BF16, "Internal")
    di("r8_scratch", (E, S), F32, "Internal")
    di("x_out", (D, S), F32, "ExternalOutput")
    with contextlib.ExitStack() as stack:
        selb_t = stack.enter_context(nc.sbuf_tensor("selb", [8, E * 128], F32))
        ctx = make_ctx(nc, stack)
        ctx.s["selb"] = selb_t
        ctx.d = d
        prep = [(d["b2"], d["b2_b"], "b2_b")]
        for e_i in range(E):
            prep.append((d["w1"][e_i], d["w1_b"][e_i], f"w1b{e_i}"))
            prep.append((d["w2"][e_i], d["w2_b"][e_i], f"w2b{e_i}"))
        emit_weight_prep(ctx, prep)
        selb8, selb8k = _load_selb8(ctx, d["c_selb"])
        emit_norm(ctx, "x0", d["x0"], d["zb"], "zb6", "ln")
        emit_l1ffn(ctx, d["zb"], "zb6", d["mall"], d["mown"], d["w1_b"], d["b1"],
                   d["w2_b"], d["b2_b"], d["x0"], "x0", d["x_out"], "x_out", selb8, selb8k)
        ctx.p.finalize()
    return nc


def _consts():
    selb = np.zeros((E, E * 128), np.float32)
    for e_i in range(E):
        selb[e_i, e_i * 128:(e_i + 1) * 128] = 1.0
    cnt = np.full((1, S), 2.0, np.float32)
    cnt[0, 0:128] = 1.0
    cnt[0, S - 128:S] = 1.0
    hsel = np.zeros((128, 128), np.float32)
    for c in range(8):
        for dd in range(128):
            hsel[dd, c * 16 + 2 * c + dd // 64] = 1.0
    selb16 = np.zeros((16, 16 * 128), np.float32)
    for h in range(16):
        selb16[h, h * 128:(h + 1) * 128] = 1.0
    selp = np.zeros((2, 128), np.float32)
    selp[0, 0:64] = 1.0
    selp[1, 64:128] = 1.0
    masks = np.zeros((128, 4 * TSUB), np.float32)
    for dd in range(4):
        i = np.arange(128)[:, None]
        j = np.arange(TSUB)[None, :]
        masks[:, dd * TSUB:(dd + 1) * TSUB] = (j >= i + dd * 128).astype(np.float32)
    return selb, cnt, masks.astype(ml_dtypes.bfloat16), hsel.astype(ml_dtypes.bfloat16), selb16, selp


_NC_CACHE = {}


def kernel(**inputs):
    inp = {k: np.asarray(v) for k, v in inputs.items()}
    x_full = inp["L0_attn_hidden_states"]
    selb, cnt, masks, hsel, selb16, selp = _consts()
    if "l1" not in _NC_CACHE:
        _NC_CACHE["l1"] = build_launch1(None)
        _NC_CACHE["l2"] = build_launch2()
        _NC_CACHE["l3"] = build_launch3()
    nc1, nc2, nc3 = _NC_CACHE["l1"], _NC_CACHE["l2"], _NC_CACHE["l3"]
    common1 = {
        "ln1_w": inp["L0_ffn_ln1_w"], "ln2_w": inp["L0_ffn_ln2_w"],
        "fq": inp["L0_ffn_wq"], "fk": inp["L0_ffn_wk"], "fv": inp["L0_ffn_wv"],
        "fo": inp["L0_ffn_wo"], "gate_w": inp["L0_ffn_gate_w"],
        "w1": inp["L0_ffn_w1"], "w2": inp["L0_ffn_w2"], "w3": inp["L0_ffn_w3"],
        "aq": inp["L0_attn_Wq"], "ak": inp["L0_attn_Wk"], "av": inp["L0_attn_Wv"],
        "ao": inp["L0_attn_Wo"],
        "gw": inp["L1_ffn_w_gate_w"], "gwb": inp["L1_ffn_w_gate_b"],
        "c_selb": selb, "c_count": cnt, "c_hsel": hsel, "c_selb16": selb16, "c_selp": selp,
    }
    in1 = []
    for b in range(8):
        m = dict(common1)
        m["x0"] = np.ascontiguousarray(x_full[b].T)
        in1.append(m)
    r1 = run_bass_kernel_spmd(nc1, in1, core_ids=list(range(8)))
    x3 = [r1.results[b]["x_out"] for b in range(8)]
    masked1 = np.stack([r1.results[b]["masked"] for b in range(8)])
    common2 = {
        "aq": inp["L1_attn_WQ"], "ak": inp["L1_attn_WK"], "av": inp["L1_attn_WV"],
        "ao": inp["L1_attn_WO"],
        "gw": inp["L1_ffn_w_gate_w"], "gwb": inp["L1_ffn_w_gate_b"],
        "w1": inp["L1_ffn_W1"], "b1": inp["L1_ffn_b1"],
        "w2": inp["L1_ffn_W2"], "b2": inp["L1_ffn_b2"],
        "c_selb": selb, "c_masks": masks, "mall": masked1, "c_hsel": hsel, "c_selb16": selb16, "c_selp": selp,
    }
    in2 = []
    for b in range(8):
        m = dict(common2)
        m["x0"] = x3[b]
        m["mown"] = masked1[b]
        in2.append(m)
    r2 = run_bass_kernel_spmd(nc2, in2, core_ids=list(range(8)))
    x5 = [r2.results[b]["x_out"] for b in range(8)]
    masked2 = np.stack([r2.results[b]["masked"] for b in range(8)])
    common3 = {
        "w1": inp["L1_ffn_W1"], "b1": inp["L1_ffn_b1"],
        "w2": inp["L1_ffn_W2"], "b2": inp["L1_ffn_b2"],
        "c_selb": selb, "mall": masked2,
    }
    in3 = []
    for b in range(8):
        m = dict(common3)
        m["x0"] = x5[b]
        m["mown"] = masked2[b]
        in3.append(m)
    r3 = run_bass_kernel_spmd(nc3, in3, core_ids=list(range(8)))
    out = np.stack([np.ascontiguousarray(r3.results[b]["x_out"].T) for b in range(8)])
    return out.astype(np.float32)

